# version 05
# baseline (speedup 1.0000x reference)
import contextlib
import numpy as np
import ml_dtypes
import concourse.bass as bass
import concourse.mybir as mybir
from concourse.bass_utils import run_bass_kernel_spmd

F32 = mybir.dt.float32
BF16 = mybir.dt.bfloat16
AF = mybir.ActivationFunctionType
ALU = mybir.AluOpType
AX = mybir.AxisListType
BFNP = ml_dtypes.bfloat16


STREAMS = ("pe", "act", "dve", "pool", "sp")
DMA_STREAMS = ("sp", "pool")
ORDERED_STREAMS = ("pe", "sp")


class Op:
    __slots__ = ("stream", "fn", "idx", "deps", "marked", "is_dma", "tag",
                 "channel", "chidx", "chan_waits")

    def __init__(self, stream, fn, idx, is_dma, tag, channel=None, chidx=-1):
        self.chan_waits = None
        self.stream = stream
        self.fn = fn
        self.idx = idx
        self.deps = {}
        self.is_dma = is_dma
        self.marked = False
        self.tag = tag
        self.channel = channel
        self.chidx = chidx


class Prog:

    def __init__(self):
        self.ops = {s: [] for s in STREAMS}
        self.chan = {}
        self.last_writer = {}
        self.readers = {}
        self.clock = {s: {} for s in STREAMS}
        self._opclock = {}
        self._nbar = 0
        self.out_tokens = []

    def op(self, stream, fn, reads=(), writes=(), is_dma=False, tag="", channel=None):
        assert stream in STREAMS, stream
        if is_dma:
            assert stream in DMA_STREAMS
            assert channel, f"DMA op needs a channel: {tag}"
        idx = len(self.ops[stream])
        chidx = -1
        if is_dma:
            channel = f"{channel}%{stream}"
            lst = self.chan.setdefault(channel, [])
            chidx = len(lst)
        o = Op(stream, fn, idx, is_dma, tag, channel, chidx)
        deps = {}

        def add_dep(s2, i2):
            o2 = self.ops[s2][i2]
            if o2.is_dma:
                k = ("c", o2.channel)
                v = o2.chidx
            else:
                k = ("s", s2)
                v = i2
            if v > deps.get(k, -1):
                deps[k] = v
        for r in reads:
            lw = self.last_writer.get(r)
            if lw:
                add_dep(*lw)
        for w in writes:
            lw = self.last_writer.get(w)
            if lw:
                add_dep(*lw)
            for (rs, ri) in self.readers.get(w, ()):
                add_dep(rs, ri)
        if is_dma and chidx > 0:
            add_dep(*self.chan[channel][chidx - 1])
        if stream == "pool" and idx > 0 and not is_dma:
            prev = self.ops["pool"][idx - 1]
            if not prev.is_dma:
                deps[("s", "pool")] = max(deps.get(("s", "pool"), -1), idx - 1)
            else:
                add_dep("pool", idx - 1)
        for w in writes:
            self.last_writer[w] = (stream, idx)
            self.readers[w] = []
        for r in reads:
            self.readers.setdefault(r, []).append((stream, idx))
        myclock = self.clock[stream]
        eff = {}
        for k, v in deps.items():
            if k == ("s", stream) and stream in ORDERED_STREAMS:
                continue
            if myclock.get(k, -1) >= v:
                continue
            eff[k] = v
        o.deps = eff
        for k, v in eff.items():
            if v > myclock.get(k, -1):
                myclock[k] = v
            if k[0] == "s":
                pc = self._opclock.get((k[1], v))
            else:
                pc = self._opclock.get(self.chan[k[1]][v])
            if pc:
                for k2, v2 in pc.items():
                    if k2 != ("s", stream) and v2 > myclock.get(k2, -1):
                        myclock[k2] = v2
        if stream in ORDERED_STREAMS:
            myclock[("s", stream)] = idx
        self._opclock[(stream, idx)] = dict(myclock)
        self.ops[stream].append(o)
        if is_dma:
            self.chan[channel].append((stream, idx))
        return o

    def out_dma(self, stream, fn, reads=(), tag=""):
        tok = f"d:out{len(self.out_tokens)}"
        self.out_tokens.append(tok)
        return self.op(stream, fn, reads=reads, writes=(tok,), is_dma=True,
                       tag=tag, channel="D:out")

    def barrier(self, flush):
        self._nbar += 1
        tok = f"__bar{self._nbar}_"
        snapshot = {ch: len(lst) for ch, lst in self.chan.items() if lst}
        prev = None
        for s in STREAMS:
            fn, xr, xw = flush[s]
            rd = ((prev,) if prev else ()) + tuple(xr)
            o = self.op(s, fn, reads=rd, writes=(tok + s,) + tuple(xw), tag="barrier")
            if s == "sp":
                o.chan_waits = snapshot
            prev = tok + s
        last = prev
        for s in STREAMS[:-1]:
            fn, xr, xw = flush[s]
            self.op(s, fn, reads=(last,) + tuple(xr), writes=tuple(xw), tag="barrier2")

    def finalize(self, block, sems, chan_sems):
        for s in STREAMS:
            for o in self.ops[s]:
                for k, v in o.deps.items():
                    if k[0] == "s":
                        self.ops[k[1]][v].marked = True
        compval = {}
        for s in STREAMS:
            acc_c, vc = 0, []
            for o in self.ops[s]:
                if o.marked and not o.is_dma:
                    acc_c += 1
                vc.append(acc_c)
            compval[s] = vc

        chtot = {ch: 16 * len(lst) for ch, lst in self.chan.items()}
        stot = {s: compval[s][-1] if self.ops[s] else 0 for s in STREAMS}
        for s in STREAMS:
            for o in self.ops[s]:
                for k, v in o.deps.items():
                    if k[0] == "c":
                        assert 16 * (v + 1) <= chtot[k[1]], (s, o.tag, k, v, chtot[k[1]])
                    else:
                        assert compval[k[1]][v] <= stot[k[1]], (s, o.tag, k, v)
                if o.chan_waits:
                    for ch, cnt in o.chan_waits.items():
                        assert 16 * cnt <= chtot[ch], (o.tag, ch, cnt)

        def emit_stream(eng, s):
            for o in self.ops[s]:
                if o.chan_waits:
                    for ch, cnt in sorted(o.chan_waits.items()):
                        eng.wait_ge(chan_sems[ch], 16 * cnt)
                for k, v in sorted(o.deps.items()):
                    if k[0] == "c":
                        eng.wait_ge(chan_sems[k[1]], 16 * (v + 1))
                    else:
                        dep_op = self.ops[k[1]][v]
                        assert not dep_op.is_dma
                        eng.wait_ge(sems[k[1]], compval[k[1]][v])
                inst = o.fn(eng)
                if o.is_dma:
                    inst.then_inc(chan_sems[o.channel], 16)
                elif o.marked:
                    if inst is None:
                        raise RuntimeError(f"marked op with no inst: {s} {o.tag}")
                    inst.then_inc(sems[s], 1)

        @block.tensor
        def _(eng):
            emit_stream(eng, "pe")

        @block.scalar
        def _(eng):
            emit_stream(eng, "act")

        @block.vector
        def _(eng):
            emit_stream(eng, "dve")

        @block.gpsimd
        def _(eng):
            emit_stream(eng, "pool")

        @block.sync
        def _(eng):
            emit_stream(eng, "sp")

    def stats(self):
        d = {s: len(self.ops[s]) for s in STREAMS}
        d["marked"] = sum(o.marked for s in STREAMS for o in self.ops[s])
        d["waits"] = sum(len(o.deps) for s in STREAMS for o in self.ops[s])
        d["channels"] = len(self.chan)
        return d



D = 2048
S = 2048
DC = D // 128
H = 8192
HC = H // 128
SUP = 8
NTH = 2
TH = S // NTH
NK, NV, DK, DV = 16, 32, 128, 128
C = 128
NCH = S // C
LN_EPS = 1e-5
L2_EPS = 1e-6
RMS_EPS = 1e-6
QSCALE = DK ** -0.5


def build_program(cfg):
    nc = bass.Bass("TRN2", target_bir_lowering=False)

    x0 = nc.dram_tensor("x0", [D, S], F32, kind="ExternalInput").ap()
    wgT = nc.dram_tensor("wgT", [D, H], F32, kind="ExternalInput").ap()
    wuT = nc.dram_tensor("wuT", [D, H], F32, kind="ExternalInput").ap()
    wdT = nc.dram_tensor("wdT", [H, D], F32, kind="ExternalInput").ap()
    wqkvz = nc.dram_tensor("wqkvz", [D, 12288], F32, kind="ExternalInput").ap()
    wba = nc.dram_tensor("wba", [D, 64], F32, kind="ExternalInput").ap()
    convw = nc.dram_tensor("convw", [8192, 4], F32, kind="ExternalInput").ap()
    dtb = nc.dram_tensor("dtb", [32], F32, kind="ExternalInput").ap()
    alog = nc.dram_tensor("alog", [32], F32, kind="ExternalInput").ap()
    normw = nc.dram_tensor("normw", [128], F32, kind="ExternalInput").ap()
    wout = nc.dram_tensor("wout", [4096, D], F32, kind="ExternalInput").ap()
    out = nc.dram_tensor("out", [D, S], F32, kind="ExternalOutput").ap()

    x1d = nc.dram_tensor("x1d", [D, S], F32, kind="Internal").ap()
    x2d = nc.dram_tensor("x2d", [D, S], F32, kind="Internal").ap()
    ln2d = nc.dram_tensor("ln2d", [D, S], BF16, kind="Internal").ap()
    ln3d = nc.dram_tensor("ln3d", [D, S], BF16, kind="Internal").ap()
    xgd = nc.dram_tensor("xgd", [NV, 128, S], BF16, kind="Internal").ap()
    qkd = nc.dram_tensor("qkd", [NK, 3, 128, S], BF16, kind="Internal").ap()
    vzd = nc.dram_tensor("vzd", [NV, 2, 128, S], BF16, kind="Internal").ap()
    lhd = nc.dram_tensor("lhd", [NCH, 32, C], F32, kind="Internal").ap()
    wgB = nc.dram_tensor("wgB", [HC, 128, D], BF16, kind="Internal").ap()
    wuB = nc.dram_tensor("wuB", [HC, 128, D], BF16, kind="Internal").ap()
    wdB = nc.dram_tensor("wdB", [HC, 128, D], BF16, kind="Internal").ap()

    p = Prog()
    st = contextlib.ExitStack()
    sb = lambda name, shape, dt: st.enter_context(nc.sbuf_tensor(name, shape, dt))
    ps = lambda name, shape, dt: st.enter_context(nc.psum_tensor(name, shape, dt))

    with st:
        pb = [ps(f"pb{i}", [128, 512], F32) for i in range(6)]
        pbt = ps("pbt", [128, 1024], BF16)
        pbs = ps("pbs", [128, 512], F32)
        PB = [f"PB{i}" for i in range(6)]
        PBT, PBS = "PBT", "PBS"

        ones_col_bf = sb("ones_col_bf", [128, 1], BF16)
        ones_row_f = sb("ones_row_f", [1, 128], F32)
        ident_bf = sb("ident_bf", [128, 128], BF16)
        ident_f = sb("ident_f", [128, 128], F32)
        triU_inc = sb("triU_inc", [128, 128], F32)
        triU_str = sb("triU_str", [128, 128], F32)
        mask_low = sb("mask_low", [128, 512], F32)
        ident_t4 = sb("ident_t4", [128, 512], F32)
        nw_bcast = sb("nw_bcast", [128, 512], F32)
        nw_row = sb("nw_row", [1, 128], F32)
        eps_t = sb("eps_t", [128, 6], F32)
        xsl = [sb(f"xsl{i}", [128, 512], F32) for i in range(3)]
        bfsl = [sb(f"bfsl{i}", [128, 512], BF16) for i in range(3)]
        sqsl2 = [sb(f"sqsl{i}", [128, 512], BF16) for i in range(2)]
        sqsl = sqsl2[0]
        srow = [sb(f"srow{i}", [1, 512], F32) for i in range(4)]

        def build_constants():
            CW = ("C:all", "C:nw_pre")

            def pop(fn, tag):
                p.op("pool", fn, reads=CW, writes=CW, tag=tag)
            pop(lambda g: g.memset(ones_col_bf[:], 1.0), "c0")
            pop(lambda g: g.memset(ones_row_f[:], 1.0), "c1")
            for t in (ident_bf, ident_f):
                pop(lambda g, t=t: g.memset(t[:], 1.0), "c2")
                pop(lambda g, t=t: g.affine_select(out=t[:], in_=t[:], compare_op=ALU.is_equal,
                                                   fill=0.0, base=0, pattern=[[-1, 128]],
                                                   channel_multiplier=1), "c3")
            pop(lambda g: g.memset(triU_inc[:], 1.0), "c4")
            pop(lambda g: g.affine_select(out=triU_inc[:], in_=triU_inc[:], compare_op=ALU.is_ge,
                                          fill=0.0, base=0, pattern=[[1, 128]],
                                          channel_multiplier=-1), "c5")
            pop(lambda g: g.memset(triU_str[:], 1.0), "c6")
            pop(lambda g: g.affine_select(out=triU_str[:], in_=triU_str[:], compare_op=ALU.is_gt,
                                          fill=0.0, base=0, pattern=[[-1, 128]],
                                          channel_multiplier=1), "c7")
            pop(lambda g: g.memset(mask_low[:], 0.0), "c8")
            pop(lambda g: g.affine_select(out=mask_low[:], in_=mask_low[:], compare_op=ALU.is_ge,
                                          fill=-10000.0, base=0, pattern=[[0, 4], [-1, 128]],
                                          channel_multiplier=1), "c9")
            pop(lambda g: g.memset(eps_t[:, 0:1], LN_EPS), "c10")
            pop(lambda g: g.memset(eps_t[:, 1:2], L2_EPS), "c11")
            pop(lambda g: g.memset(eps_t[:, 2:3], RMS_EPS), "c12")
            pop(lambda g: g.memset(eps_t[:, 3:4], 1.0), "c13")
            pop(lambda g: g.memset(eps_t[:, 4:5], float(np.log(QSCALE))), "c14")
            pop(lambda g: g.memset(eps_t[:, 5:6], 0.0), "c15")
            pop(lambda g: g.memset(ident_t4[:], 1.0), "c16")
            pop(lambda g: g.affine_select(out=ident_t4[:], in_=ident_t4[:],
                                          compare_op=ALU.is_equal, fill=0.0, base=0,
                                          pattern=[[0, 4], [-1, 128]], channel_multiplier=1), "c17")
            p.op("sp", lambda e: e.dma_start(out=nw_row[:], in_=normw.rearrange("(o f) -> o f", o=1)),
                 writes=("C:nw_row",), is_dma=True, tag="nwrow", channel="nw_row")
            p.op("pe", lambda e: e.matmul(pbs[:, 0:128], ones_row_f[:], nw_row[:], start=True, stop=True),
                 reads=("C:all", "C:nw_row"), writes=(PBS,), tag="nwmm")

            def nw_evac(e):
                for j in range(4):
                    i = e.tensor_copy(nw_bcast[:, j * 128:(j + 1) * 128], pbs[:, 0:128])
                return i
            p.op("dve", nw_evac, reads=(PBS, "C:nw_pre"), writes=("C:nw_bcast",), tag="nwevac")

        build_constants()

        FLUSH = {
            "pe": (lambda e: e.matmul(pbs[0:1, 0:4], ones_row_f[0:1, 0:1],
                                      ones_row_f[0:1, 0:4], start=True, stop=True),
                   ("C:all",), (PBS,)),
            "act": (lambda e: e.activation(out=srow[0][0:1, 0:4], in_=ones_row_f[0:1, 0:4],
                                           func=AF.Copy),
                    ("C:all",), ("srow0",)),
            "dve": (lambda e: e.engine_nop(), (), ()),
            "pool": (lambda e: e.engine_nop(), (), ()),
            "sp": (lambda e: e.nop(), (), ()),
        }

        def alloc_ffn(stk, sfx):
            s2 = lambda n, sh, dt: stk.enter_context(
                nc.sbuf_tensor(n + sfx, sh, dt, unsafe_allow_realloc=True))
            fs = {}
            fs["lnres"] = s2("lnres", [128, DC * TH], BF16)
            fs["y_acc"] = s2("y_acc", [128, DC * TH], F32)
            fs["gated"] = [s2(f"gated{j}", [128, TH], BF16) for j in range(SUP)]
            fs["wg_r"] = [s2(f"wg_r{i}", [128, D], BF16) for i in range(3)]
            fs["wu_r"] = [s2(f"wu_r{i}", [128, D], BF16) for i in range(3)]
            fs["wd_r"] = [s2(f"wd_r{i}", [128, D], BF16) for i in range(SUP + 2)]
            fs["sfx"] = sfx
            return fs

        def stats_rows(tagp, tt):
            p.op("act", lambda e: e.activation(out=srow[0][:], in_=pb[4][:1, :], func=AF.Copy,
                                               scale=1.0 / D),
                 reads=(PB[4],), writes=("srow0",), tag=f"{tagp}m{tt}")
            p.op("act", lambda e: e.activation(out=srow[1][:], in_=srow[0][:], func=AF.Square),
                 reads=("srow0",), writes=("srow1",), tag=f"{tagp}m2{tt}")
            p.op("dve", lambda e: e.tensor_scalar(out=srow[2][:], in0=pb[5][:1, :],
                                                  scalar1=1.0 / D, scalar2=None, op0=ALU.mult),
                 reads=(PB[5],), writes=("srow2",), tag=f"{tagp}ex2{tt}")
            p.op("dve", lambda e: e.tensor_tensor(out=srow[2][:], in0=srow[2][:], in1=srow[1][:],
                                                  op=ALU.subtract),
                 reads=("srow2", "srow1"), writes=("srow2",), tag=f"{tagp}var{tt}")
            p.op("act", lambda e: e.activation(out=srow[3][:], in_=srow[2][:], func=AF.Ln,
                                               bias=eps_t[0:1, 0:1]),
                 reads=("srow2", "C:all"), writes=("srow3",), tag=f"{tagp}lnv{tt}")
            p.op("act", lambda e: e.activation(out=srow[3][:], in_=srow[3][:], func=AF.Exp,
                                               scale=-0.5),
                 reads=("srow3",), writes=("srow3",), tag=f"{tagp}rstd{tt}")
            p.op("pe", lambda e: e.matmul(pb[4][:, :], ones_row_f[:], srow[0][:], start=True, stop=True),
                 reads=("srow0", "C:all"), writes=(PB[4],), tag=f"{tagp}mb{tt}")
            p.op("pe", lambda e: e.matmul(pbs[:, :], ones_row_f[:], srow[3][:], start=True, stop=True),
                 reads=("srow3", "C:all"), writes=(PBS,), tag=f"{tagp}rb{tt}")

        def stat_mms(Xb, xb, dc, tagp, tt):
            p.op("pe", lambda e, xb=xb, dc=dc: e.matmul(
                pb[4][:1, :], ones_col_bf[:], xb[:], start=(dc == 0), stop=(dc == DC - 1)),
                reads=(Xb, "C:all"), writes=(PB[4],), tag=f"{tagp}s1.{tt}.{dc}")
            p.op("act", lambda e, xb=xb: e.activation(out=sqsl[:], in_=xb[:], func=AF.Square),
                 reads=(Xb,), writes=("sqsl0",), tag=f"{tagp}sq{tt}.{dc}")
            p.op("pe", lambda e, dc=dc: e.matmul(
                pb[5][:1, :], ones_col_bf[:], sqsl[:], start=(dc == 0), stop=(dc == DC - 1)),
                reads=("sqsl0", "C:all"), writes=(PB[5],), tag=f"{tagp}s2.{tt}.{dc}")

        def ln_pass(fs, src, src_tok_fn, th, tagp):
            lnres = fs["lnres"]

            def LNR(dc, tt):
                return lnres[:, dc * TH + tt * 512: dc * TH + (tt + 1) * 512]
            for tt in range(2):
                t0 = th * TH + tt * 512
                for dc in range(DC):
                    slot = dc % 3
                    xa, xb = xsl[slot], bfsl[slot]
                    Xa, Xb = f"xsl{slot}", f"bfsl{slot}"
                    p.op("sp", lambda e, dc=dc, t0=t0, xa=xa: e.dma_start(
                        out=xa[:], in_=src[dc * 128:(dc + 1) * 128, t0:t0 + 512]),
                        reads=src_tok_fn(dc, t0), writes=(Xa,), is_dma=True,
                        tag=f"{tagp}lx{tt}.{dc}", channel=Xa)
                    p.op("dve", lambda e, xa=xa, xb=xb: e.tensor_copy(xb[:], xa[:]),
                         reads=(Xa,), writes=(Xb,), tag=f"{tagp}cast{tt}.{dc}")
                    stat_mms(Xb, xb, dc, tagp, tt)
                stats_rows(tagp, tt)
                for dc in range(DC):
                    slot = dc % 3
                    xa = xsl[slot]
                    Xa = f"xsl{slot}"
                    p.op("sp", lambda e, dc=dc, t0=t0, xa=xa: e.dma_start(
                        out=xa[:], in_=src[dc * 128:(dc + 1) * 128, t0:t0 + 512]),
                        reads=src_tok_fn(dc, t0), writes=(Xa,), is_dma=True,
                        tag=f"{tagp}lx2{tt}.{dc}", channel=Xa)
                    p.op("dve", lambda e, xa=xa: e.tensor_tensor(
                        out=xa[:], in0=xa[:], in1=pb[4][:, :], op=ALU.subtract),
                        reads=(Xa, PB[4]), writes=(Xa,), tag=f"{tagp}sub{tt}.{dc}")
                    p.op("dve", lambda e, xa=xa, dc=dc, tt=tt: e.tensor_tensor(
                        out=LNR(dc, tt), in0=xa[:], in1=pbs[:, :], op=ALU.mult),
                        reads=(Xa, PBS), writes=("lnres" + fs["sfx"],), tag=f"{tagp}mul{tt}.{dc}")

        def ffn_phase(fs, src, src_tok_fn, dst, dst_name, ln_dst, ln_dst_name,
                      first, write_out, tagp):
            sfx = fs["sfx"]
            lnres, y_acc, gated = fs["lnres"], fs["y_acc"], fs["gated"]
            wg_r, wu_r, wd_r = fs["wg_r"], fs["wu_r"], fs["wd_r"]
            WRING, WDRING = len(wg_r), len(wd_r)
            LNTOK = "lnres" + sfx
            YTOK = "ya" + sfx

            def LNR(dc, tt):
                return lnres[:, dc * TH + tt * 512: dc * TH + (tt + 1) * 512]

            def YA(dc, tt):
                return y_acc[:, dc * TH + tt * 512: dc * TH + (tt + 1) * 512]

            for th in range(NTH):
                ln_pass(fs, src, src_tok_fn, th, tagp + f"ln{th}.")
                for hc in range(HC):
                    gslot, uslot = wg_r[hc % WRING], wu_r[hc % WRING]
                    Gs, Us = f"wg_r{hc % WRING}{sfx}", f"wu_r{hc % WRING}{sfx}"
                    dslot = wd_r[hc % WDRING]
                    Ds = f"wd_r{hc % WDRING}{sfx}"
                    if first and th == 0:
                        for (wsrc, slot, tok, bdst, bnm) in ((wgT, gslot, Gs, wgB, "wgB"),
                                                             (wuT, uslot, Us, wuB, "wuB")):
                            p.op("pool", lambda e, wsrc=wsrc, slot=slot, hc=hc: e.dma_start(
                                out=slot[:].rearrange("p (dc h) -> p dc h", dc=DC),
                                in_=wsrc[:, hc * 128:(hc + 1) * 128]
                                .rearrange("(dc p) h -> p dc h", p=128)),
                                writes=(tok,), is_dma=True, tag=f"{tagp}wl{hc}",
                                channel=f"wg_r{hc % WRING}" if wsrc is wgT else f"wu_r{hc % WRING}")
                            p.op("pool", lambda e, slot=slot, hc=hc, bdst=bdst: e.dma_start(
                                out=bdst[hc], in_=slot[:]),
                                reads=(tok,), writes=(f"wB:{bnm}:{hc}",),
                                is_dma=True, tag=f"{tagp}wb{hc}", channel=f"D:{bnm}")
                        p.op("pool", lambda e, hc=hc, dslot=dslot: e.dma_start(
                            out=dslot[:], in_=wdT[hc * 128:(hc + 1) * 128, :]),
                            writes=(Ds,), is_dma=True, tag=f"{tagp}wdl{hc}",
                            channel=f"wd_r{hc % WDRING}")
                        p.op("pool", lambda e, hc=hc, dslot=dslot: e.dma_start(
                            out=wdB[hc], in_=dslot[:]),
                            reads=(Ds,), writes=(f"wB:wdB:{hc}",), is_dma=True,
                            tag=f"{tagp}wdb{hc}", channel="D:wdB")
                    else:
                        for (bsrc, slot, tok, nm) in ((wgB, gslot, Gs, "wgB"), (wuB, uslot, Us, "wuB"),
                                                      (wdB, dslot, Ds, "wdB")):
                            p.op("sp", lambda e, bsrc=bsrc, slot=slot, hc=hc: e.dma_start(
                                out=slot[:], in_=bsrc[hc]),
                                reads=(f"wB:{nm}:{hc}",), writes=(tok,), is_dma=True,
                                tag=f"{tagp}wr{hc}",
                                channel=(f"wg_r{hc % WRING}" if nm == "wgB" else
                                         f"wu_r{hc % WRING}" if nm == "wuB" else
                                         f"wd_r{hc % WDRING}"))
                    for tt in range(2):
                        gsel = (hc * 2 + tt) % 3
                        usel = (hc * 2 + tt) % 2
                        bg, bu = pb[gsel], pb[3 + usel]
                        BG, BU = PB[gsel], PB[3 + usel]
                        sq = sqsl2[(hc * 2 + tt) % 2]
                        SQ = f"sqsl{(hc * 2 + tt) % 2}"

                        def mk_mm(slot, tt, bank):
                            def f(e):
                                for dc in range(DC):
                                    i = e.matmul(bank[:, :], slot[:, dc * 128:(dc + 1) * 128],
                                                 LNR(dc, tt), start=(dc == 0), stop=(dc == DC - 1))
                                return i
                            return f
                        p.op("pe", mk_mm(gslot, tt, bg), reads=(Gs, LNTOK), writes=(BG,),
                             tag=f"{tagp}gm{hc}.{tt}")
                        p.op("pe", mk_mm(uslot, tt, bu), reads=(Us, LNTOK), writes=(BU,),
                             tag=f"{tagp}um{hc}.{tt}")
                        gt = gated[hc % SUP]
                        GT = f"gated{hc % SUP}{sfx}"
                        p.op("act", lambda e, bank=bg, sq=sq: e.activation(
                            out=sq[:], in_=bank[:, :], func=AF.Silu),
                            reads=(BG,), writes=(SQ,), tag=f"{tagp}silu{hc}.{tt}")
                        p.op("dve", lambda e, gt=gt, tt=tt, bank=bu, sq=sq: e.tensor_tensor(
                            out=gt[:, tt * 512:(tt + 1) * 512], in0=sq[:], in1=bank[:, :],
                            op=ALU.mult),
                            reads=(SQ, BU), writes=(GT,), tag=f"{tagp}gated{hc}.{tt}")
                    if hc % SUP == SUP - 1:
                        s0 = hc - SUP + 1
                        for dc in range(DC):
                            for tt in range(2):
                                bank = (pb[5], pbs)[(dc * 2 + tt) % 2]
                                BK = (PB[5], PBS)[(dc * 2 + tt) % 2]

                                def mmd(e, s0=s0, dc=dc, tt=tt, bank=bank):
                                    for j in range(SUP):
                                        i = e.matmul(
                                            bank[:, :],
                                            wd_r[(s0 + j) % WDRING][:, dc * 128:(dc + 1) * 128],
                                            gated[j][:, tt * 512:(tt + 1) * 512],
                                            start=(j == 0), stop=(j == SUP - 1))
                                    return i
                                p.op("pe", mmd,
                                     reads=tuple(f"wd_r{(s0 + j) % WDRING}{sfx}" for j in range(SUP))
                                     + tuple(f"gated{j}{sfx}" for j in range(SUP)),
                                     writes=(BK,), tag=f"{tagp}dm{hc}.{dc}.{tt}")
                                if s0 == 0:
                                    p.op("dve", lambda e, dc=dc, tt=tt, bank=bank: e.tensor_copy(
                                        YA(dc, tt), bank[:, :]),
                                        reads=(BK,), writes=(YTOK,), tag=f"{tagp}y0.{dc}.{tt}")
                                else:
                                    p.op("dve", lambda e, dc=dc, tt=tt, bank=bank: e.tensor_tensor(
                                        out=YA(dc, tt), in0=YA(dc, tt), in1=bank[:, :], op=ALU.add),
                                        reads=(BK, YTOK), writes=(YTOK,),
                                        tag=f"{tagp}ya{hc}.{dc}.{tt}")
                for tt in range(2):
                    t0 = th * TH + tt * 512
                    for dc in range(DC):
                        slot = dc % 3
                        xa = xsl[slot]
                        Xa = f"xsl{slot}"
                        p.op("sp", lambda e, dc=dc, t0=t0, xa=xa: e.dma_start(
                            out=xa[:], in_=src[dc * 128:(dc + 1) * 128, t0:t0 + 512]),
                            reads=src_tok_fn(dc, t0), writes=(Xa,), is_dma=True,
                            tag=f"{tagp}rx{tt}.{dc}", channel=Xa)
                        p.op("dve", lambda e, dc=dc, tt=tt, xa=xa: e.scalar_tensor_tensor(
                            out=YA(dc, tt), in0=YA(dc, tt), scalar=0.5, in1=xa[:],
                            op0=ALU.mult, op1=ALU.add),
                            reads=(Xa, YTOK), writes=(YTOK,), tag=f"{tagp}rr{tt}.{dc}")
                        if write_out:
                            p.out_dma("sp", lambda e, dc=dc, t0=t0, tt=tt: e.dma_start(
                                out=out[dc * 128:(dc + 1) * 128, t0:t0 + 512], in_=YA(dc, tt)),
                                reads=(YTOK,), tag=f"{tagp}ow{tt}.{dc}")
                        if dst is not None:
                            p.op("sp", lambda e, dc=dc, t0=t0, tt=tt: e.dma_start(
                                out=dst[dc * 128:(dc + 1) * 128, t0:t0 + 512], in_=YA(dc, tt)),
                                reads=(YTOK,), writes=(f"{dst_name}:{dc}:{t0}",), is_dma=True,
                                tag=f"{tagp}rw{tt}.{dc}", channel=f"D:{dst_name}")
                if ln_dst is not None:
                    for tt in range(2):
                        t0 = th * TH + tt * 512
                        for dc in range(DC):
                            slot = dc % 3
                            xb = bfsl[slot]
                            Xb = f"bfsl{slot}"
                            p.op("dve", lambda e, dc=dc, tt=tt, xb=xb: e.tensor_copy(
                                xb[:], YA(dc, tt)),
                                reads=(YTOK,), writes=(Xb,), tag=f"{tagp}rc{tt}.{dc}")
                            stat_mms(Xb, xb, dc, tagp + "r", tt)
                        stats_rows(tagp + "r", tt)
                        for dc in range(DC):
                            slot = dc % 3
                            xa, xb = xsl[slot], bfsl[slot]
                            Xa, Xb = f"xsl{slot}", f"bfsl{slot}"
                            p.op("dve", lambda e, xa=xa, dc=dc, tt=tt: e.tensor_tensor(
                                out=xa[:], in0=YA(dc, tt), in1=pb[4][:, :], op=ALU.subtract),
                                reads=(YTOK, PB[4]), writes=(Xa,), tag=f"{tagp}ra{tt}.{dc}")
                            p.op("dve", lambda e, xa=xa, xb=xb: e.tensor_tensor(
                                out=xb[:], in0=xa[:], in1=pbs[:, :], op=ALU.mult),
                                reads=(Xa, PBS), writes=(Xb,), tag=f"{tagp}rn{tt}.{dc}")
                            p.op("sp", lambda e, dc=dc, t0=t0, xb=xb: e.dma_start(
                                out=ln_dst[dc * 128:(dc + 1) * 128, t0:t0 + 512], in_=xb[:]),
                                reads=(Xb,), writes=(f"{ln_dst_name}:{dc}:{t0}",), is_dma=True,
                                tag=f"{tagp}lw{tt}.{dc}", channel=f"D:{ln_dst_name}")

        def alloc_attn_pre(stk):
            s2 = lambda n, sh, dt: stk.enter_context(
                nc.sbuf_tensor(n, sh, dt, unsafe_allow_realloc=True))
            a = {}
            a["ba_all"] = s2("ba_all", [64, S], F32)
            a["wba_s"] = s2("wba_s", [128, DC * 64], BF16)
            a["araw"] = s2("araw", [128, 512], F32)
            a["dt_b"] = s2("dt_b", [128, 512], F32)
            a["al_b"] = s2("al_b", [128, 512], F32)
            a["spa"] = s2("spa", [128, 512], F32)
            a["spb"] = s2("spb", [128, 512], F32)
            a["gof"] = s2("gof", [128, 512], F32)
            a["dtrow"] = s2("dtrow", [1, 512], F32)
            a["alrow"] = s2("alrow", [1, 512], F32)
            return a

        def alloc_attn_p1(stk):
            s2 = lambda n, sh, dt: stk.enter_context(
                nc.sbuf_tensor(n, sh, dt, unsafe_allow_realloc=True))
            a = {}
            a["lnr"] = [s2(f"lnrs{i}", [128, DC * 512], BF16) for i in range(2)]
            a["wq_r"] = [s2(f"wq_r{i}", [128, D], BF16) for i in range(6)]
            a["raw"] = s2("raw", [128, 515], F32)
            a["hist"] = s2("hist", [128, 4 * 4], F32)
            a["qhat"] = s2("qhat", [128, S], BF16)
            a["khat"] = s2("khat", [128, S], BF16)
            a["khat_tm"] = s2("khat_tm", [128, S], BF16)
            a["vtm"] = [s2(f"vtm{j}", [128, S], BF16) for j in range(2)]
            a["zstm"] = [s2(f"zstm{j}", [128, S], BF16) for j in range(2)]
            return a

        def alloc_attn_p2(stk):
            s2 = lambda n, sh, dt: stk.enter_context(
                nc.sbuf_tensor(n, sh, dt, unsafe_allow_realloc=True))
            a = {}
            for par in range(2):
                for nm in ("rq", "rk", "rkt", "rv0", "rv1", "rz0", "rz1"):
                    a[f"{nm}{par}"] = s2(f"{nm}{par}", [128, S], BF16)
            a["xg_st"] = [s2(f"xg_st{j}", [128, S], BF16) for j in range(2)]
            a["elow"] = s2("elow", [128, 256], F32)
            a["eup"] = s2("eup", [128, 256], F32)
            a["kk"] = s2("kk", [128, 256], F32)
            a["qkt"] = s2("qkt", [128, 256], F32)
            a["msc"] = s2("msc", [128, 256], F32)
            a["mcur"] = s2("mcur", [128, 256], BF16)
            a["mtall"] = s2("mtall", [128, 7 * 256], BF16)
            a["at"] = s2("at", [128, 256], BF16)
            a["xcur"] = s2("xcur", [128, 256], BF16)
            a["ucur"] = s2("ucur", [128, 256], BF16)
            a["oall"] = s2("oall", [128, 256], F32)
            a["zs"] = s2("zs", [128, 256], F32)
            a["sqa"] = s2("sqa", [128, 256], F32)
            a["rms2"] = s2("rms2", [128, 2], F32)
            a["xgn"] = s2("xgn", [128, 256], BF16)
            a["kts"] = s2("kts", [128, 256], BF16)
            a["tmp0"] = s2("tmp0", [128, 256], F32)
            return a

        def alloc_attn_persist(stk):
            s2 = lambda n, sh, dt: stk.enter_context(
                nc.sbuf_tensor(n, sh, dt, unsafe_allow_realloc=True))
            a = {}
            a["rhsD"] = s2("rhsD", [33, 4096], F32)
            a["lhsD"] = s2("lhsD", [33, NCH * 128], F32)
            a["btm"] = s2("btm", [128, 512], F32)
            a["ltm"] = s2("ltm", [128, 512], F32)
            a["lrtm"] = s2("lrtm", [128, 512], F32)
            a["gtm"] = s2("gtm", [128, 512], F32)
            a["grtm"] = s2("grtm", [128, 512], F32)
            a["gcb"] = s2("gcb", [128, 512], F32)
            a["states"] = s2("states", [128, NV * 128], F32)
            a["states_bf"] = s2("states_bf", [128, NV * 128], BF16)
            a["convs"] = s2("convs", [128, 64 * 4], F32)
            a["lcrow"] = s2("lcrow", [1, 512], F32)
            return a

        def attn_prepass(ap, an):
            pre_n = cfg.get("pre_n", 99)
            p.op("pool", lambda e: e.dma_start(
                out=an["convs"][:].rearrange("p (ci k) -> p ci k", ci=64),
                in_=convw.rearrange("(ci p) k -> p ci k", p=128)),
                writes=("convs",), is_dma=True, tag="convl", channel="convs")
            p.op("pool", lambda g: g.memset(an["rhsD"][0:32, :], 1.0),
                 writes=("rhsDc",), tag="pc0")
            p.op("pool", lambda g: g.affine_select(
                out=an["rhsD"][0:32, :], in_=an["rhsD"][0:32, :],
                compare_op=ALU.is_equal, fill=0.0, base=0,
                pattern=[[-2, 16], [-1, 2], [0, 128]], channel_multiplier=1),
                reads=("rhsDc",), writes=("rhsDc",), tag="pc1")
            p.op("pool", lambda g: g.memset(an["lhsD"][32:33, :], -1.0),
                 writes=("lhsDr",), tag="pc2")
            p.op("pool", lambda g: g.memset(an["states"][:], 0.0),
                 writes=("states",), tag="pc3")
            p.op("pool", lambda g: g.memset(an["states_bf"][:], 0.0),
                 writes=("states_bf",), tag="pc4")
            for (row, srcv, tok) in ((ap["dtrow"], dtb, "dtrow"), (ap["alrow"], alog, "alrow")):
                p.op("sp", lambda e, row=row, srcv=srcv: e.dma_start(
                    out=row[:, 0:32], in_=srcv.rearrange("(o f) -> o f", o=1)),
                    writes=(tok,), is_dma=True, tag=f"ld{tok}", channel=tok)

                def tile16(e, row=row):
                    for c in range(1, NCH):
                        r = e.tensor_copy(row[:, c * 32:(c + 1) * 32], row[:, 0:32])
                    return r
                p.op("dve", tile16, reads=(tok,), writes=(tok,), tag=f"tl{tok}")
            p.op("pe", lambda e: e.matmul(pb[4][:, :], ones_row_f[:], ap["dtrow"][:],
                                          start=True, stop=True),
                 reads=("dtrow", "C:all"), writes=(PB[4],), tag="dtbm")
            p.op("dve", lambda e: e.tensor_copy(ap["dt_b"][:], pb[4][:, :]),
                 reads=(PB[4],), writes=("dt_b",), tag="dtbe")
            p.op("pe", lambda e: e.matmul(pb[4][:, :], ones_row_f[:], ap["alrow"][:],
                                          start=True, stop=True),
                 reads=("alrow", "C:all"), writes=(PB[4],), tag="albm")
            p.op("act", lambda e: e.activation(out=ap["al_b"][:], in_=pb[4][:, :], func=AF.Exp),
                 reads=(PB[4],), writes=("al_b",), tag="albe")
            if pre_n < 2:
                return
            p.op("pool", lambda e: e.dma_start(
                out=ap["wba_s"][:].rearrange("p (dc o) -> p dc o", dc=DC),
                in_=wba.rearrange("(dc p) o -> p dc o", p=128)),
                writes=("wba_s",), is_dma=True, tag="wbal", channel="wba_s")
            for tt in range(4):
                lbuf = an["lnr"][tt % 2]
                LB = f"lnrs{tt % 2}"
                p.op("sp", lambda e, tt=tt, lbuf=lbuf: e.dma_start(
                    out=lbuf[:].rearrange("p (dc t) -> p dc t", dc=DC),
                    in_=ln2d[:, tt * 512:(tt + 1) * 512].rearrange("(dc p) t -> p dc t", p=128)),
                    reads=tuple(f"ln2d:{dc}:{tt * 512}" for dc in range(DC)),
                    writes=(LB,), is_dma=True, tag=f"balnr{tt}", channel=LB)

                def mmba(e, tt=tt, lbuf=lbuf):
                    for dc in range(DC):
                        i = e.matmul(pb[4][0:64, :], ap["wba_s"][:, dc * 64:(dc + 1) * 64],
                                     lbuf[:, dc * 512:(dc + 1) * 512],
                                     start=(dc == 0), stop=(dc == DC - 1))
                    return i
                p.op("pe", mmba, reads=("wba_s", LB), writes=(PB[4],), tag=f"bam{tt}")
                p.op("dve", lambda e, tt=tt: e.tensor_copy(
                    ap["ba_all"][:, tt * 512:(tt + 1) * 512], pb[4][0:64, :]),
                    reads=(PB[4],), writes=("ba_all",), tag=f"bae{tt}")
            if pre_n < 3:
                return
            for c in range(NCH):
                cs = slice(c * 128, (c + 1) * 128)
                p.op("pe", lambda e, cs=cs: e.transpose(pbs[:, 0:64], ap["ba_all"][:, cs],
                                                        ident_f[0:64, 0:64]),
                     reads=("ba_all", "C:all"), writes=(PBS,), tag=f"btr{c}")
                p.op("dve", lambda e, c=c: e.tensor_copy(
                    an["btm"][:, c * 32:(c + 1) * 32].rearrange("p (i j) -> p i j", j=2),
                    pbs[:, 0:64].rearrange("p (i r) -> p i r", r=4)[:, :, 0:2]),
                    reads=(PBS,), writes=("btm",), tag=f"bcp{c}")
                p.op("dve", lambda e, c=c: e.tensor_copy(
                    ap["araw"][:, c * 32:(c + 1) * 32].rearrange("p (i j) -> p i j", j=2),
                    pbs[:, 0:64].rearrange("p (i r) -> p i r", r=4)[:, :, 2:4]),
                    reads=(PBS,), writes=("araw",), tag=f"acp{c}")
            p.op("act", lambda e: e.activation(out=an["btm"][:], in_=an["btm"][:], func=AF.Sigmoid),
                 reads=("btm",), writes=("btm",), tag="bsig")
            p.op("dve", lambda e: e.tensor_tensor(out=ap["araw"][:], in0=ap["araw"][:],
                                                  in1=ap["dt_b"][:], op=ALU.add),
                 reads=("araw", "dt_b"), writes=("araw",), tag="adt")
            p.op("act", lambda e: e.activation(out=ap["spa"][:], in_=ap["araw"][:], func=AF.Abs),
                 reads=("araw",), writes=("spa",), tag="sp_abs")
            p.op("act", lambda e: e.activation(out=ap["spa"][:], in_=ap["spa"][:], func=AF.Exp,
                                               scale=-1.0),
                 reads=("spa",), writes=("spa",), tag="sp_exp")
            p.op("act", lambda e: e.activation(out=ap["spa"][:], in_=ap["spa"][:], func=AF.Ln,
                                               bias=eps_t[:, 3:4]),
                 reads=("spa", "C:all"), writes=("spa",), tag="sp_ln")
            p.op("act", lambda e: e.activation(out=ap["spb"][:], in_=ap["araw"][:], func=AF.Relu),
                 reads=("araw",), writes=("spb",), tag="sp_relu")
            p.op("dve", lambda e: e.tensor_tensor(out=ap["spa"][:], in0=ap["spa"][:],
                                                  in1=ap["spb"][:], op=ALU.add),
                 reads=("spa", "spb"), writes=("spa",), tag="sp_add")
            p.op("dve", lambda e: e.scalar_tensor_tensor(out=ap["gof"][:], in0=ap["spa"][:],
                                                         scalar=-1.0, in1=ap["al_b"][:],
                                                         op0=ALU.mult, op1=ALU.mult),
                 reads=("spa", "al_b"), writes=("gof",), tag="gmul")
            if pre_n < 4:
                return
            for c in range(NCH):
                p.op("pe", lambda e, c=c: e.matmul(
                    pb[4][:, 0:32], triU_inc[:], ap["gof"][:, c * 32:(c + 1) * 32],
                    start=True, stop=True),
                    reads=("gof", "C:all"), writes=(PB[4],), tag=f"lmm{c}")
                p.op("pe", lambda e, c=c: e.matmul(
                    pb[4][:, 32:64], triU_str[:], ap["gof"][:, c * 32:(c + 1) * 32],
                    start=True, stop=True),
                    reads=("gof", "C:all"), writes=(PB[4],), tag=f"lrmm{c}")
                p.op("dve", lambda e, c=c: e.tensor_scalar(
                    out=an["ltm"][:, c * 32:(c + 1) * 32], in0=pb[4][:, 0:32],
                    scalar1=0.0, scalar2=None, op0=ALU.min),
                    reads=(PB[4],), writes=("ltm",), tag=f"lcp{c}")
                p.op("dve", lambda e, c=c: e.tensor_scalar(
                    out=an["lrtm"][:, c * 32:(c + 1) * 32], in0=pb[4][:, 32:64],
                    scalar1=0.0, scalar2=None, op0=ALU.min),
                    reads=(PB[4],), writes=("lrtm",), tag=f"lrcp{c}")
                p.op("pe", lambda e, c=c: e.transpose(
                    pbs[0:32, 128:256], an["ltm"][:, c * 32:(c + 1) * 32], ident_f[:]),
                    reads=("ltm", "C:all"), writes=(PBS,), tag=f"lhm{c}")
                p.op("dve", lambda e, c=c: e.tensor_copy(
                    an["lhsD"][0:32, c * 128:(c + 1) * 128], pbs[0:32, 128:256]),
                    reads=(PBS,), writes=("lhsD",), tag=f"lhme{c}")
                p.op("sp", lambda e, c=c: e.dma_start(
                    out=lhd[c], in_=an["lhsD"][0:32, c * 128:(c + 1) * 128]),
                    reads=("lhsD",), writes=(f"lhd:{c}",), is_dma=True, tag=f"lhw{c}",
                    channel="lhd")
            if pre_n < 5:
                return
            p.op("act", lambda e: e.activation(out=an["gtm"][:], in_=an["ltm"][:], func=AF.Exp),
                 reads=("ltm",), writes=("gtm",), tag="gexp")
            p.op("act", lambda e: e.activation(out=an["grtm"][:], in_=an["lrtm"][:], func=AF.Exp),
                 reads=("lrtm",), writes=("grtm",), tag="grexp")
            p.op("dve", lambda e: e.tensor_tensor(out=an["lcrow"][:], in0=an["ltm"][0:1, :],
                                                  in1=an["lrtm"][0:1, :], op=ALU.add),
                 reads=("ltm", "lrtm"), writes=("lcrow",), tag="lcadd")
            p.op("act", lambda e: e.activation(out=an["lcrow"][:], in_=an["lcrow"][:], func=AF.Exp),
                 reads=("lcrow",), writes=("lcrow",), tag="lcexp")
            p.op("pe", lambda e: e.matmul(pb[4][:, :], ones_row_f[:], an["lcrow"][:],
                                          start=True, stop=True),
                 reads=("lcrow", "C:all"), writes=(PB[4],), tag="gcmm")
            p.op("dve", lambda e: e.tensor_copy(an["gcb"][:], pb[4][:, :]),
                 reads=(PB[4],), writes=("gcb",), tag="gce")

        def attn_head_proj(an, aa, i, taps):
            base = 768 * i
            occols = [base, base + 128, base + 256, base + 384, base + 512, base + 640]
            convci = [i, 16 + i, 32 + 2 * i, 32 + 2 * i + 1]
            p.op("dve", lambda e: e.memset(aa["hist"][:], 0.0), writes=("hist",),
                 tag=f"hz{i}")
            for tt in range(4):
                lbuf = an["lnr"][tt % 2]
                LB = f"lnrs{tt % 2}"
                p.op("sp", lambda e, tt=tt, lbuf=lbuf: e.dma_start(
                    out=lbuf[:].rearrange("p (dc t) -> p dc t", dc=DC),
                    in_=ln2d[:, tt * 512:(tt + 1) * 512].rearrange("(dc p) t -> p dc t", p=128)),
                    reads=tuple(f"ln2d:{dc}:{tt * 512}" for dc in range(DC)),
                    writes=(LB,), is_dma=True, tag=f"plnr{i}.{tt}", channel=LB)
                for oc in range(6):
                    slot = aa["wq_r"][oc]
                    SL = f"wq_r{oc}"
                    if tt == 0:
                        p.op("pool", lambda e, oc=oc, slot=slot: e.dma_start(
                            out=slot[:].rearrange("p (dc o) -> p dc o", dc=DC),
                            in_=wqkvz[:, occols[oc]:occols[oc] + 128]
                            .rearrange("(dc p) o -> p dc o", p=128)),
                            writes=(SL,), is_dma=True, tag=f"pwl{i}.{oc}", channel=SL)
                    bank = pb[(tt * 6 + oc) % 4]
                    BK = PB[(tt * 6 + oc) % 4]

                    def mmq(e, slot=slot, lbuf=lbuf, bank=bank):
                        for dc in range(DC):
                            r = e.matmul(bank[:, :], slot[:, dc * 128:(dc + 1) * 128],
                                         lbuf[:, dc * 512:(dc + 1) * 512],
                                         start=(dc == 0), stop=(dc == DC - 1))
                        return r
                    p.op("pe", mmq, reads=(SL, LB), writes=(BK,), tag=f"pm{i}.{oc}.{tt}")
                    if oc < 4:
                        p.op("dve", lambda e, oc=oc: e.tensor_copy(
                            aa["raw"][:, 0:3], aa["hist"][:, oc * 4:oc * 4 + 3]),
                            reads=("hist",), writes=("raw",), tag=f"ph{i}.{oc}.{tt}")
                        p.op("dve", lambda e, bank=bank: e.tensor_copy(
                            aa["raw"][:, 3:515], bank[:, :]),
                            reads=(BK, "raw"), writes=("raw",), tag=f"pe{i}.{oc}.{tt}")
                        p.op("dve", lambda e, oc=oc: e.tensor_copy(
                            aa["hist"][:, oc * 4:oc * 4 + 3], aa["raw"][:, 512:515]),
                            reads=("raw",), writes=("hist",), tag=f"phs{i}.{oc}.{tt}")
                        ci = convci[oc]
                        p.op("dve", lambda e, ci=ci: e.tensor_scalar(
                            out=xsl[0][:], in0=aa["raw"][:, 0:512],
                            scalar1=an["convs"][:, ci * 4:ci * 4 + 1],
                            scalar2=None, op0=ALU.mult),
                            reads=("raw", "convs"), writes=("xsl0",),
                            tag=f"pc{i}.{oc}.{tt}.0")
                        for j in range(1, 4):
                            p.op("dve", lambda e, ci=ci, j=j: e.scalar_tensor_tensor(
                                out=xsl[0][:], in0=aa["raw"][:, j:j + 512],
                                scalar=an["convs"][:, ci * 4 + j:ci * 4 + j + 1],
                                in1=xsl[0][:], op0=ALU.mult, op1=ALU.add),
                                reads=("raw", "convs", "xsl0"), writes=("xsl0",),
                                tag=f"pc{i}.{oc}.{tt}.{j}")
                        if oc < 2:
                            dstb = aa["qhat"] if oc == 0 else aa["khat"]
                            DT = "qhat" if oc == 0 else "khat"
                            p.op("act", lambda e, dstb=dstb, tt=tt: e.activation(
                                out=dstb[:, tt * 512:(tt + 1) * 512], in_=xsl[0][:], func=AF.Silu),
                                reads=("xsl0",), writes=(DT,), tag=f"ps{i}.{oc}.{tt}")
                        else:
                            j = oc - 2
                            p.op("act", lambda e: e.activation(
                                out=bfsl[0][:], in_=xsl[0][:], func=AF.Silu),
                                reads=("xsl0",), writes=("bfsl0",), tag=f"ps{i}.{oc}.{tt}")

                            def vtr(e):
                                for k in range(4):
                                    r = e.transpose(pbt[:, k * 128:(k + 1) * 128],
                                                    bfsl[0][:, k * 128:(k + 1) * 128], ident_bf[:])
                                return r
                            p.op("pe", vtr, reads=("bfsl0", "C:all"), writes=(PBT,),
                                 tag=f"pvt{i}.{oc}.{tt}")
                            p.op("dve", lambda e, j=j, tt=tt: e.tensor_copy(
                                aa["vtm"][j][:, tt * 512:(tt + 1) * 512], pbt[:, 0:512]),
                                reads=(PBT,), writes=(f"vtm{j}",), tag=f"pve{i}.{oc}.{tt}")
                    else:
                        j = oc - 4
                        p.op("act", lambda e, bank=bank: e.activation(
                            out=bfsl[0][:], in_=bank[:, :], func=AF.Silu),
                            reads=(BK,), writes=("bfsl0",), tag=f"pz{i}.{oc}.{tt}")

                        def ztr(e):
                            for k in range(4):
                                r = e.transpose(pbt[:, k * 128:(k + 1) * 128],
                                                bfsl[0][:, k * 128:(k + 1) * 128], ident_bf[:])
                            return r
                        p.op("pe", ztr, reads=("bfsl0", "C:all"), writes=(PBT,),
                             tag=f"pzt{i}.{oc}.{tt}")
                        p.op("dve", lambda e, j=j, tt=tt: e.tensor_copy(
                            aa["zstm"][j][:, tt * 512:(tt + 1) * 512], pbt[:, 0:512]),
                            reads=(PBT,), writes=(f"zstm{j}",), tag=f"pze{i}.{oc}.{tt}")
            for (nm, scale_bias_col) in (("qhat", 4), ("khat", 5)):
                buf = aa[nm]
                for tt in range(4):
                    sl = slice(tt * 512, (tt + 1) * 512)
                    p.op("act", lambda e, buf=buf, sl=sl: e.activation(
                        out=sqsl[:], in_=buf[:, sl], func=AF.Square),
                        reads=(nm,), writes=("sqsl0",), tag=f"n2s{i}.{nm}.{tt}")
                    p.op("pe", lambda e: e.matmul(pb[4][0:1, :], ones_col_bf[:], sqsl[:],
                                                  start=True, stop=True),
                         reads=("sqsl0", "C:all"), writes=(PB[4],), tag=f"n2m{i}.{nm}.{tt}")
                    p.op("act", lambda e: e.activation(out=srow[0][:], in_=pb[4][0:1, :], func=AF.Ln,
                                                       bias=eps_t[0:1, 1:2]),
                         reads=(PB[4], "C:all"), writes=("srow0",), tag=f"n2l{i}.{nm}.{tt}")
                    p.op("act", lambda e, sc=scale_bias_col: e.activation(
                        out=srow[0][:], in_=srow[0][:], func=AF.Exp, scale=-0.5,
                        bias=eps_t[0:1, sc:sc + 1]),
                        reads=("srow0", "C:all"), writes=("srow0",), tag=f"n2e{i}.{nm}.{tt}")
                    p.op("pe", lambda e: e.matmul(pb[5][:, :], ones_row_f[:], srow[0][:],
                                                  start=True, stop=True),
                         reads=("srow0", "C:all"), writes=(PB[5],), tag=f"n2b{i}.{nm}.{tt}")
                    p.op("dve", lambda e, buf=buf, sl=sl: e.tensor_tensor(
                        out=buf[:, sl], in0=buf[:, sl], in1=pb[5][:, :], op=ALU.mult),
                        reads=(nm, PB[5]), writes=(nm,), tag=f"n2n{i}.{nm}.{tt}")
            for tt in range(4):
                def ktr(e, tt=tt):
                    for k in range(4):
                        r = e.transpose(pbt[:, k * 128:(k + 1) * 128],
                                        aa["khat"][:, tt * 512 + k * 128:tt * 512 + (k + 1) * 128],
                                        ident_bf[:])
                    return r
                p.op("pe", ktr, reads=("khat", "C:all"), writes=(PBT,), tag=f"ktt{i}.{tt}")
                p.op("dve", lambda e, tt=tt: e.tensor_copy(
                    aa["khat_tm"][:, tt * 512:(tt + 1) * 512], pbt[:, 0:512]),
                    reads=(PBT,), writes=("khat_tm",), tag=f"kte{i}.{tt}")
            for (nm, dst) in (("qhat", qkd[i, 0]), ("khat", qkd[i, 1]), ("khat_tm", qkd[i, 2])):
                p.op("sp", lambda e, nm=nm, dst=dst: e.dma_start(out=dst, in_=aa[nm][:]),
                     reads=(nm,), writes=(f"D:{nm}:{i}",), is_dma=True,
                     tag=f"pq{i}.{nm}", channel="D:qkd")
            for j in range(2):
                h = 2 * i + j
                p.op("sp", lambda e, j=j, h=h: e.dma_start(out=vzd[h, 0], in_=aa["vtm"][j][:]),
                     reads=(f"vtm{j}",), writes=(f"D:v:{h}",), is_dma=True,
                     tag=f"pv{i}.{j}", channel="D:vzd")
                p.op("sp", lambda e, j=j, h=h: e.dma_start(out=vzd[h, 1], in_=aa["zstm"][j][:]),
                     reads=(f"zstm{j}",), writes=(f"D:z:{h}",), is_dma=True,
                     tag=f"pz{i}.{j}", channel="D:vzd")

        def attn_head_rec(an, aa, i, taps):
            par = i % 2
            Rq, Rk, Rkt = f"rq{par}", f"rk{par}", f"rkt{par}"
            qhat, khat, khat_tm = aa[f"rq{par}"], aa[f"rk{par}"], aa[f"rkt{par}"]
            vtm = [aa[f"rv0{par}"], aa[f"rv1{par}"]]
            zstm = [aa[f"rz0{par}"], aa[f"rz1{par}"]]
            VT = [f"rv0{par}", f"rv1{par}"]
            ZT = [f"rz0{par}", f"rz1{par}"]
            for (tok, buf, srcap, rdt) in ((Rq, qhat, qkd[i, 0], f"D:qhat:{i}"),
                                           (Rk, khat, qkd[i, 1], f"D:khat:{i}"),
                                           (Rkt, khat_tm, qkd[i, 2], f"D:khat_tm:{i}")):
                p.op("sp", lambda e, buf=buf, srcap=srcap: e.dma_start(out=buf[:], in_=srcap),
                     reads=(rdt,), writes=(tok,), is_dma=True, tag=f"rl{i}.{tok}", channel=tok)
            for j in range(2):
                h = 2 * i + j
                p.op("sp", lambda e, j=j, h=h: e.dma_start(out=vtm[j][:], in_=vzd[h, 0]),
                     reads=(f"D:v:{h}",), writes=(VT[j],), is_dma=True,
                     tag=f"rlv{i}.{j}", channel=VT[j])
                p.op("sp", lambda e, j=j, h=h: e.dma_start(out=zstm[j][:], in_=vzd[h, 1]),
                     reads=(f"D:z:{h}",), writes=(ZT[j],), is_dma=True,
                     tag=f"rlz{i}.{j}", channel=ZT[j])
            g = i
            def vcol(buf, j, c):
                return buf[:, c * 32 + 2 * i + j: c * 32 + 2 * i + j + 1]
            for c in range(NCH):
                cs = slice(c * 128, (c + 1) * 128)
                p.op("sp", lambda e, c=c: e.dma_start(
                    out=an["rhsD"][32:33, g * 256:(g + 1) * 256],
                    in_=lhd[c, 2 * i:2 * i + 2].rearrange("(o a) b -> o (a b)", o=1)),
                    reads=(f"lhd:{c}",), writes=("rhsDr",), is_dma=True, tag=f"rr{i}.{c}",
                    channel="rhsrow")
                p.op("pe", lambda e, c=c: e.matmul(
                    pb[0][:, 0:256], an["lhsD"][:, c * 128:(c + 1) * 128],
                    an["rhsD"][:, g * 256:(g + 1) * 256], start=True, stop=True),
                    reads=("lhsD", "lhsDr", "rhsDc", "rhsDr"), writes=(PB[0],), tag=f"dm{i}.{c}")
                p.op("dve", lambda e: e.tensor_tensor(out=aa["elow"][:], in0=pb[0][:, 0:256],
                                                      in1=mask_low[:, 0:256], op=ALU.add),
                     reads=(PB[0], "C:all"), writes=("elow",), tag=f"em{i}.{c}")
                p.op("act", lambda e: e.activation(out=aa["elow"][:], in_=aa["elow"][:], func=AF.Exp),
                     reads=("elow",), writes=("elow",), tag=f"ee{i}.{c}")

                def eutr(e):
                    for k in range(2):
                        r = e.transpose(pbs[:, k * 128:(k + 1) * 128],
                                        aa["elow"][:, k * 128:(k + 1) * 128], ident_f[:])
                    return r
                p.op("pe", eutr, reads=("elow", "C:all"), writes=(PBS,), tag=f"eut{i}.{c}")
                p.op("dve", lambda e: e.tensor_copy(aa["eup"][:], pbs[:, 0:256]),
                     reads=(PBS,), writes=("eup",), tag=f"eue{i}.{c}")
                p.op("pe", lambda e, cs=cs: e.matmul(pb[2][:, 0:128], khat[:, cs],
                                                     khat[:, cs], start=True, stop=True),
                     reads=(Rk,), writes=(PB[2],), tag=f"kk{i}.{c}")
                p.op("pe", lambda e, cs=cs: e.matmul(pb[2][:, 128:256], khat[:, cs],
                                                     qhat[:, cs], start=True, stop=True),
                     reads=(Rk, Rq), writes=(PB[2],), tag=f"qk{i}.{c}")

                def kke(e):
                    e.tensor_copy(aa["kk"][:, 0:128], pb[2][:, 0:128])
                    e.tensor_copy(aa["kk"][:, 128:256], pb[2][:, 0:128])
                    e.tensor_copy(aa["qkt"][:, 0:128], pb[2][:, 128:256])
                    return e.tensor_copy(aa["qkt"][:, 128:256], pb[2][:, 128:256])
                p.op("dve", kke, reads=(PB[2],), writes=("kk", "qkt"), tag=f"kke{i}.{c}")
                p.op("dve", lambda e: e.tensor_tensor(out=aa["elow"][:], in0=aa["elow"][:],
                                                      in1=ident_t4[:, 0:256], op=ALU.subtract),
                     reads=("elow", "C:all"), writes=("elow",), tag=f"es{i}.{c}")
                p.op("dve", lambda e: e.tensor_tensor(out=aa["msc"][:], in0=aa["elow"][:],
                                                      in1=aa["kk"][:], op=ALU.mult),
                     reads=("elow", "kk"), writes=("msc",), tag=f"ms{i}.{c}")
                for j in range(2):
                    p.op("dve", lambda e, j=j, c=c: e.tensor_scalar(
                        out=aa["mcur"][:, j * 128:(j + 1) * 128],
                        in0=aa["msc"][:, j * 128:(j + 1) * 128],
                        scalar1=vcol(an["btm"], j, c), scalar2=None, op0=ALU.mult),
                        reads=("msc", "btm"), writes=("mcur",), tag=f"mb{i}.{c}.{j}")
                p.op("dve", lambda e: e.tensor_tensor(out=aa["at"][:], in0=aa["eup"][:],
                                                      in1=aa["qkt"][:], op=ALU.mult),
                     reads=("eup", "qkt"), writes=("at",), tag=f"at{i}.{c}")

                def mtr(e):
                    for j in range(2):
                        r = e.transpose(pbt[:, j * 128:(j + 1) * 128],
                                        aa["mcur"][:, j * 128:(j + 1) * 128], ident_bf[:])
                    return r
                p.op("pe", mtr, reads=("mcur", "C:all"), writes=(PBT,), tag=f"mt{i}.{c}")
                p.op("dve", lambda e: e.tensor_copy(aa["mtall"][:, 0:256], pbt[:, 0:256]),
                     reads=(PBT,), writes=("mtall",), tag=f"mte{i}.{c}")
                for L in range(1, 7):
                    def sqm(e, L=L):
                        for j in range(2):
                            js = slice(j * 128, (j + 1) * 128)
                            e.matmul(pb[4][:, j * 128:(j + 1) * 128],
                                     aa["mtall"][:, (L - 1) * 256 + j * 128:(L - 1) * 256 + (j + 1) * 128],
                                     aa["mcur"][:, js], start=True, stop=True)
                            r = e.matmul(pb[5][:, j * 128:(j + 1) * 128],
                                         aa["mcur"][:, js],
                                         aa["mtall"][:, (L - 1) * 256 + j * 128:(L - 1) * 256 + (j + 1) * 128],
                                         start=True, stop=True)
                        return r
                    p.op("pe", sqm, reads=("mtall", "mcur"), writes=(PB[4], PB[5]),
                         tag=f"sq{i}.{c}.{L}")
                    if L < 6:
                        p.op("dve", lambda e: e.tensor_copy(aa["mcur"][:], pb[4][:, 0:256]),
                             reads=(PB[4],), writes=("mcur",), tag=f"sqe{i}.{c}.{L}")
                    p.op("dve", lambda e, L=L: e.tensor_copy(
                        aa["mtall"][:, L * 256:(L + 1) * 256], pb[5][:, 0:256]),
                        reads=(PB[5],), writes=("mtall",), tag=f"sqt{i}.{c}.{L}")
                for j in range(2):
                    h = 2 * i + j
                    p.op("pe", lambda e, cs=cs, j=j, h=h: e.matmul(
                        pb[0][:, j * 128:(j + 1) * 128], khat[:, cs],
                        an["states_bf"][:, h * 128:(h + 1) * 128], start=True, stop=True),
                        reads=(Rk, "states_bf"), writes=(PB[0],), tag=f"ks{i}.{c}.{j}")
                for j in range(2):
                    p.op("dve", lambda e, j=j, c=c: e.tensor_scalar(
                        out=aa["tmp0"][:, j * 128:(j + 1) * 128],
                        in0=pb[0][:, j * 128:(j + 1) * 128],
                        scalar1=vcol(an["gtm"], j, c), scalar2=None, op0=ALU.mult),
                        reads=(PB[0], "gtm"), writes=("tmp0",), tag=f"bv1{i}.{c}.{j}")
                    p.op("dve", lambda e, j=j, cs=cs: e.tensor_tensor(
                        out=aa["tmp0"][:, j * 128:(j + 1) * 128],
                        in0=vtm[j][:, cs], in1=aa["tmp0"][:, j * 128:(j + 1) * 128],
                        op=ALU.subtract),
                        reads=(VT[j], "tmp0"), writes=("tmp0",), tag=f"bv2{i}.{c}.{j}")
                    p.op("dve", lambda e, j=j, c=c: e.tensor_scalar(
                        out=aa["xcur"][:, j * 128:(j + 1) * 128],
                        in0=aa["tmp0"][:, j * 128:(j + 1) * 128],
                        scalar1=vcol(an["btm"], j, c), scalar2=None, op0=ALU.mult),
                        reads=("tmp0", "btm"), writes=("xcur",), tag=f"bv3{i}.{c}.{j}")
                for L in range(6, -1, -1):
                    def apm(e, L=L):
                        for j in range(2):
                            js = slice(j * 128, (j + 1) * 128)
                            r = e.matmul(pb[1][:, j * 128:(j + 1) * 128],
                                         aa["mtall"][:, L * 256 + j * 128:L * 256 + (j + 1) * 128],
                                         aa["xcur"][:, js], start=True, stop=True)
                        return r
                    p.op("pe", apm, reads=("mtall", "xcur"), writes=(PB[1],), tag=f"ap{i}.{c}.{L}")
                    if L > 0:
                        p.op("dve", lambda e: e.tensor_tensor(
                            out=aa["xcur"][:], in0=aa["xcur"][:], in1=pb[1][:, 0:256], op=ALU.add),
                            reads=(PB[1], "xcur"), writes=("xcur",), tag=f"apa{i}.{c}.{L}")
                    else:
                        p.op("dve", lambda e: e.tensor_tensor(
                            out=aa["ucur"][:], in0=aa["xcur"][:], in1=pb[1][:, 0:256],
                            op=ALU.subtract),
                            reads=(PB[1], "xcur"), writes=("ucur",), tag=f"apu{i}.{c}")
                for j in range(2):
                    h = 2 * i + j
                    p.op("pe", lambda e, cs=cs, j=j, h=h: e.matmul(
                        pb[2][:, j * 128:(j + 1) * 128], qhat[:, cs],
                        an["states_bf"][:, h * 128:(h + 1) * 128], start=True, stop=True),
                        reads=(Rq, "states_bf"), writes=(PB[2],), tag=f"qs{i}.{c}.{j}")
                    p.op("pe", lambda e, j=j: e.matmul(
                        pb[3][:, j * 128:(j + 1) * 128],
                        aa["at"][:, j * 128:(j + 1) * 128],
                        aa["ucur"][:, j * 128:(j + 1) * 128], start=True, stop=True),
                        reads=("at", "ucur"), writes=(PB[3],), tag=f"au{i}.{c}.{j}")
                for j in range(2):
                    p.op("dve", lambda e, j=j, c=c: e.tensor_scalar(
                        out=aa["oall"][:, j * 128:(j + 1) * 128],
                        in0=pb[2][:, j * 128:(j + 1) * 128],
                        scalar1=vcol(an["gtm"], j, c), scalar2=None, op0=ALU.mult),
                        reads=(PB[2], "gtm"), writes=("oall",), tag=f"oi{i}.{c}.{j}")
                p.op("dve", lambda e: e.tensor_tensor(out=aa["oall"][:], in0=aa["oall"][:],
                                                      in1=pb[3][:, 0:256], op=ALU.add),
                     reads=(PB[3], "oall"), writes=("oall",), tag=f"oa{i}.{c}")
                for j in range(2):
                    p.op("dve", lambda e, j=j, cs=cs: e.tensor_copy(
                        aa["zs"][:, j * 128:(j + 1) * 128], zstm[j][:, cs]),
                        reads=(ZT[j],), writes=("zs",), tag=f"zg{i}.{c}.{j}")
                p.op("dve", lambda e: e.tensor_tensor(out=aa["oall"][:], in0=aa["oall"][:],
                                                      in1=aa["zs"][:], op=ALU.mult),
                     reads=("zs", "oall"), writes=("oall",), tag=f"xg{i}.{c}")
                p.op("act", lambda e: e.activation(out=aa["sqa"][:], in_=aa["oall"][:],
                                                   func=AF.Square),
                     reads=("oall",), writes=("sqa",), tag=f"xs{i}.{c}")
                p.op("dve", lambda e: e.tensor_reduce(
                    out=aa["rms2"][:], in_=aa["sqa"][:].rearrange("p (j v) -> p j v", j=2),
                    axis=AX.X, op=ALU.add),
                    reads=("sqa",), writes=("rms2",), tag=f"xr{i}.{c}")
                p.op("act", lambda e: e.activation(out=aa["rms2"][:], in_=aa["rms2"][:], func=AF.Ln,
                                                   scale=1.0 / DV, bias=eps_t[:, 2:3]),
                     reads=("rms2", "C:all"), writes=("rms2",), tag=f"xl{i}.{c}")
                p.op("act", lambda e: e.activation(out=aa["rms2"][:], in_=aa["rms2"][:], func=AF.Exp,
                                                   scale=-0.5),
                     reads=("rms2",), writes=("rms2",), tag=f"xe{i}.{c}")
                for j in range(2):
                    p.op("dve", lambda e, j=j: e.tensor_scalar(
                        out=aa["oall"][:, j * 128:(j + 1) * 128],
                        in0=aa["oall"][:, j * 128:(j + 1) * 128],
                        scalar1=aa["rms2"][:, j:j + 1], scalar2=None, op0=ALU.mult),
                        reads=("rms2", "oall"), writes=("oall",), tag=f"xn{i}.{c}.{j}")
                p.op("dve", lambda e: e.tensor_tensor(out=aa["xgn"][:], in0=aa["oall"][:],
                                                      in1=nw_bcast[:, 0:256], op=ALU.mult),
                     reads=("oall", "C:nw_bcast"), writes=("xgn",), tag=f"xw{i}.{c}")

                def xtr(e):
                    for j in range(2):
                        r = e.transpose(pbt[:, j * 128:(j + 1) * 128],
                                        aa["xgn"][:, j * 128:(j + 1) * 128], ident_bf[:])
                    return r
                p.op("pe", xtr, reads=("xgn", "C:all"), writes=(PBT,), tag=f"xt{i}.{c}")
                for j in range(2):
                    p.op("dve", lambda e, j=j, cs=cs: e.tensor_copy(
                        aa["xg_st"][j][:, cs], pbt[:, j * 128:(j + 1) * 128]),
                        reads=(PBT,), writes=(f"xg_st{j}",), tag=f"xse{i}.{c}.{j}")
                for j in range(2):
                    p.op("dve", lambda e, j=j, cs=cs, c=c: e.tensor_scalar(
                        out=aa["kts"][:, j * 128:(j + 1) * 128],
                        in0=khat_tm[:, cs],
                        scalar1=vcol(an["grtm"], j, c), scalar2=None, op0=ALU.mult),
                        reads=(Rkt, "grtm"), writes=("kts",), tag=f"kt{i}.{c}.{j}")
                for j in range(2):
                    p.op("pe", lambda e, j=j: e.matmul(
                        pb[4][:, j * 128:(j + 1) * 128],
                        aa["kts"][:, j * 128:(j + 1) * 128],
                        aa["ucur"][:, j * 128:(j + 1) * 128], start=True, stop=True),
                        reads=("kts", "ucur"), writes=(PB[4],), tag=f"st{i}.{c}.{j}")
                for j in range(2):
                    h = 2 * i + j
                    p.op("dve", lambda e, j=j, h=h, c=c: e.tensor_scalar(
                        out=an["states"][:, h * 128:(h + 1) * 128],
                        in0=an["states"][:, h * 128:(h + 1) * 128],
                        scalar1=vcol(an["gcb"], j, c), scalar2=None, op0=ALU.mult),
                        reads=("states", "gcb"), writes=("states",), tag=f"ss{i}.{c}.{j}")
                    p.op("dve", lambda e, j=j, h=h: e.tensor_tensor(
                        out=an["states"][:, h * 128:(h + 1) * 128],
                        in0=an["states"][:, h * 128:(h + 1) * 128],
                        in1=pb[4][:, j * 128:(j + 1) * 128], op=ALU.add),
                        reads=("states", PB[4]), writes=("states",), tag=f"sa{i}.{c}.{j}")
                    p.op("dve", lambda e, h=h: e.tensor_copy(
                        an["states_bf"][:, h * 128:(h + 1) * 128],
                        an["states"][:, h * 128:(h + 1) * 128]),
                        reads=("states",), writes=("states_bf",), tag=f"sb{i}.{c}.{j}")
            for j in range(2):
                h = 2 * i + j
                p.op("sp", lambda e, j=j, h=h: e.dma_start(out=xgd[h], in_=aa["xg_st"][j][:]),
                     reads=(f"xg_st{j}",), writes=(f"xgd:{h}",), is_dma=True,
                     tag=f"xgw{i}.{j}", channel="D:xgd")

        def alloc_outp(stk):
            s2 = lambda n, sh, dt: stk.enter_context(
                nc.sbuf_tensor(n, sh, dt, unsafe_allow_realloc=True))
            o = {}
            o["y_att"] = s2("y_att", [128, DC * 512], F32)
            o["xg_ring"] = s2("xg_ring", [128, NV * 512], BF16)
            o["wo_r"] = [s2(f"wo_r{i}", [128, D], BF16) for i in range(10)]
            return o

        def outproj(os_):
            y_att, xg_ring, wo_r = os_["y_att"], os_["xg_ring"], os_["wo_r"]
            for tt in range(4):
                t0 = tt * 512
                p.op("sp", lambda e, tt=tt: e.dma_start(
                    out=xg_ring[:].rearrange("p (v t) -> p v t", v=NV),
                    in_=xgd[:, :, tt * 512:(tt + 1) * 512].rearrange("v p t -> p v t")),
                    reads=tuple(f"xgd:{h}" for h in range(NV)),
                    writes=("xg_ring",), is_dma=True, tag=f"oxg{tt}", channel="xg_ring")
                for vs in range(4):
                    for k in range(8):
                        vc = vs * 8 + k
                        slot = wo_r[vc % 10]
                        SL = f"wo_r{vc % 10}"
                        p.op("pool", lambda e, vc=vc, slot=slot: e.dma_start(
                            out=slot[:], in_=wout[vc * 128:(vc + 1) * 128, :]),
                            writes=(SL,), is_dma=True, tag=f"owl{tt}.{vc}", channel=SL)
                    for dc in range(DC):
                        bank = pb[4 + dc % 2]
                        BK = PB[4 + dc % 2]

                        def mmo(e, vs=vs, dc=dc, bank=bank):
                            for k in range(8):
                                vc = vs * 8 + k
                                r = e.matmul(bank[:, :], wo_r[vc % 10][:, dc * 128:(dc + 1) * 128],
                                             xg_ring[:, vc * 512:(vc + 1) * 512],
                                             start=(k == 0), stop=(k == 7))
                            return r
                        p.op("pe", mmo,
                             reads=tuple(f"wo_r{(vs * 8 + k) % 10}" for k in range(8)) + ("xg_ring",),
                             writes=(BK,), tag=f"om{tt}.{vs}.{dc}")
                        if vs == 0:
                            p.op("dve", lambda e, dc=dc, bank=bank: e.tensor_copy(
                                y_att[:, dc * 512:(dc + 1) * 512], bank[:, :]),
                                reads=(BK,), writes=("y_att",), tag=f"oy0{tt}.{dc}")
                        else:
                            p.op("dve", lambda e, dc=dc, bank=bank: e.tensor_tensor(
                                out=y_att[:, dc * 512:(dc + 1) * 512],
                                in0=y_att[:, dc * 512:(dc + 1) * 512], in1=bank[:, :], op=ALU.add),
                                reads=(BK, "y_att"), writes=("y_att",), tag=f"oy{tt}.{vs}.{dc}")
                for dc in range(DC):
                    slot = dc % 3
                    xa = xsl[slot]
                    Xa = f"xsl{slot}"
                    p.op("sp", lambda e, dc=dc, t0=t0, xa=xa: e.dma_start(
                        out=xa[:], in_=x1d[dc * 128:(dc + 1) * 128, t0:t0 + 512]),
                        reads=(f"x1d:{dc}:{t0}",), writes=(Xa,), is_dma=True,
                        tag=f"orx{tt}.{dc}", channel=Xa)
                    p.op("dve", lambda e, dc=dc, xa=xa: e.tensor_tensor(
                        out=y_att[:, dc * 512:(dc + 1) * 512],
                        in0=y_att[:, dc * 512:(dc + 1) * 512], in1=xa[:], op=ALU.add),
                        reads=(Xa, "y_att"), writes=("y_att",), tag=f"orr{tt}.{dc}")
                    p.op("sp", lambda e, dc=dc, t0=t0: e.dma_start(
                        out=x2d[dc * 128:(dc + 1) * 128, t0:t0 + 512],
                        in_=y_att[:, dc * 512:(dc + 1) * 512]),
                        reads=("y_att",), writes=(f"x2d:{dc}:{t0}",), is_dma=True,
                        tag=f"orw{tt}.{dc}", channel="D:x2d")
                for dc in range(DC):
                    slot = dc % 3
                    xb = bfsl[slot]
                    Xb = f"bfsl{slot}"
                    p.op("dve", lambda e, dc=dc, xb=xb: e.tensor_copy(
                        xb[:], y_att[:, dc * 512:(dc + 1) * 512]),
                        reads=("y_att",), writes=(Xb,), tag=f"oc{tt}.{dc}")
                    stat_mms(Xb, xb, dc, f"o{tt}.", tt)
                stats_rows(f"o{tt}.", tt)
                for dc in range(DC):
                    slot = dc % 3
                    xa, xb = xsl[slot], bfsl[slot]
                    Xa, Xb = f"xsl{slot}", f"bfsl{slot}"
                    p.op("dve", lambda e, xa=xa, dc=dc: e.tensor_tensor(
                        out=xa[:], in0=y_att[:, dc * 512:(dc + 1) * 512], in1=pb[4][:, :],
                        op=ALU.subtract),
                        reads=("y_att", PB[4]), writes=(Xa,), tag=f"oa{tt}.{dc}")
                    p.op("dve", lambda e, xa=xa, xb=xb: e.tensor_tensor(
                        out=xb[:], in0=xa[:], in1=pbs[:, :], op=ALU.mult),
                        reads=(Xa, PBS), writes=(Xb,), tag=f"on{tt}.{dc}")
                    p.op("sp", lambda e, dc=dc, t0=t0, xb=xb: e.dma_start(
                        out=ln3d[dc * 128:(dc + 1) * 128, t0:t0 + 512], in_=xb[:]),
                        reads=(Xb,), writes=(f"ln3d:{dc}:{t0}",), is_dma=True,
                        tag=f"olw{tt}.{dc}", channel="D:ln3d")

        no_tok = lambda dc, t0: ()
        x2_tok = lambda dc, t0: (f"x2d:{dc}:{t0}",)
        stop = cfg.get("stop")

        if stop == "bar":
            fstk = contextlib.ExitStack()
            with fstk:
                fs = alloc_ffn(fstk, "a")
                ffn_phase(fs, x0, no_tok, out, "outd", ln2d, "ln2d",
                          first=True, write_out=False, tagp="f1.")
            p.barrier(FLUSH)
            p.out_tokens += [f"outd:{dc}:{t0}" for dc in range(DC) for t0 in range(0, S, 512)]
        elif stop == "ffn1":
            fstk = contextlib.ExitStack()
            with fstk:
                fs = alloc_ffn(fstk, "a")
                ffn_phase(fs, x0, no_tok, out, "outd", ln2d, "ln2d",
                          first=True, write_out=False, tagp="f1.")
            p.out_tokens += [f"outd:{dc}:{t0}" for dc in range(DC) for t0 in range(0, S, 512)]
        else:
            fstk = contextlib.ExitStack()
            with fstk:
                fs = alloc_ffn(fstk, "a")
                ffn_phase(fs, x0, no_tok, x1d, "x1d", ln2d, "ln2d",
                          first=True, write_out=False, tagp="f1.")
            p.barrier(FLUSH)
            astk = contextlib.ExitStack()
            with astk:
                an = alloc_attn_persist(astk)
                p1stk = contextlib.ExitStack()
                with p1stk:
                    aa = alloc_attn_p1(p1stk)
                    an["lnr"] = aa["lnr"]
                    prestk = contextlib.ExitStack()
                    with prestk:
                        ap = alloc_attn_pre(prestk)
                        attn_prepass(ap, an)
                if stop == "prepass" and cfg.get("pre_n", 99) < 4:
                    p.out_dma("sp", lambda e: e.dma_start(
                        out=out[0:64, 0:2048], in_=ap["ba_all"][:]),
                        reads=("ba_all",), tag="tapba")
                    p.out_dma("sp", lambda e: e.dma_start(
                        out=out[64:192, 0:512], in_=ap["gof"][:]),
                        reads=("gof",), tag="tapgof")
                    p.out_dma("sp", lambda e: e.dma_start(
                        out=out[192:320, 0:512], in_=an["btm"][:]),
                        reads=("btm",), tag="tapbtm")
                elif stop == "prepass":
                    for (nm, r0) in (("btm", 0), ("ltm", 1), ("gtm", 2), ("grtm", 3)):
                        p.out_dma("sp", lambda e, nm=nm, r0=r0: e.dma_start(
                            out=out[r0 * 128:(r0 + 1) * 128, 0:512], in_=an[nm][:]),
                            reads=(nm,), tag=f"tapp{nm}")
                    p.out_dma("sp", lambda e: e.dma_start(
                        out=out[4 * 128:4 * 128 + 33, 0:2048], in_=an["lhsD"][:]),
                        reads=("lhsD", "lhsDr"), tag="tapl")
                    p.out_dma("sp", lambda e: e.dma_start(
                        out=out[6 * 128:6 * 128 + 32, 0:2048], in_=an["rhsD"][0:32, 0:2048]),
                        reads=("rhsDc",), tag="tapr")
                    p.out_dma("sp", lambda e: e.dma_start(
                        out=out[9 * 128:10 * 128, 0:512], in_=an["gcb"][:]),
                        reads=("gcb",), tag="tapg")
                else:
                    nheads = cfg.get("nheads", NK)
                    for i in range(nheads):
                        attn_head_proj(an, aa, i, taps=None)
                        if stop == "proj":
                            break
                    p1stk.close()
                    if stop != "proj":
                        p.barrier(FLUSH)
                        p2stk = contextlib.ExitStack()
                        with p2stk:
                            aa = alloc_attn_p2(p2stk)
                            for i in range(nheads):
                                attn_head_rec(an, aa, i, taps=None)
                    if stop == "proj":
                        for (nm, r0) in (("qhat", 0), ("khat", 1), ("khat_tm", 2)):
                            p.out_dma("pool", lambda e, nm=nm, r0=r0: e.dma_start(
                                out=out[r0 * 128:(r0 + 1) * 128, :], in_=aa[nm][:]),
                                reads=(nm,), tag=f"tapq{nm}")
                        for j in range(2):
                            p.out_dma("pool", lambda e, j=j: e.dma_start(
                                out=out[(3 + j) * 128:(4 + j) * 128, :], in_=aa["vtm"][j][:]),
                                reads=(f"vtm{j}",), tag=f"tapv{j}")
                            p.out_dma("pool", lambda e, j=j: e.dma_start(
                                out=out[(5 + j) * 128:(6 + j) * 128, :], in_=aa["zstm"][j][:]),
                                reads=(f"zstm{j}",), tag=f"tapz{j}")
                    elif stop == "xg":
                        for h in range(min(2 * nheads, 16)):
                            p.out_dma("pool", lambda e, h=h: e.dma_start(
                                out=out[h * 128:(h + 1) * 128, :], in_=xgd[h]),
                                reads=(f"xgd:{h}",), tag=f"tapxg{h}")
            if stop in (None, "", "attn", "full"):
                p.barrier(FLUSH)
                ostk = contextlib.ExitStack()
                with ostk:
                    os_ = alloc_outp(ostk)
                    outproj(os_)
                if stop == "attn":
                    x2_toks = tuple(f"x2d:{dc}:{t0}" for dc in range(DC)
                                    for t0 in range(0, S, 512))
                    p.out_dma("sp", lambda e: e.dma_start(out=out, in_=x2d),
                              reads=x2_toks, tag="tapx2")
                else:
                    p.barrier(FLUSH)
                    fstk2 = contextlib.ExitStack()
                    with fstk2:
                        fs2 = alloc_ffn(fstk2, "b")
                        ffn_phase(fs2, x2d, x2_tok, None, "xf", None, None,
                                  first=False, write_out=True, tagp="f3.")

        p.op("sp", lambda e: e.nop(), reads=tuple(p.out_tokens), tag="final_wait")

        sems, chan_sems = {}, {}
        for s in STREAMS:
            sems[s] = st.enter_context(nc.semaphore(f"sem_{s}"))
        for i, ch in enumerate(sorted(p.chan)):
            chan_sems[ch] = st.enter_context(nc.semaphore(f"csem{i}"))
        block = st.enter_context(nc.Block())
        p.finalize(block, sems, chan_sems)
        if cfg.get("verbose"):
            print("prog stats:", p.stats())
    return nc


def make_inputs(inputs):
    x = np.asarray(inputs["L0_attn_hidden_states"])
    base = {
        "wgT": np.ascontiguousarray(np.asarray(inputs["L0_ffn_w_gate"]).T),
        "wuT": np.ascontiguousarray(np.asarray(inputs["L0_ffn_w_up"]).T),
        "wdT": np.ascontiguousarray(np.asarray(inputs["L0_ffn_w_down"]).T),
        "wqkvz": np.ascontiguousarray(np.asarray(inputs["L0_attn_w_qkvz"])),
        "wba": np.ascontiguousarray(np.asarray(inputs["L0_attn_w_ba"])),
        "convw": np.ascontiguousarray(np.asarray(inputs["L0_attn_conv_w"])[:, 0, :]),
        "dtb": np.ascontiguousarray(np.asarray(inputs["L0_attn_dt_bias"])),
        "alog": np.ascontiguousarray(np.asarray(inputs["L0_attn_A_log"])),
        "normw": np.ascontiguousarray(np.asarray(inputs["L0_attn_norm_w"])),
        "wout": np.ascontiguousarray(np.asarray(inputs["L0_attn_w_out"])),
    }
    in_maps = []
    for b in range(8):
        m = dict(base)
        m["x0"] = np.ascontiguousarray(x[b].T)
        in_maps.append(m)
    return in_maps


_CFG = {}


def kernel(**inputs):
    cfg = dict(_CFG)
    nc = build_program(cfg)
    in_maps = make_inputs(inputs)
    res = run_bass_kernel_spmd(nc, in_maps, core_ids=list(range(8)))
    outs = [r["out"].T for r in res.results]
    return np.ascontiguousarray(np.stack(outs, axis=0)).astype(np.float32)


# version 06
# speedup vs baseline: 1.0084x; 1.0084x over previous
import contextlib
import numpy as np
import ml_dtypes
import concourse.bass as bass
import concourse.mybir as mybir
from concourse.bass_utils import run_bass_kernel_spmd

F32 = mybir.dt.float32
BF16 = mybir.dt.bfloat16
AF = mybir.ActivationFunctionType
ALU = mybir.AluOpType
AX = mybir.AxisListType
BFNP = ml_dtypes.bfloat16


STREAMS = ("pe", "act", "dve", "pool", "sp")
DMA_STREAMS = ("sp", "pool")
ORDERED_STREAMS = ("pe", "sp")


class Op:
    __slots__ = ("stream", "fn", "idx", "deps", "marked", "is_dma", "tag",
                 "channel", "chidx", "chan_waits")

    def __init__(self, stream, fn, idx, is_dma, tag, channel=None, chidx=-1):
        self.chan_waits = None
        self.stream = stream
        self.fn = fn
        self.idx = idx
        self.deps = {}
        self.is_dma = is_dma
        self.marked = False
        self.tag = tag
        self.channel = channel
        self.chidx = chidx


class Prog:

    def __init__(self):
        self.ops = {s: [] for s in STREAMS}
        self.chan = {}
        self.last_writer = {}
        self.readers = {}
        self.clock = {s: {} for s in STREAMS}
        self._opclock = {}
        self._nbar = 0
        self.out_tokens = []

    def op(self, stream, fn, reads=(), writes=(), is_dma=False, tag="", channel=None):
        assert stream in STREAMS, stream
        if is_dma:
            assert stream in DMA_STREAMS
            assert channel, f"DMA op needs a channel: {tag}"
        idx = len(self.ops[stream])
        chidx = -1
        if is_dma:
            channel = f"{channel}%{stream}"
            lst = self.chan.setdefault(channel, [])
            chidx = len(lst)
        o = Op(stream, fn, idx, is_dma, tag, channel, chidx)
        deps = {}

        def add_dep(s2, i2):
            o2 = self.ops[s2][i2]
            if o2.is_dma:
                k = ("c", o2.channel)
                v = o2.chidx
            else:
                k = ("s", s2)
                v = i2
            if v > deps.get(k, -1):
                deps[k] = v
        for r in reads:
            lw = self.last_writer.get(r)
            if lw:
                add_dep(*lw)
        for w in writes:
            lw = self.last_writer.get(w)
            if lw:
                add_dep(*lw)
            for (rs, ri) in self.readers.get(w, ()):
                add_dep(rs, ri)
        if is_dma and chidx > 0:
            add_dep(*self.chan[channel][chidx - 1])
        if stream == "pool" and idx > 0 and not is_dma:
            prev = self.ops["pool"][idx - 1]
            if not prev.is_dma:
                deps[("s", "pool")] = max(deps.get(("s", "pool"), -1), idx - 1)
            else:
                add_dep("pool", idx - 1)
        for w in writes:
            self.last_writer[w] = (stream, idx)
            self.readers[w] = []
        for r in reads:
            self.readers.setdefault(r, []).append((stream, idx))
        myclock = self.clock[stream]
        eff = {}
        for k, v in deps.items():
            if k == ("s", stream) and stream in ORDERED_STREAMS:
                continue
            if myclock.get(k, -1) >= v:
                continue
            eff[k] = v
        o.deps = eff
        for k, v in eff.items():
            if v > myclock.get(k, -1):
                myclock[k] = v
            if k[0] == "s":
                pc = self._opclock.get((k[1], v))
            else:
                pc = self._opclock.get(self.chan[k[1]][v])
            if pc:
                for k2, v2 in pc.items():
                    if k2 != ("s", stream) and v2 > myclock.get(k2, -1):
                        myclock[k2] = v2
        if stream in ORDERED_STREAMS:
            myclock[("s", stream)] = idx
        self._opclock[(stream, idx)] = dict(myclock)
        self.ops[stream].append(o)
        if is_dma:
            self.chan[channel].append((stream, idx))
        return o

    def out_dma(self, stream, fn, reads=(), tag=""):
        tok = f"d:out{len(self.out_tokens)}"
        self.out_tokens.append(tok)
        return self.op(stream, fn, reads=reads, writes=(tok,), is_dma=True,
                       tag=tag, channel="D:out")

    def barrier(self, flush):
        self._nbar += 1
        tok = f"__bar{self._nbar}_"
        snapshot = {ch: len(lst) for ch, lst in self.chan.items() if lst}
        prev = None
        for s in STREAMS:
            fn, xr, xw = flush[s]
            rd = ((prev,) if prev else ()) + tuple(xr)
            o = self.op(s, fn, reads=rd, writes=(tok + s,) + tuple(xw), tag="barrier")
            if s == "sp":
                o.chan_waits = snapshot
            prev = tok + s
        last = prev
        for s in STREAMS[:-1]:
            fn, xr, xw = flush[s]
            self.op(s, fn, reads=(last,) + tuple(xr), writes=tuple(xw), tag="barrier2")

    def finalize(self, block, sems, chan_sems):
        for s in STREAMS:
            for o in self.ops[s]:
                for k, v in o.deps.items():
                    if k[0] == "s":
                        self.ops[k[1]][v].marked = True
        compval = {}
        for s in STREAMS:
            acc_c, vc = 0, []
            for o in self.ops[s]:
                if o.marked and not o.is_dma:
                    acc_c += 1
                vc.append(acc_c)
            compval[s] = vc

        chtot = {ch: 16 * len(lst) for ch, lst in self.chan.items()}
        stot = {s: compval[s][-1] if self.ops[s] else 0 for s in STREAMS}
        for s in STREAMS:
            for o in self.ops[s]:
                for k, v in o.deps.items():
                    if k[0] == "c":
                        assert 16 * (v + 1) <= chtot[k[1]], (s, o.tag, k, v, chtot[k[1]])
                    else:
                        assert compval[k[1]][v] <= stot[k[1]], (s, o.tag, k, v)
                if o.chan_waits:
                    for ch, cnt in o.chan_waits.items():
                        assert 16 * cnt <= chtot[ch], (o.tag, ch, cnt)

        def emit_stream(eng, s):
            for o in self.ops[s]:
                if o.chan_waits:
                    for ch, cnt in sorted(o.chan_waits.items()):
                        eng.wait_ge(chan_sems[ch], 16 * cnt)
                for k, v in sorted(o.deps.items()):
                    if k[0] == "c":
                        eng.wait_ge(chan_sems[k[1]], 16 * (v + 1))
                    else:
                        dep_op = self.ops[k[1]][v]
                        assert not dep_op.is_dma
                        eng.wait_ge(sems[k[1]], compval[k[1]][v])
                inst = o.fn(eng)
                if o.is_dma:
                    inst.then_inc(chan_sems[o.channel], 16)
                elif o.marked:
                    if inst is None:
                        raise RuntimeError(f"marked op with no inst: {s} {o.tag}")
                    inst.then_inc(sems[s], 1)

        @block.tensor
        def _(eng):
            emit_stream(eng, "pe")

        @block.scalar
        def _(eng):
            emit_stream(eng, "act")

        @block.vector
        def _(eng):
            emit_stream(eng, "dve")

        @block.gpsimd
        def _(eng):
            emit_stream(eng, "pool")

        @block.sync
        def _(eng):
            emit_stream(eng, "sp")

    def stats(self):
        d = {s: len(self.ops[s]) for s in STREAMS}
        d["marked"] = sum(o.marked for s in STREAMS for o in self.ops[s])
        d["waits"] = sum(len(o.deps) for s in STREAMS for o in self.ops[s])
        d["channels"] = len(self.chan)
        return d



D = 2048
S = 2048
DC = D // 128
H = 8192
HC = H // 128
SUP = 8
NTH = 2
TH = S // NTH
NK, NV, DK, DV = 16, 32, 128, 128
C = 128
NCH = S // C
LN_EPS = 1e-5
L2_EPS = 1e-6
RMS_EPS = 1e-6
QSCALE = DK ** -0.5


def build_program(cfg):
    nc = bass.Bass("TRN2", target_bir_lowering=False)

    x0 = nc.dram_tensor("x0", [D, S], F32, kind="ExternalInput").ap()
    wgT = nc.dram_tensor("wgT", [D, H], F32, kind="ExternalInput").ap()
    wuT = nc.dram_tensor("wuT", [D, H], F32, kind="ExternalInput").ap()
    wdT = nc.dram_tensor("wdT", [H, D], F32, kind="ExternalInput").ap()
    wqkvz = nc.dram_tensor("wqkvz", [D, 12288], F32, kind="ExternalInput").ap()
    wba = nc.dram_tensor("wba", [D, 64], F32, kind="ExternalInput").ap()
    convw = nc.dram_tensor("convw", [8192, 4], F32, kind="ExternalInput").ap()
    dtb = nc.dram_tensor("dtb", [32], F32, kind="ExternalInput").ap()
    alog = nc.dram_tensor("alog", [32], F32, kind="ExternalInput").ap()
    normw = nc.dram_tensor("normw", [128], F32, kind="ExternalInput").ap()
    wout = nc.dram_tensor("wout", [4096, D], F32, kind="ExternalInput").ap()
    out = nc.dram_tensor("out", [D, S], F32, kind="ExternalOutput").ap()

    x1d = nc.dram_tensor("x1d", [D, S], F32, kind="Internal").ap()
    x2d = nc.dram_tensor("x2d", [D, S], F32, kind="Internal").ap()
    ln2d = nc.dram_tensor("ln2d", [D, S], BF16, kind="Internal").ap()
    ln3d = nc.dram_tensor("ln3d", [D, S], BF16, kind="Internal").ap()
    xgd = nc.dram_tensor("xgd", [NV, 128, S], BF16, kind="Internal").ap()
    qkd = nc.dram_tensor("qkd", [NK, 3, 128, S], BF16, kind="Internal").ap()
    vzd = nc.dram_tensor("vzd", [NV, 2, 128, S], BF16, kind="Internal").ap()
    lhd = nc.dram_tensor("lhd", [NCH, 32, C], F32, kind="Internal").ap()
    wgB = nc.dram_tensor("wgB", [HC, 128, D], BF16, kind="Internal").ap()
    wuB = nc.dram_tensor("wuB", [HC, 128, D], BF16, kind="Internal").ap()
    wdB = nc.dram_tensor("wdB", [HC, 128, D], BF16, kind="Internal").ap()

    p = Prog()
    st = contextlib.ExitStack()
    sb = lambda name, shape, dt: st.enter_context(nc.sbuf_tensor(name, shape, dt))
    ps = lambda name, shape, dt: st.enter_context(nc.psum_tensor(name, shape, dt))

    with st:
        pb = [ps(f"pb{i}", [128, 512], F32) for i in range(6)]
        pbt = ps("pbt", [128, 1024], BF16)
        pbs = ps("pbs", [128, 512], F32)
        PB = [f"PB{i}" for i in range(6)]
        PBT, PBS = "PBT", "PBS"

        ones_col_bf = sb("ones_col_bf", [128, 1], BF16)
        ones_row_f = sb("ones_row_f", [1, 128], F32)
        ident_bf = sb("ident_bf", [128, 128], BF16)
        ident_f = sb("ident_f", [128, 128], F32)
        triU_inc = sb("triU_inc", [128, 128], F32)
        triU_str = sb("triU_str", [128, 128], F32)
        mask_low = sb("mask_low", [128, 512], F32)
        ident_t4 = sb("ident_t4", [128, 512], F32)
        nw_bcast = sb("nw_bcast", [128, 512], F32)
        nw_row = sb("nw_row", [1, 128], F32)
        eps_t = sb("eps_t", [128, 6], F32)
        xsl = [sb(f"xsl{i}", [128, 512], F32) for i in range(3)]
        bfsl = [sb(f"bfsl{i}", [128, 512], BF16) for i in range(3)]
        sqsl2 = [sb(f"sqsl{i}", [128, 512], BF16) for i in range(2)]
        sqsl = sqsl2[0]
        srow = [sb(f"srow{i}", [1, 512], F32) for i in range(4)]

        def build_constants():
            CW = ("C:all", "C:nw_pre")

            def pop(fn, tag):
                p.op("pool", fn, reads=CW, writes=CW, tag=tag)
            pop(lambda g: g.memset(ones_col_bf[:], 1.0), "c0")
            pop(lambda g: g.memset(ones_row_f[:], 1.0), "c1")
            for t in (ident_bf, ident_f):
                pop(lambda g, t=t: g.memset(t[:], 1.0), "c2")
                pop(lambda g, t=t: g.affine_select(out=t[:], in_=t[:], compare_op=ALU.is_equal,
                                                   fill=0.0, base=0, pattern=[[-1, 128]],
                                                   channel_multiplier=1), "c3")
            pop(lambda g: g.memset(triU_inc[:], 1.0), "c4")
            pop(lambda g: g.affine_select(out=triU_inc[:], in_=triU_inc[:], compare_op=ALU.is_ge,
                                          fill=0.0, base=0, pattern=[[1, 128]],
                                          channel_multiplier=-1), "c5")
            pop(lambda g: g.memset(triU_str[:], 1.0), "c6")
            pop(lambda g: g.affine_select(out=triU_str[:], in_=triU_str[:], compare_op=ALU.is_gt,
                                          fill=0.0, base=0, pattern=[[-1, 128]],
                                          channel_multiplier=1), "c7")
            pop(lambda g: g.memset(mask_low[:], 0.0), "c8")
            pop(lambda g: g.affine_select(out=mask_low[:], in_=mask_low[:], compare_op=ALU.is_ge,
                                          fill=-10000.0, base=0, pattern=[[0, 4], [-1, 128]],
                                          channel_multiplier=1), "c9")
            pop(lambda g: g.memset(eps_t[:, 0:1], LN_EPS), "c10")
            pop(lambda g: g.memset(eps_t[:, 1:2], L2_EPS), "c11")
            pop(lambda g: g.memset(eps_t[:, 2:3], RMS_EPS), "c12")
            pop(lambda g: g.memset(eps_t[:, 3:4], 1.0), "c13")
            pop(lambda g: g.memset(eps_t[:, 4:5], float(np.log(QSCALE))), "c14")
            pop(lambda g: g.memset(eps_t[:, 5:6], 0.0), "c15")
            pop(lambda g: g.memset(ident_t4[:], 1.0), "c16")
            pop(lambda g: g.affine_select(out=ident_t4[:], in_=ident_t4[:],
                                          compare_op=ALU.is_equal, fill=0.0, base=0,
                                          pattern=[[0, 4], [-1, 128]], channel_multiplier=1), "c17")
            p.op("sp", lambda e: e.dma_start(out=nw_row[:], in_=normw.rearrange("(o f) -> o f", o=1)),
                 writes=("C:nw_row",), is_dma=True, tag="nwrow", channel="nw_row")
            p.op("pe", lambda e: e.matmul(pbs[:, 0:128], ones_row_f[:], nw_row[:], start=True, stop=True),
                 reads=("C:all", "C:nw_row"), writes=(PBS,), tag="nwmm")

            def nw_evac(e):
                for j in range(4):
                    i = e.tensor_copy(nw_bcast[:, j * 128:(j + 1) * 128], pbs[:, 0:128])
                return i
            p.op("dve", nw_evac, reads=(PBS, "C:nw_pre"), writes=("C:nw_bcast",), tag="nwevac")

        build_constants()

        FLUSH = {
            "pe": (lambda e: e.matmul(pbs[0:1, 0:4], ones_row_f[0:1, 0:1],
                                      ones_row_f[0:1, 0:4], start=True, stop=True),
                   ("C:all",), (PBS,)),
            "act": (lambda e: e.activation(out=srow[0][0:1, 0:4], in_=ones_row_f[0:1, 0:4],
                                           func=AF.Copy),
                    ("C:all",), ("srow0",)),
            "dve": (lambda e: e.engine_nop(), (), ()),
            "pool": (lambda e: e.engine_nop(), (), ()),
            "sp": (lambda e: e.nop(), (), ()),
        }

        def alloc_ffn(stk, sfx):
            s2 = lambda n, sh, dt: stk.enter_context(
                nc.sbuf_tensor(n + sfx, sh, dt, unsafe_allow_realloc=True))
            fs = {}
            fs["lnres"] = s2("lnres", [128, DC * TH], BF16)
            fs["y_acc"] = s2("y_acc", [128, DC * TH], F32)
            fs["gated"] = [s2(f"gated{j}", [128, TH], BF16) for j in range(SUP)]
            fs["wg_r"] = [s2(f"wg_r{i}", [128, D], BF16) for i in range(3)]
            fs["wu_r"] = [s2(f"wu_r{i}", [128, D], BF16) for i in range(3)]
            fs["wd_r"] = [s2(f"wd_r{i}", [128, D], BF16) for i in range(SUP + 2)]
            fs["sfx"] = sfx
            return fs

        def stats_rows(tagp, tt):
            p.op("act", lambda e: e.activation(out=srow[0][:], in_=pb[4][:1, :], func=AF.Copy,
                                               scale=1.0 / D),
                 reads=(PB[4],), writes=("srow0",), tag=f"{tagp}m{tt}")
            p.op("act", lambda e: e.activation(out=srow[1][:], in_=srow[0][:], func=AF.Square),
                 reads=("srow0",), writes=("srow1",), tag=f"{tagp}m2{tt}")
            p.op("dve", lambda e: e.tensor_scalar(out=srow[2][:], in0=pb[5][:1, :],
                                                  scalar1=1.0 / D, scalar2=None, op0=ALU.mult),
                 reads=(PB[5],), writes=("srow2",), tag=f"{tagp}ex2{tt}")
            p.op("dve", lambda e: e.tensor_tensor(out=srow[2][:], in0=srow[2][:], in1=srow[1][:],
                                                  op=ALU.subtract),
                 reads=("srow2", "srow1"), writes=("srow2",), tag=f"{tagp}var{tt}")
            p.op("act", lambda e: e.activation(out=srow[3][:], in_=srow[2][:], func=AF.Ln,
                                               bias=eps_t[0:1, 0:1]),
                 reads=("srow2", "C:all"), writes=("srow3",), tag=f"{tagp}lnv{tt}")
            p.op("act", lambda e: e.activation(out=srow[3][:], in_=srow[3][:], func=AF.Exp,
                                               scale=-0.5),
                 reads=("srow3",), writes=("srow3",), tag=f"{tagp}rstd{tt}")
            p.op("pe", lambda e: e.matmul(pb[4][:, :], ones_row_f[:], srow[0][:], start=True, stop=True),
                 reads=("srow0", "C:all"), writes=(PB[4],), tag=f"{tagp}mb{tt}")
            p.op("pe", lambda e: e.matmul(pbs[:, :], ones_row_f[:], srow[3][:], start=True, stop=True),
                 reads=("srow3", "C:all"), writes=(PBS,), tag=f"{tagp}rb{tt}")

        def stat_mms(Xb, xb, dc, tagp, tt):
            p.op("pe", lambda e, xb=xb, dc=dc: e.matmul(
                pb[4][:1, :], ones_col_bf[:], xb[:], start=(dc == 0), stop=(dc == DC - 1)),
                reads=(Xb, "C:all"), writes=(PB[4],), tag=f"{tagp}s1.{tt}.{dc}")
            p.op("act", lambda e, xb=xb: e.activation(out=sqsl[:], in_=xb[:], func=AF.Square),
                 reads=(Xb,), writes=("sqsl0",), tag=f"{tagp}sq{tt}.{dc}")
            p.op("pe", lambda e, dc=dc: e.matmul(
                pb[5][:1, :], ones_col_bf[:], sqsl[:], start=(dc == 0), stop=(dc == DC - 1)),
                reads=("sqsl0", "C:all"), writes=(PB[5],), tag=f"{tagp}s2.{tt}.{dc}")

        def ln_pass(fs, src, src_tok_fn, th, tagp):
            lnres = fs["lnres"]

            def LNR(dc, tt):
                return lnres[:, dc * TH + tt * 512: dc * TH + (tt + 1) * 512]
            for tt in range(2):
                t0 = th * TH + tt * 512
                for dc in range(DC):
                    slot = dc % 3
                    xa, xb = xsl[slot], bfsl[slot]
                    Xa, Xb = f"xsl{slot}", f"bfsl{slot}"
                    p.op("sp", lambda e, dc=dc, t0=t0, xa=xa: e.dma_start(
                        out=xa[:], in_=src[dc * 128:(dc + 1) * 128, t0:t0 + 512]),
                        reads=src_tok_fn(dc, t0), writes=(Xa,), is_dma=True,
                        tag=f"{tagp}lx{tt}.{dc}", channel=Xa)
                    p.op("dve", lambda e, xa=xa, xb=xb: e.tensor_copy(xb[:], xa[:]),
                         reads=(Xa,), writes=(Xb,), tag=f"{tagp}cast{tt}.{dc}")
                    stat_mms(Xb, xb, dc, tagp, tt)
                stats_rows(tagp, tt)
                for dc in range(DC):
                    slot = dc % 3
                    xa = xsl[slot]
                    Xa = f"xsl{slot}"
                    p.op("sp", lambda e, dc=dc, t0=t0, xa=xa: e.dma_start(
                        out=xa[:], in_=src[dc * 128:(dc + 1) * 128, t0:t0 + 512]),
                        reads=src_tok_fn(dc, t0), writes=(Xa,), is_dma=True,
                        tag=f"{tagp}lx2{tt}.{dc}", channel=Xa)
                    p.op("dve", lambda e, xa=xa: e.tensor_tensor(
                        out=xa[:], in0=xa[:], in1=pb[4][:, :], op=ALU.subtract),
                        reads=(Xa, PB[4]), writes=(Xa,), tag=f"{tagp}sub{tt}.{dc}")
                    p.op("dve", lambda e, xa=xa, dc=dc, tt=tt: e.tensor_tensor(
                        out=LNR(dc, tt), in0=xa[:], in1=pbs[:, :], op=ALU.mult),
                        reads=(Xa, PBS), writes=("lnres" + fs["sfx"],), tag=f"{tagp}mul{tt}.{dc}")

        def ffn_phase(fs, src, src_tok_fn, dst, dst_name, ln_dst, ln_dst_name,
                      first, write_out, tagp):
            sfx = fs["sfx"]
            lnres, y_acc, gated = fs["lnres"], fs["y_acc"], fs["gated"]
            wg_r, wu_r, wd_r = fs["wg_r"], fs["wu_r"], fs["wd_r"]
            WRING, WDRING = len(wg_r), len(wd_r)
            LNTOK = "lnres" + sfx
            YTOK = "ya" + sfx

            def LNR(dc, tt):
                return lnres[:, dc * TH + tt * 512: dc * TH + (tt + 1) * 512]

            def YA(dc, tt):
                return y_acc[:, dc * TH + tt * 512: dc * TH + (tt + 1) * 512]

            for th in range(NTH):
                ln_pass(fs, src, src_tok_fn, th, tagp + f"ln{th}.")
                for hc in range(HC):
                    gslot, uslot = wg_r[hc % WRING], wu_r[hc % WRING]
                    Gs, Us = f"wg_r{hc % WRING}{sfx}", f"wu_r{hc % WRING}{sfx}"
                    dslot = wd_r[hc % WDRING]
                    Ds = f"wd_r{hc % WDRING}{sfx}"
                    if first and th == 0:
                        for (wsrc, slot, tok, bdst, bnm) in ((wgT, gslot, Gs, wgB, "wgB"),
                                                             (wuT, uslot, Us, wuB, "wuB")):
                            p.op("pool", lambda e, wsrc=wsrc, slot=slot, hc=hc: e.dma_start(
                                out=slot[:].rearrange("p (dc h) -> p dc h", dc=DC),
                                in_=wsrc[:, hc * 128:(hc + 1) * 128]
                                .rearrange("(dc p) h -> p dc h", p=128)),
                                writes=(tok,), is_dma=True, tag=f"{tagp}wl{hc}",
                                channel=f"wg_r{hc % WRING}" if wsrc is wgT else f"wu_r{hc % WRING}")
                            p.op("pool", lambda e, slot=slot, hc=hc, bdst=bdst: e.dma_start(
                                out=bdst[hc], in_=slot[:]),
                                reads=(tok,), writes=(f"wB:{bnm}:{hc}",),
                                is_dma=True, tag=f"{tagp}wb{hc}", channel=f"D:{bnm}")
                        p.op("pool", lambda e, hc=hc, dslot=dslot: e.dma_start(
                            out=dslot[:], in_=wdT[hc * 128:(hc + 1) * 128, :]),
                            writes=(Ds,), is_dma=True, tag=f"{tagp}wdl{hc}",
                            channel=f"wd_r{hc % WDRING}")
                        p.op("pool", lambda e, hc=hc, dslot=dslot: e.dma_start(
                            out=wdB[hc], in_=dslot[:]),
                            reads=(Ds,), writes=(f"wB:wdB:{hc}",), is_dma=True,
                            tag=f"{tagp}wdb{hc}", channel="D:wdB")
                    else:
                        for (bsrc, slot, tok, nm) in ((wgB, gslot, Gs, "wgB"), (wuB, uslot, Us, "wuB"),
                                                      (wdB, dslot, Ds, "wdB")):
                            p.op("sp", lambda e, bsrc=bsrc, slot=slot, hc=hc: e.dma_start(
                                out=slot[:], in_=bsrc[hc]),
                                reads=(f"wB:{nm}:{hc}",), writes=(tok,), is_dma=True,
                                tag=f"{tagp}wr{hc}",
                                channel=(f"wg_r{hc % WRING}" if nm == "wgB" else
                                         f"wu_r{hc % WRING}" if nm == "wuB" else
                                         f"wd_r{hc % WDRING}"))
                    for tt in range(2):
                        gsel = (hc * 2 + tt) % 3
                        usel = (hc * 2 + tt) % 2
                        bg, bu = pb[gsel], pb[3 + usel]
                        BG, BU = PB[gsel], PB[3 + usel]
                        sq = sqsl2[(hc * 2 + tt) % 2]
                        SQ = f"sqsl{(hc * 2 + tt) % 2}"

                        def mk_mm(slot, tt, bank):
                            def f(e):
                                for dc in range(DC):
                                    i = e.matmul(bank[:, :], slot[:, dc * 128:(dc + 1) * 128],
                                                 LNR(dc, tt), start=(dc == 0), stop=(dc == DC - 1))
                                return i
                            return f
                        p.op("pe", mk_mm(gslot, tt, bg), reads=(Gs, LNTOK), writes=(BG,),
                             tag=f"{tagp}gm{hc}.{tt}")
                        p.op("pe", mk_mm(uslot, tt, bu), reads=(Us, LNTOK), writes=(BU,),
                             tag=f"{tagp}um{hc}.{tt}")
                        gt = gated[hc % SUP]
                        GT = f"gated{hc % SUP}{sfx}"
                        p.op("act", lambda e, bank=bg, sq=sq: e.activation(
                            out=sq[:], in_=bank[:, :], func=AF.Silu),
                            reads=(BG,), writes=(SQ,), tag=f"{tagp}silu{hc}.{tt}")
                        p.op("dve", lambda e, gt=gt, tt=tt, bank=bu, sq=sq: e.tensor_tensor(
                            out=gt[:, tt * 512:(tt + 1) * 512], in0=sq[:], in1=bank[:, :],
                            op=ALU.mult),
                            reads=(SQ, BU), writes=(GT,), tag=f"{tagp}gated{hc}.{tt}")
                    if hc % SUP == SUP - 1:
                        s0 = hc - SUP + 1
                        for dc in range(DC):
                            for tt in range(2):
                                bank = (pb[5], pbs)[(dc * 2 + tt) % 2]
                                BK = (PB[5], PBS)[(dc * 2 + tt) % 2]

                                def mmd(e, s0=s0, dc=dc, tt=tt, bank=bank):
                                    for j in range(SUP):
                                        i = e.matmul(
                                            bank[:, :],
                                            wd_r[(s0 + j) % WDRING][:, dc * 128:(dc + 1) * 128],
                                            gated[j][:, tt * 512:(tt + 1) * 512],
                                            start=(j == 0), stop=(j == SUP - 1))
                                    return i
                                p.op("pe", mmd,
                                     reads=tuple(f"wd_r{(s0 + j) % WDRING}{sfx}" for j in range(SUP))
                                     + tuple(f"gated{j}{sfx}" for j in range(SUP)),
                                     writes=(BK,), tag=f"{tagp}dm{hc}.{dc}.{tt}")
                                if s0 == 0:
                                    p.op("dve", lambda e, dc=dc, tt=tt, bank=bank: e.tensor_copy(
                                        YA(dc, tt), bank[:, :]),
                                        reads=(BK,), writes=(YTOK,), tag=f"{tagp}y0.{dc}.{tt}")
                                else:
                                    p.op("dve", lambda e, dc=dc, tt=tt, bank=bank: e.tensor_tensor(
                                        out=YA(dc, tt), in0=YA(dc, tt), in1=bank[:, :], op=ALU.add),
                                        reads=(BK, YTOK), writes=(YTOK,),
                                        tag=f"{tagp}ya{hc}.{dc}.{tt}")
                for tt in range(2):
                    t0 = th * TH + tt * 512
                    for dc in range(DC):
                        slot = dc % 3
                        xa = xsl[slot]
                        Xa = f"xsl{slot}"
                        p.op("sp", lambda e, dc=dc, t0=t0, xa=xa: e.dma_start(
                            out=xa[:], in_=src[dc * 128:(dc + 1) * 128, t0:t0 + 512]),
                            reads=src_tok_fn(dc, t0), writes=(Xa,), is_dma=True,
                            tag=f"{tagp}rx{tt}.{dc}", channel=Xa)
                        p.op("dve", lambda e, dc=dc, tt=tt, xa=xa: e.scalar_tensor_tensor(
                            out=YA(dc, tt), in0=YA(dc, tt), scalar=0.5, in1=xa[:],
                            op0=ALU.mult, op1=ALU.add),
                            reads=(Xa, YTOK), writes=(YTOK,), tag=f"{tagp}rr{tt}.{dc}")
                        if write_out:
                            p.out_dma("sp", lambda e, dc=dc, t0=t0, tt=tt: e.dma_start(
                                out=out[dc * 128:(dc + 1) * 128, t0:t0 + 512], in_=YA(dc, tt)),
                                reads=(YTOK,), tag=f"{tagp}ow{tt}.{dc}")
                        if dst is not None:
                            p.op("sp", lambda e, dc=dc, t0=t0, tt=tt: e.dma_start(
                                out=dst[dc * 128:(dc + 1) * 128, t0:t0 + 512], in_=YA(dc, tt)),
                                reads=(YTOK,), writes=(f"{dst_name}:{dc}:{t0}",), is_dma=True,
                                tag=f"{tagp}rw{tt}.{dc}", channel=f"D:{dst_name}")
                if ln_dst is not None:
                    for tt in range(2):
                        t0 = th * TH + tt * 512
                        for dc in range(DC):
                            slot = dc % 3
                            xb = bfsl[slot]
                            Xb = f"bfsl{slot}"
                            p.op("dve", lambda e, dc=dc, tt=tt, xb=xb: e.tensor_copy(
                                xb[:], YA(dc, tt)),
                                reads=(YTOK,), writes=(Xb,), tag=f"{tagp}rc{tt}.{dc}")
                            stat_mms(Xb, xb, dc, tagp + "r", tt)
                        stats_rows(tagp + "r", tt)
                        for dc in range(DC):
                            slot = dc % 3
                            xa, xb = xsl[slot], bfsl[slot]
                            Xa, Xb = f"xsl{slot}", f"bfsl{slot}"
                            p.op("dve", lambda e, xa=xa, dc=dc, tt=tt: e.tensor_tensor(
                                out=xa[:], in0=YA(dc, tt), in1=pb[4][:, :], op=ALU.subtract),
                                reads=(YTOK, PB[4]), writes=(Xa,), tag=f"{tagp}ra{tt}.{dc}")
                            p.op("dve", lambda e, xa=xa, xb=xb: e.tensor_tensor(
                                out=xb[:], in0=xa[:], in1=pbs[:, :], op=ALU.mult),
                                reads=(Xa, PBS), writes=(Xb,), tag=f"{tagp}rn{tt}.{dc}")
                            p.op("sp", lambda e, dc=dc, t0=t0, xb=xb: e.dma_start(
                                out=ln_dst[dc * 128:(dc + 1) * 128, t0:t0 + 512], in_=xb[:]),
                                reads=(Xb,), writes=(f"{ln_dst_name}:{dc}:{t0}",), is_dma=True,
                                tag=f"{tagp}lw{tt}.{dc}", channel=f"D:{ln_dst_name}")

        def alloc_attn_pre(stk):
            s2 = lambda n, sh, dt: stk.enter_context(
                nc.sbuf_tensor(n, sh, dt, unsafe_allow_realloc=True))
            a = {}
            a["ba_all"] = s2("ba_all", [64, S], F32)
            a["wba_s"] = s2("wba_s", [128, DC * 64], BF16)
            a["araw"] = s2("araw", [128, 512], F32)
            a["dt_b"] = s2("dt_b", [128, 512], F32)
            a["al_b"] = s2("al_b", [128, 512], F32)
            a["spa"] = s2("spa", [128, 512], F32)
            a["spb"] = s2("spb", [128, 512], F32)
            a["gof"] = s2("gof", [128, 512], F32)
            a["dtrow"] = s2("dtrow", [1, 512], F32)
            a["alrow"] = s2("alrow", [1, 512], F32)
            return a

        def alloc_attn_p1(stk):
            s2 = lambda n, sh, dt: stk.enter_context(
                nc.sbuf_tensor(n, sh, dt, unsafe_allow_realloc=True))
            a = {}
            a["lnr"] = [s2(f"lnrs{i}", [128, DC * 512], BF16) for i in range(2)]
            a["wq_r"] = [s2(f"wq_r{i}", [128, D], BF16) for i in range(6)]
            a["raw"] = s2("raw", [128, 515], F32)
            a["hist"] = s2("hist", [128, 4 * 4], F32)
            a["qhat"] = s2("qhat", [128, S], BF16)
            a["khat"] = s2("khat", [128, S], BF16)
            a["khat_tm"] = s2("khat_tm", [128, S], BF16)
            a["vtm"] = [s2(f"vtm{j}", [128, S], BF16) for j in range(2)]
            a["zstm"] = [s2(f"zstm{j}", [128, S], BF16) for j in range(2)]
            return a

        def alloc_attn_p2(stk):
            s2 = lambda n, sh, dt: stk.enter_context(
                nc.sbuf_tensor(n, sh, dt, unsafe_allow_realloc=True))
            a = {}
            for par in range(2):
                for nm in ("rq", "rk", "rkt", "rv0", "rv1", "rz0", "rz1"):
                    a[f"{nm}{par}"] = s2(f"{nm}{par}", [128, S], BF16)
            a["scr"] = []
            for par in range(2):
                sc = {}
                sc["xg_st"] = [s2(f"xg_st{par}{j}", [128, S], BF16) for j in range(2)]
                for nm, dt in (("elow", F32), ("eup", F32), ("kk", F32), ("qkt", F32),
                               ("msc", F32), ("mcur", BF16), ("at", BF16), ("xcur", BF16),
                               ("ucur", BF16), ("oall", F32), ("zs", F32), ("sqa", F32),
                               ("xgn", BF16), ("kts", BF16), ("tmp0", F32)):
                    sc[nm] = s2(f"{nm}{par}", [128, 256], dt)
                sc["mtall"] = s2(f"mtall{par}", [128, 7 * 256], BF16)
                sc["rms2"] = s2(f"rms2{par}", [128, 2], F32)
                a["scr"].append(sc)
            return a

        def alloc_attn_persist(stk):
            s2 = lambda n, sh, dt: stk.enter_context(
                nc.sbuf_tensor(n, sh, dt, unsafe_allow_realloc=True))
            a = {}
            a["rhsD"] = s2("rhsD", [33, 4096], F32)
            a["lhsD"] = s2("lhsD", [33, NCH * 128], F32)
            a["btm"] = s2("btm", [128, 512], F32)
            a["ltm"] = s2("ltm", [128, 512], F32)
            a["lrtm"] = s2("lrtm", [128, 512], F32)
            a["gtm"] = s2("gtm", [128, 512], F32)
            a["grtm"] = s2("grtm", [128, 512], F32)
            a["gcb"] = s2("gcb", [128, 512], F32)
            a["states"] = s2("states", [128, NV * 128], F32)
            a["states_bf"] = s2("states_bf", [128, NV * 128], BF16)
            a["convs"] = s2("convs", [128, 64 * 4], F32)
            a["lcrow"] = s2("lcrow", [1, 512], F32)
            return a

        def attn_prepass(ap, an):
            pre_n = cfg.get("pre_n", 99)
            p.op("pool", lambda e: e.dma_start(
                out=an["convs"][:].rearrange("p (ci k) -> p ci k", ci=64),
                in_=convw.rearrange("(ci p) k -> p ci k", p=128)),
                writes=("convs",), is_dma=True, tag="convl", channel="convs")
            p.op("pool", lambda g: g.memset(an["rhsD"][0:32, :], 1.0),
                 writes=("rhsDc",), tag="pc0")
            p.op("pool", lambda g: g.affine_select(
                out=an["rhsD"][0:32, :], in_=an["rhsD"][0:32, :],
                compare_op=ALU.is_equal, fill=0.0, base=0,
                pattern=[[-2, 16], [-1, 2], [0, 128]], channel_multiplier=1),
                reads=("rhsDc",), writes=("rhsDc",), tag="pc1")
            p.op("pool", lambda g: g.memset(an["lhsD"][32:33, :], -1.0),
                 writes=("lhsDr",), tag="pc2")
            p.op("pool", lambda g: g.memset(an["states"][:], 0.0),
                 writes=("states",), tag="pc3")
            p.op("pool", lambda g: g.memset(an["states_bf"][:], 0.0),
                 writes=("states_bf",), tag="pc4")
            for (row, srcv, tok) in ((ap["dtrow"], dtb, "dtrow"), (ap["alrow"], alog, "alrow")):
                p.op("sp", lambda e, row=row, srcv=srcv: e.dma_start(
                    out=row[:, 0:32], in_=srcv.rearrange("(o f) -> o f", o=1)),
                    writes=(tok,), is_dma=True, tag=f"ld{tok}", channel=tok)

                def tile16(e, row=row):
                    for c in range(1, NCH):
                        r = e.tensor_copy(row[:, c * 32:(c + 1) * 32], row[:, 0:32])
                    return r
                p.op("dve", tile16, reads=(tok,), writes=(tok,), tag=f"tl{tok}")
            p.op("pe", lambda e: e.matmul(pb[4][:, :], ones_row_f[:], ap["dtrow"][:],
                                          start=True, stop=True),
                 reads=("dtrow", "C:all"), writes=(PB[4],), tag="dtbm")
            p.op("dve", lambda e: e.tensor_copy(ap["dt_b"][:], pb[4][:, :]),
                 reads=(PB[4],), writes=("dt_b",), tag="dtbe")
            p.op("pe", lambda e: e.matmul(pb[4][:, :], ones_row_f[:], ap["alrow"][:],
                                          start=True, stop=True),
                 reads=("alrow", "C:all"), writes=(PB[4],), tag="albm")
            p.op("act", lambda e: e.activation(out=ap["al_b"][:], in_=pb[4][:, :], func=AF.Exp),
                 reads=(PB[4],), writes=("al_b",), tag="albe")
            if pre_n < 2:
                return
            p.op("pool", lambda e: e.dma_start(
                out=ap["wba_s"][:].rearrange("p (dc o) -> p dc o", dc=DC),
                in_=wba.rearrange("(dc p) o -> p dc o", p=128)),
                writes=("wba_s",), is_dma=True, tag="wbal", channel="wba_s")
            for tt in range(4):
                lbuf = an["lnr"][tt % 2]
                LB = f"lnrs{tt % 2}"
                p.op("sp", lambda e, tt=tt, lbuf=lbuf: e.dma_start(
                    out=lbuf[:].rearrange("p (dc t) -> p dc t", dc=DC),
                    in_=ln2d[:, tt * 512:(tt + 1) * 512].rearrange("(dc p) t -> p dc t", p=128)),
                    reads=tuple(f"ln2d:{dc}:{tt * 512}" for dc in range(DC)),
                    writes=(LB,), is_dma=True, tag=f"balnr{tt}", channel=LB)

                def mmba(e, tt=tt, lbuf=lbuf):
                    for dc in range(DC):
                        i = e.matmul(pb[4][0:64, :], ap["wba_s"][:, dc * 64:(dc + 1) * 64],
                                     lbuf[:, dc * 512:(dc + 1) * 512],
                                     start=(dc == 0), stop=(dc == DC - 1))
                    return i
                p.op("pe", mmba, reads=("wba_s", LB), writes=(PB[4],), tag=f"bam{tt}")
                p.op("dve", lambda e, tt=tt: e.tensor_copy(
                    ap["ba_all"][:, tt * 512:(tt + 1) * 512], pb[4][0:64, :]),
                    reads=(PB[4],), writes=("ba_all",), tag=f"bae{tt}")
            if pre_n < 3:
                return
            for c in range(NCH):
                cs = slice(c * 128, (c + 1) * 128)
                p.op("pe", lambda e, cs=cs: e.transpose(pbs[:, 0:64], ap["ba_all"][:, cs],
                                                        ident_f[0:64, 0:64]),
                     reads=("ba_all", "C:all"), writes=(PBS,), tag=f"btr{c}")
                p.op("dve", lambda e, c=c: e.tensor_copy(
                    an["btm"][:, c * 32:(c + 1) * 32].rearrange("p (i j) -> p i j", j=2),
                    pbs[:, 0:64].rearrange("p (i r) -> p i r", r=4)[:, :, 0:2]),
                    reads=(PBS,), writes=("btm",), tag=f"bcp{c}")
                p.op("dve", lambda e, c=c: e.tensor_copy(
                    ap["araw"][:, c * 32:(c + 1) * 32].rearrange("p (i j) -> p i j", j=2),
                    pbs[:, 0:64].rearrange("p (i r) -> p i r", r=4)[:, :, 2:4]),
                    reads=(PBS,), writes=("araw",), tag=f"acp{c}")
            p.op("act", lambda e: e.activation(out=an["btm"][:], in_=an["btm"][:], func=AF.Sigmoid),
                 reads=("btm",), writes=("btm",), tag="bsig")
            p.op("dve", lambda e: e.tensor_tensor(out=ap["araw"][:], in0=ap["araw"][:],
                                                  in1=ap["dt_b"][:], op=ALU.add),
                 reads=("araw", "dt_b"), writes=("araw",), tag="adt")
            p.op("act", lambda e: e.activation(out=ap["spa"][:], in_=ap["araw"][:], func=AF.Abs),
                 reads=("araw",), writes=("spa",), tag="sp_abs")
            p.op("act", lambda e: e.activation(out=ap["spa"][:], in_=ap["spa"][:], func=AF.Exp,
                                               scale=-1.0),
                 reads=("spa",), writes=("spa",), tag="sp_exp")
            p.op("act", lambda e: e.activation(out=ap["spa"][:], in_=ap["spa"][:], func=AF.Ln,
                                               bias=eps_t[:, 3:4]),
                 reads=("spa", "C:all"), writes=("spa",), tag="sp_ln")
            p.op("act", lambda e: e.activation(out=ap["spb"][:], in_=ap["araw"][:], func=AF.Relu),
                 reads=("araw",), writes=("spb",), tag="sp_relu")
            p.op("dve", lambda e: e.tensor_tensor(out=ap["spa"][:], in0=ap["spa"][:],
                                                  in1=ap["spb"][:], op=ALU.add),
                 reads=("spa", "spb"), writes=("spa",), tag="sp_add")
            p.op("dve", lambda e: e.scalar_tensor_tensor(out=ap["gof"][:], in0=ap["spa"][:],
                                                         scalar=-1.0, in1=ap["al_b"][:],
                                                         op0=ALU.mult, op1=ALU.mult),
                 reads=("spa", "al_b"), writes=("gof",), tag="gmul")
            if pre_n < 4:
                return
            for c in range(NCH):
                p.op("pe", lambda e, c=c: e.matmul(
                    pb[4][:, 0:32], triU_inc[:], ap["gof"][:, c * 32:(c + 1) * 32],
                    start=True, stop=True),
                    reads=("gof", "C:all"), writes=(PB[4],), tag=f"lmm{c}")
                p.op("pe", lambda e, c=c: e.matmul(
                    pb[4][:, 32:64], triU_str[:], ap["gof"][:, c * 32:(c + 1) * 32],
                    start=True, stop=True),
                    reads=("gof", "C:all"), writes=(PB[4],), tag=f"lrmm{c}")
                p.op("dve", lambda e, c=c: e.tensor_scalar(
                    out=an["ltm"][:, c * 32:(c + 1) * 32], in0=pb[4][:, 0:32],
                    scalar1=0.0, scalar2=None, op0=ALU.min),
                    reads=(PB[4],), writes=("ltm",), tag=f"lcp{c}")
                p.op("dve", lambda e, c=c: e.tensor_scalar(
                    out=an["lrtm"][:, c * 32:(c + 1) * 32], in0=pb[4][:, 32:64],
                    scalar1=0.0, scalar2=None, op0=ALU.min),
                    reads=(PB[4],), writes=("lrtm",), tag=f"lrcp{c}")
                p.op("pe", lambda e, c=c: e.transpose(
                    pbs[0:32, 128:256], an["ltm"][:, c * 32:(c + 1) * 32], ident_f[:]),
                    reads=("ltm", "C:all"), writes=(PBS,), tag=f"lhm{c}")
                p.op("dve", lambda e, c=c: e.tensor_copy(
                    an["lhsD"][0:32, c * 128:(c + 1) * 128], pbs[0:32, 128:256]),
                    reads=(PBS,), writes=("lhsD",), tag=f"lhme{c}")
                p.op("sp", lambda e, c=c: e.dma_start(
                    out=lhd[c], in_=an["lhsD"][0:32, c * 128:(c + 1) * 128]),
                    reads=("lhsD",), writes=(f"lhd:{c}",), is_dma=True, tag=f"lhw{c}",
                    channel="lhd")
            if pre_n < 5:
                return
            p.op("act", lambda e: e.activation(out=an["gtm"][:], in_=an["ltm"][:], func=AF.Exp),
                 reads=("ltm",), writes=("gtm",), tag="gexp")
            p.op("act", lambda e: e.activation(out=an["grtm"][:], in_=an["lrtm"][:], func=AF.Exp),
                 reads=("lrtm",), writes=("grtm",), tag="grexp")
            p.op("dve", lambda e: e.tensor_tensor(out=an["lcrow"][:], in0=an["ltm"][0:1, :],
                                                  in1=an["lrtm"][0:1, :], op=ALU.add),
                 reads=("ltm", "lrtm"), writes=("lcrow",), tag="lcadd")
            p.op("act", lambda e: e.activation(out=an["lcrow"][:], in_=an["lcrow"][:], func=AF.Exp),
                 reads=("lcrow",), writes=("lcrow",), tag="lcexp")
            p.op("pe", lambda e: e.matmul(pb[4][:, :], ones_row_f[:], an["lcrow"][:],
                                          start=True, stop=True),
                 reads=("lcrow", "C:all"), writes=(PB[4],), tag="gcmm")
            p.op("dve", lambda e: e.tensor_copy(an["gcb"][:], pb[4][:, :]),
                 reads=(PB[4],), writes=("gcb",), tag="gce")

        def attn_head_proj(an, aa, i, taps):
            base = 768 * i
            occols = [base, base + 128, base + 256, base + 384, base + 512, base + 640]
            convci = [i, 16 + i, 32 + 2 * i, 32 + 2 * i + 1]
            p.op("dve", lambda e: e.memset(aa["hist"][:], 0.0), writes=("hist",),
                 tag=f"hz{i}")
            for tt in range(4):
                lbuf = an["lnr"][tt % 2]
                LB = f"lnrs{tt % 2}"
                p.op("sp", lambda e, tt=tt, lbuf=lbuf: e.dma_start(
                    out=lbuf[:].rearrange("p (dc t) -> p dc t", dc=DC),
                    in_=ln2d[:, tt * 512:(tt + 1) * 512].rearrange("(dc p) t -> p dc t", p=128)),
                    reads=tuple(f"ln2d:{dc}:{tt * 512}" for dc in range(DC)),
                    writes=(LB,), is_dma=True, tag=f"plnr{i}.{tt}", channel=LB)
                for oc in range(6):
                    slot = aa["wq_r"][oc]
                    SL = f"wq_r{oc}"
                    if tt == 0:
                        p.op("pool", lambda e, oc=oc, slot=slot: e.dma_start(
                            out=slot[:].rearrange("p (dc o) -> p dc o", dc=DC),
                            in_=wqkvz[:, occols[oc]:occols[oc] + 128]
                            .rearrange("(dc p) o -> p dc o", p=128)),
                            writes=(SL,), is_dma=True, tag=f"pwl{i}.{oc}", channel=SL)
                    bank = pb[(tt * 6 + oc) % 4]
                    BK = PB[(tt * 6 + oc) % 4]

                    def mmq(e, slot=slot, lbuf=lbuf, bank=bank):
                        for dc in range(DC):
                            r = e.matmul(bank[:, :], slot[:, dc * 128:(dc + 1) * 128],
                                         lbuf[:, dc * 512:(dc + 1) * 512],
                                         start=(dc == 0), stop=(dc == DC - 1))
                        return r
                    p.op("pe", mmq, reads=(SL, LB), writes=(BK,), tag=f"pm{i}.{oc}.{tt}")
                    if oc < 4:
                        p.op("dve", lambda e, oc=oc: e.tensor_copy(
                            aa["raw"][:, 0:3], aa["hist"][:, oc * 4:oc * 4 + 3]),
                            reads=("hist",), writes=("raw",), tag=f"ph{i}.{oc}.{tt}")
                        p.op("dve", lambda e, bank=bank: e.tensor_copy(
                            aa["raw"][:, 3:515], bank[:, :]),
                            reads=(BK, "raw"), writes=("raw",), tag=f"pe{i}.{oc}.{tt}")
                        p.op("dve", lambda e, oc=oc: e.tensor_copy(
                            aa["hist"][:, oc * 4:oc * 4 + 3], aa["raw"][:, 512:515]),
                            reads=("raw",), writes=("hist",), tag=f"phs{i}.{oc}.{tt}")
                        ci = convci[oc]
                        p.op("dve", lambda e, ci=ci: e.tensor_scalar(
                            out=xsl[0][:], in0=aa["raw"][:, 0:512],
                            scalar1=an["convs"][:, ci * 4:ci * 4 + 1],
                            scalar2=None, op0=ALU.mult),
                            reads=("raw", "convs"), writes=("xsl0",),
                            tag=f"pc{i}.{oc}.{tt}.0")
                        for j in range(1, 4):
                            p.op("dve", lambda e, ci=ci, j=j: e.scalar_tensor_tensor(
                                out=xsl[0][:], in0=aa["raw"][:, j:j + 512],
                                scalar=an["convs"][:, ci * 4 + j:ci * 4 + j + 1],
                                in1=xsl[0][:], op0=ALU.mult, op1=ALU.add),
                                reads=("raw", "convs", "xsl0"), writes=("xsl0",),
                                tag=f"pc{i}.{oc}.{tt}.{j}")
                        if oc < 2:
                            dstb = aa["qhat"] if oc == 0 else aa["khat"]
                            DT = "qhat" if oc == 0 else "khat"
                            p.op("act", lambda e, dstb=dstb, tt=tt: e.activation(
                                out=dstb[:, tt * 512:(tt + 1) * 512], in_=xsl[0][:], func=AF.Silu),
                                reads=("xsl0",), writes=(DT,), tag=f"ps{i}.{oc}.{tt}")
                        else:
                            j = oc - 2
                            p.op("act", lambda e: e.activation(
                                out=bfsl[0][:], in_=xsl[0][:], func=AF.Silu),
                                reads=("xsl0",), writes=("bfsl0",), tag=f"ps{i}.{oc}.{tt}")

                            def vtr(e):
                                for k in range(4):
                                    r = e.transpose(pbt[:, k * 128:(k + 1) * 128],
                                                    bfsl[0][:, k * 128:(k + 1) * 128], ident_bf[:])
                                return r
                            p.op("pe", vtr, reads=("bfsl0", "C:all"), writes=(PBT,),
                                 tag=f"pvt{i}.{oc}.{tt}")
                            p.op("dve", lambda e, j=j, tt=tt: e.tensor_copy(
                                aa["vtm"][j][:, tt * 512:(tt + 1) * 512], pbt[:, 0:512]),
                                reads=(PBT,), writes=(f"vtm{j}",), tag=f"pve{i}.{oc}.{tt}")
                    else:
                        j = oc - 4
                        p.op("act", lambda e, bank=bank: e.activation(
                            out=bfsl[0][:], in_=bank[:, :], func=AF.Silu),
                            reads=(BK,), writes=("bfsl0",), tag=f"pz{i}.{oc}.{tt}")

                        def ztr(e):
                            for k in range(4):
                                r = e.transpose(pbt[:, k * 128:(k + 1) * 128],
                                                bfsl[0][:, k * 128:(k + 1) * 128], ident_bf[:])
                            return r
                        p.op("pe", ztr, reads=("bfsl0", "C:all"), writes=(PBT,),
                             tag=f"pzt{i}.{oc}.{tt}")
                        p.op("dve", lambda e, j=j, tt=tt: e.tensor_copy(
                            aa["zstm"][j][:, tt * 512:(tt + 1) * 512], pbt[:, 0:512]),
                            reads=(PBT,), writes=(f"zstm{j}",), tag=f"pze{i}.{oc}.{tt}")
            for (nm, scale_bias_col) in (("qhat", 4), ("khat", 5)):
                buf = aa[nm]
                for tt in range(4):
                    sl = slice(tt * 512, (tt + 1) * 512)
                    p.op("act", lambda e, buf=buf, sl=sl: e.activation(
                        out=sqsl[:], in_=buf[:, sl], func=AF.Square),
                        reads=(nm,), writes=("sqsl0",), tag=f"n2s{i}.{nm}.{tt}")
                    p.op("pe", lambda e: e.matmul(pb[4][0:1, :], ones_col_bf[:], sqsl[:],
                                                  start=True, stop=True),
                         reads=("sqsl0", "C:all"), writes=(PB[4],), tag=f"n2m{i}.{nm}.{tt}")
                    p.op("act", lambda e: e.activation(out=srow[0][:], in_=pb[4][0:1, :], func=AF.Ln,
                                                       bias=eps_t[0:1, 1:2]),
                         reads=(PB[4], "C:all"), writes=("srow0",), tag=f"n2l{i}.{nm}.{tt}")
                    p.op("act", lambda e, sc=scale_bias_col: e.activation(
                        out=srow[0][:], in_=srow[0][:], func=AF.Exp, scale=-0.5,
                        bias=eps_t[0:1, sc:sc + 1]),
                        reads=("srow0", "C:all"), writes=("srow0",), tag=f"n2e{i}.{nm}.{tt}")
                    p.op("pe", lambda e: e.matmul(pb[5][:, :], ones_row_f[:], srow[0][:],
                                                  start=True, stop=True),
                         reads=("srow0", "C:all"), writes=(PB[5],), tag=f"n2b{i}.{nm}.{tt}")
                    p.op("dve", lambda e, buf=buf, sl=sl: e.tensor_tensor(
                        out=buf[:, sl], in0=buf[:, sl], in1=pb[5][:, :], op=ALU.mult),
                        reads=(nm, PB[5]), writes=(nm,), tag=f"n2n{i}.{nm}.{tt}")
            for tt in range(4):
                def ktr(e, tt=tt):
                    for k in range(4):
                        r = e.transpose(pbt[:, k * 128:(k + 1) * 128],
                                        aa["khat"][:, tt * 512 + k * 128:tt * 512 + (k + 1) * 128],
                                        ident_bf[:])
                    return r
                p.op("pe", ktr, reads=("khat", "C:all"), writes=(PBT,), tag=f"ktt{i}.{tt}")
                p.op("dve", lambda e, tt=tt: e.tensor_copy(
                    aa["khat_tm"][:, tt * 512:(tt + 1) * 512], pbt[:, 0:512]),
                    reads=(PBT,), writes=("khat_tm",), tag=f"kte{i}.{tt}")
            for (nm, dst) in (("qhat", qkd[i, 0]), ("khat", qkd[i, 1]), ("khat_tm", qkd[i, 2])):
                p.op("sp", lambda e, nm=nm, dst=dst: e.dma_start(out=dst, in_=aa[nm][:]),
                     reads=(nm,), writes=(f"D:{nm}:{i}",), is_dma=True,
                     tag=f"pq{i}.{nm}", channel="D:qkd")
            for j in range(2):
                h = 2 * i + j
                p.op("sp", lambda e, j=j, h=h: e.dma_start(out=vzd[h, 0], in_=aa["vtm"][j][:]),
                     reads=(f"vtm{j}",), writes=(f"D:v:{h}",), is_dma=True,
                     tag=f"pv{i}.{j}", channel="D:vzd")
                p.op("sp", lambda e, j=j, h=h: e.dma_start(out=vzd[h, 1], in_=aa["zstm"][j][:]),
                     reads=(f"zstm{j}",), writes=(f"D:z:{h}",), is_dma=True,
                     tag=f"pz{i}.{j}", channel="D:vzd")

        def attn_rec_loads(an, aa, i):
            par = i % 2
            Rq, Rk, Rkt = f"rq{par}", f"rk{par}", f"rkt{par}"
            qhat, khat, khat_tm = aa[f"rq{par}"], aa[f"rk{par}"], aa[f"rkt{par}"]
            vtm = [aa[f"rv0{par}"], aa[f"rv1{par}"]]
            zstm = [aa[f"rz0{par}"], aa[f"rz1{par}"]]
            VT = [f"rv0{par}", f"rv1{par}"]
            ZT = [f"rz0{par}", f"rz1{par}"]
            for (tok, buf, srcap, rdt) in ((Rq, qhat, qkd[i, 0], f"D:qhat:{i}"),
                                           (Rk, khat, qkd[i, 1], f"D:khat:{i}"),
                                           (Rkt, khat_tm, qkd[i, 2], f"D:khat_tm:{i}")):
                p.op("sp", lambda e, buf=buf, srcap=srcap: e.dma_start(out=buf[:], in_=srcap),
                     reads=(rdt,), writes=(tok,), is_dma=True, tag=f"rl{i}.{tok}", channel=tok)
            for j in range(2):
                h = 2 * i + j
                p.op("sp", lambda e, j=j, h=h: e.dma_start(out=vtm[j][:], in_=vzd[h, 0]),
                     reads=(f"D:v:{h}",), writes=(VT[j],), is_dma=True,
                     tag=f"rlv{i}.{j}", channel=VT[j])
                p.op("sp", lambda e, j=j, h=h: e.dma_start(out=zstm[j][:], in_=vzd[h, 1]),
                     reads=(f"D:z:{h}",), writes=(ZT[j],), is_dma=True,
                     tag=f"rlz{i}.{j}", channel=ZT[j])

        def rec_chunk(an, aa, i, c):
            par = i % 2
            sc = aa["scr"][par]
            sc = {**sc, **{k + str(par): v for k, v in sc.items() if isinstance(k, str)}}
            T_elow = "elow" + str(par)
            T_eup = "eup" + str(par)
            T_kk = "kk" + str(par)
            T_qkt = "qkt" + str(par)
            T_msc = "msc" + str(par)
            T_mcur = "mcur" + str(par)
            T_mtall = "mtall" + str(par)
            T_at = "at" + str(par)
            T_xcur = "xcur" + str(par)
            T_ucur = "ucur" + str(par)
            T_oall = "oall" + str(par)
            T_zs = "zs" + str(par)
            T_sqa = "sqa" + str(par)
            T_rms2 = "rms2" + str(par)
            T_xgn = "xgn" + str(par)
            T_kts = "kts" + str(par)
            T_tmp0 = "tmp0" + str(par)
            Rq, Rk, Rkt = f"rq{par}", f"rk{par}", f"rkt{par}"
            qhat, khat, khat_tm = aa[f"rq{par}"], aa[f"rk{par}"], aa[f"rkt{par}"]
            vtm = [aa[f"rv0{par}"], aa[f"rv1{par}"]]
            zstm = [aa[f"rz0{par}"], aa[f"rz1{par}"]]
            VT = [f"rv0{par}", f"rv1{par}"]
            ZT = [f"rz0{par}", f"rz1{par}"]
            g = i

            def vcol(buf, j, c):
                return buf[:, c * 32 + 2 * i + j: c * 32 + 2 * i + j + 1]
            cs = slice(c * 128, (c + 1) * 128)
            cs = slice(c * 128, (c + 1) * 128)
            p.op("sp", lambda e, c=c: e.dma_start(
                out=an["rhsD"][32:33, g * 256:(g + 1) * 256],
                in_=lhd[c, 2 * i:2 * i + 2].rearrange("(o a) b -> o (a b)", o=1)),
                reads=(f"lhd:{c}",), writes=(f"rhsDr:{i}",), is_dma=True, tag=f"rr{i}.{c}",
                channel=f"rhsrow{par}")
            p.op("pe", lambda e, c=c: e.matmul(
                pb[0][:, 0:256], an["lhsD"][:, c * 128:(c + 1) * 128],
                an["rhsD"][:, g * 256:(g + 1) * 256], start=True, stop=True),
                reads=("lhsD", "lhsDr", "rhsDc", f"rhsDr:{i}"), writes=(PB[0],), tag=f"dm{i}.{c}")
            p.op("dve", lambda e: e.tensor_tensor(out=sc[T_elow][:], in0=pb[0][:, 0:256],
                                                  in1=mask_low[:, 0:256], op=ALU.add),
                 reads=(PB[0], "C:all"), writes=(T_elow,), tag=f"em{i}.{c}")
            p.op("act", lambda e: e.activation(out=sc[T_elow][:], in_=sc[T_elow][:], func=AF.Exp),
                 reads=(T_elow,), writes=(T_elow,), tag=f"ee{i}.{c}")

            def eutr(e):
                for k in range(2):
                    r = e.transpose(pbs[:, k * 128:(k + 1) * 128],
                                    sc[T_elow][:, k * 128:(k + 1) * 128], ident_f[:])
                return r
            p.op("pe", eutr, reads=(T_elow, "C:all"), writes=(PBS,), tag=f"eut{i}.{c}")
            p.op("dve", lambda e: e.tensor_copy(sc[T_eup][:], pbs[:, 0:256]),
                 reads=(PBS,), writes=(T_eup,), tag=f"eue{i}.{c}")
            p.op("pe", lambda e, cs=cs: e.matmul(pb[2][:, 0:128], khat[:, cs],
                                                 khat[:, cs], start=True, stop=True),
                 reads=(Rk,), writes=(PB[2],), tag=f"kk{i}.{c}")
            p.op("pe", lambda e, cs=cs: e.matmul(pb[2][:, 128:256], khat[:, cs],
                                                 qhat[:, cs], start=True, stop=True),
                 reads=(Rk, Rq), writes=(PB[2],), tag=f"qk{i}.{c}")

            def kke(e):
                e.tensor_copy(sc[T_kk][:, 0:128], pb[2][:, 0:128])
                e.tensor_copy(sc[T_kk][:, 128:256], pb[2][:, 0:128])
                e.tensor_copy(sc[T_qkt][:, 0:128], pb[2][:, 128:256])
                return e.tensor_copy(sc[T_qkt][:, 128:256], pb[2][:, 128:256])
            p.op("dve", kke, reads=(PB[2],), writes=(T_kk, T_qkt), tag=f"kke{i}.{c}")
            p.op("dve", lambda e: e.tensor_tensor(out=sc[T_elow][:], in0=sc[T_elow][:],
                                                  in1=ident_t4[:, 0:256], op=ALU.subtract),
                 reads=(T_elow, "C:all"), writes=(T_elow,), tag=f"es{i}.{c}")
            p.op("dve", lambda e: e.tensor_tensor(out=sc[T_msc][:], in0=sc[T_elow][:],
                                                  in1=sc[T_kk][:], op=ALU.mult),
                 reads=(T_elow, T_kk), writes=(T_msc,), tag=f"ms{i}.{c}")
            for j in range(2):
                p.op("dve", lambda e, j=j, c=c: e.tensor_scalar(
                    out=sc[T_mcur][:, j * 128:(j + 1) * 128],
                    in0=sc[T_msc][:, j * 128:(j + 1) * 128],
                    scalar1=vcol(an["btm"], j, c), scalar2=None, op0=ALU.mult),
                    reads=(T_msc, "btm"), writes=(T_mcur,), tag=f"mb{i}.{c}.{j}")
            p.op("dve", lambda e: e.tensor_tensor(out=sc[T_at][:], in0=sc[T_eup][:],
                                                  in1=sc[T_qkt][:], op=ALU.mult),
                 reads=(T_eup, T_qkt), writes=(T_at,), tag=f"at{i}.{c}")

            def mtr(e):
                for j in range(2):
                    r = e.transpose(pbt[:, j * 128:(j + 1) * 128],
                                    sc[T_mcur][:, j * 128:(j + 1) * 128], ident_bf[:])
                return r
            p.op("pe", mtr, reads=(T_mcur, "C:all"), writes=(PBT,), tag=f"mt{i}.{c}")
            p.op("dve", lambda e: e.tensor_copy(sc[T_mtall][:, 0:256], pbt[:, 0:256]),
                 reads=(PBT,), writes=(T_mtall,), tag=f"mte{i}.{c}")
            for L in range(1, 7):
                def sqm(e, L=L):
                    for j in range(2):
                        js = slice(j * 128, (j + 1) * 128)
                        e.matmul(pb[4][:, j * 128:(j + 1) * 128],
                                 sc[T_mtall][:, (L - 1) * 256 + j * 128:(L - 1) * 256 + (j + 1) * 128],
                                 sc[T_mcur][:, js], start=True, stop=True)
                        r = e.matmul(pb[5][:, j * 128:(j + 1) * 128],
                                     sc[T_mcur][:, js],
                                     sc[T_mtall][:, (L - 1) * 256 + j * 128:(L - 1) * 256 + (j + 1) * 128],
                                     start=True, stop=True)
                    return r
                p.op("pe", sqm, reads=(T_mtall, T_mcur), writes=(PB[4], PB[5]),
                     tag=f"sq{i}.{c}.{L}")
                if L < 6:
                    p.op("dve", lambda e: e.tensor_copy(sc[T_mcur][:], pb[4][:, 0:256]),
                         reads=(PB[4],), writes=(T_mcur,), tag=f"sqe{i}.{c}.{L}")
                p.op("dve", lambda e, L=L: e.tensor_copy(
                    sc[T_mtall][:, L * 256:(L + 1) * 256], pb[5][:, 0:256]),
                    reads=(PB[5],), writes=(T_mtall,), tag=f"sqt{i}.{c}.{L}")
            for j in range(2):
                h = 2 * i + j
                p.op("pe", lambda e, cs=cs, j=j, h=h: e.matmul(
                    pb[0][:, j * 128:(j + 1) * 128], khat[:, cs],
                    an["states_bf"][:, h * 128:(h + 1) * 128], start=True, stop=True),
                    reads=(Rk, f"stb{h}"), writes=(PB[0],), tag=f"ks{i}.{c}.{j}")
            for j in range(2):
                p.op("dve", lambda e, j=j, c=c: e.tensor_scalar(
                    out=sc[T_tmp0][:, j * 128:(j + 1) * 128],
                    in0=pb[0][:, j * 128:(j + 1) * 128],
                    scalar1=vcol(an["gtm"], j, c), scalar2=None, op0=ALU.mult),
                    reads=(PB[0], "gtm"), writes=(T_tmp0,), tag=f"bv1{i}.{c}.{j}")
                p.op("dve", lambda e, j=j, cs=cs: e.tensor_tensor(
                    out=sc[T_tmp0][:, j * 128:(j + 1) * 128],
                    in0=vtm[j][:, cs], in1=sc[T_tmp0][:, j * 128:(j + 1) * 128],
                    op=ALU.subtract),
                    reads=(VT[j], T_tmp0), writes=(T_tmp0,), tag=f"bv2{i}.{c}.{j}")
                p.op("dve", lambda e, j=j, c=c: e.tensor_scalar(
                    out=sc[T_xcur][:, j * 128:(j + 1) * 128],
                    in0=sc[T_tmp0][:, j * 128:(j + 1) * 128],
                    scalar1=vcol(an["btm"], j, c), scalar2=None, op0=ALU.mult),
                    reads=(T_tmp0, "btm"), writes=(T_xcur,), tag=f"bv3{i}.{c}.{j}")
            for L in range(6, -1, -1):
                def apm(e, L=L):
                    for j in range(2):
                        js = slice(j * 128, (j + 1) * 128)
                        r = e.matmul(pb[1][:, j * 128:(j + 1) * 128],
                                     sc[T_mtall][:, L * 256 + j * 128:L * 256 + (j + 1) * 128],
                                     sc[T_xcur][:, js], start=True, stop=True)
                    return r
                p.op("pe", apm, reads=(T_mtall, T_xcur), writes=(PB[1],), tag=f"ap{i}.{c}.{L}")
                if L > 0:
                    p.op("dve", lambda e: e.tensor_tensor(
                        out=sc[T_xcur][:], in0=sc[T_xcur][:], in1=pb[1][:, 0:256], op=ALU.add),
                        reads=(PB[1], T_xcur), writes=(T_xcur,), tag=f"apa{i}.{c}.{L}")
                else:
                    p.op("dve", lambda e: e.tensor_tensor(
                        out=sc[T_ucur][:], in0=sc[T_xcur][:], in1=pb[1][:, 0:256],
                        op=ALU.subtract),
                        reads=(PB[1], T_xcur), writes=(T_ucur,), tag=f"apu{i}.{c}")
            for j in range(2):
                h = 2 * i + j
                p.op("pe", lambda e, cs=cs, j=j, h=h: e.matmul(
                    pb[2][:, j * 128:(j + 1) * 128], qhat[:, cs],
                    an["states_bf"][:, h * 128:(h + 1) * 128], start=True, stop=True),
                    reads=(Rq, f"stb{h}"), writes=(PB[2],), tag=f"qs{i}.{c}.{j}")
                p.op("pe", lambda e, j=j: e.matmul(
                    pb[3][:, j * 128:(j + 1) * 128],
                    sc[T_at][:, j * 128:(j + 1) * 128],
                    sc[T_ucur][:, j * 128:(j + 1) * 128], start=True, stop=True),
                    reads=(T_at, T_ucur), writes=(PB[3],), tag=f"au{i}.{c}.{j}")
            for j in range(2):
                p.op("dve", lambda e, j=j, c=c: e.tensor_scalar(
                    out=sc[T_oall][:, j * 128:(j + 1) * 128],
                    in0=pb[2][:, j * 128:(j + 1) * 128],
                    scalar1=vcol(an["gtm"], j, c), scalar2=None, op0=ALU.mult),
                    reads=(PB[2], "gtm"), writes=(T_oall,), tag=f"oi{i}.{c}.{j}")
            p.op("dve", lambda e: e.tensor_tensor(out=sc[T_oall][:], in0=sc[T_oall][:],
                                                  in1=pb[3][:, 0:256], op=ALU.add),
                 reads=(PB[3], T_oall), writes=(T_oall,), tag=f"oa{i}.{c}")
            for j in range(2):
                p.op("dve", lambda e, j=j, cs=cs: e.tensor_copy(
                    sc[T_zs][:, j * 128:(j + 1) * 128], zstm[j][:, cs]),
                    reads=(ZT[j],), writes=(T_zs,), tag=f"zg{i}.{c}.{j}")
            p.op("dve", lambda e: e.tensor_tensor(out=sc[T_oall][:], in0=sc[T_oall][:],
                                                  in1=sc[T_zs][:], op=ALU.mult),
                 reads=(T_zs, T_oall), writes=(T_oall,), tag=f"xg{i}.{c}")
            p.op("act", lambda e: e.activation(out=sc[T_sqa][:], in_=sc[T_oall][:],
                                               func=AF.Square),
                 reads=(T_oall,), writes=(T_sqa,), tag=f"xs{i}.{c}")
            p.op("dve", lambda e: e.tensor_reduce(
                out=sc[T_rms2][:], in_=sc[T_sqa][:].rearrange("p (j v) -> p j v", j=2),
                axis=AX.X, op=ALU.add),
                reads=(T_sqa,), writes=(T_rms2,), tag=f"xr{i}.{c}")
            p.op("act", lambda e: e.activation(out=sc[T_rms2][:], in_=sc[T_rms2][:], func=AF.Ln,
                                               scale=1.0 / DV, bias=eps_t[:, 2:3]),
                 reads=(T_rms2, "C:all"), writes=(T_rms2,), tag=f"xl{i}.{c}")
            p.op("act", lambda e: e.activation(out=sc[T_rms2][:], in_=sc[T_rms2][:], func=AF.Exp,
                                               scale=-0.5),
                 reads=(T_rms2,), writes=(T_rms2,), tag=f"xe{i}.{c}")
            for j in range(2):
                p.op("dve", lambda e, j=j: e.tensor_scalar(
                    out=sc[T_oall][:, j * 128:(j + 1) * 128],
                    in0=sc[T_oall][:, j * 128:(j + 1) * 128],
                    scalar1=sc[T_rms2][:, j:j + 1], scalar2=None, op0=ALU.mult),
                    reads=(T_rms2, T_oall), writes=(T_oall,), tag=f"xn{i}.{c}.{j}")
            p.op("dve", lambda e: e.tensor_tensor(out=sc[T_xgn][:], in0=sc[T_oall][:],
                                                  in1=nw_bcast[:, 0:256], op=ALU.mult),
                 reads=(T_oall, "C:nw_bcast"), writes=(T_xgn,), tag=f"xw{i}.{c}")

            def xtr(e):
                for j in range(2):
                    r = e.transpose(pbt[:, j * 128:(j + 1) * 128],
                                    sc[T_xgn][:, j * 128:(j + 1) * 128], ident_bf[:])
                return r
            p.op("pe", xtr, reads=(T_xgn, "C:all"), writes=(PBT,), tag=f"xt{i}.{c}")
            for j in range(2):
                p.op("dve", lambda e, j=j, cs=cs: e.tensor_copy(
                    sc["xg_st"][j][:, cs], pbt[:, j * 128:(j + 1) * 128]),
                    reads=(PBT,), writes=(f"xg_st{par}{j}",), tag=f"xse{i}.{c}.{j}")
            for j in range(2):
                p.op("dve", lambda e, j=j, cs=cs, c=c: e.tensor_scalar(
                    out=sc[T_kts][:, j * 128:(j + 1) * 128],
                    in0=khat_tm[:, cs],
                    scalar1=vcol(an["grtm"], j, c), scalar2=None, op0=ALU.mult),
                    reads=(Rkt, "grtm"), writes=(T_kts,), tag=f"kt{i}.{c}.{j}")
            for j in range(2):
                p.op("pe", lambda e, j=j: e.matmul(
                    pb[4][:, j * 128:(j + 1) * 128],
                    sc[T_kts][:, j * 128:(j + 1) * 128],
                    sc[T_ucur][:, j * 128:(j + 1) * 128], start=True, stop=True),
                    reads=(T_kts, T_ucur), writes=(PB[4],), tag=f"st{i}.{c}.{j}")
            for j in range(2):
                h = 2 * i + j
                p.op("dve", lambda e, j=j, h=h, c=c: e.tensor_scalar(
                    out=an["states"][:, h * 128:(h + 1) * 128],
                    in0=an["states"][:, h * 128:(h + 1) * 128],
                    scalar1=vcol(an["gcb"], j, c), scalar2=None, op0=ALU.mult),
                    reads=(f"st{h}", "gcb"), writes=(f"st{h}",), tag=f"ss{i}.{c}.{j}")
                p.op("dve", lambda e, j=j, h=h: e.tensor_tensor(
                    out=an["states"][:, h * 128:(h + 1) * 128],
                    in0=an["states"][:, h * 128:(h + 1) * 128],
                    in1=pb[4][:, j * 128:(j + 1) * 128], op=ALU.add),
                    reads=(f"st{h}", PB[4]), writes=(f"st{h}",), tag=f"sa{i}.{c}.{j}")
                p.op("dve", lambda e, h=h: e.tensor_copy(
                    an["states_bf"][:, h * 128:(h + 1) * 128],
                    an["states"][:, h * 128:(h + 1) * 128]),
                    reads=(f"st{h}",), writes=(f"stb{h}",), tag=f"sb{i}.{c}.{j}")

        def attn_rec_ship(an, aa, i):
            par = i % 2
            sc = aa["scr"][par]
            for j in range(2):
                h = 2 * i + j
                p.op("sp", lambda e, j=j, h=h, sc=sc: e.dma_start(out=xgd[h], in_=sc["xg_st"][j][:]),
                     reads=(f"xg_st{par}{j}",), writes=(f"xgd:{h}",), is_dma=True,
                     tag=f"xgw{i}.{j}", channel="D:xgd")

        def alloc_outp(stk):
            s2 = lambda n, sh, dt: stk.enter_context(
                nc.sbuf_tensor(n, sh, dt, unsafe_allow_realloc=True))
            o = {}
            o["y_att"] = s2("y_att", [128, DC * 512], F32)
            o["xg_ring"] = s2("xg_ring", [128, NV * 512], BF16)
            o["wo_r"] = [s2(f"wo_r{i}", [128, D], BF16) for i in range(10)]
            return o

        def outproj(os_):
            y_att, xg_ring, wo_r = os_["y_att"], os_["xg_ring"], os_["wo_r"]
            for tt in range(4):
                t0 = tt * 512
                p.op("sp", lambda e, tt=tt: e.dma_start(
                    out=xg_ring[:].rearrange("p (v t) -> p v t", v=NV),
                    in_=xgd[:, :, tt * 512:(tt + 1) * 512].rearrange("v p t -> p v t")),
                    reads=tuple(f"xgd:{h}" for h in range(NV)),
                    writes=("xg_ring",), is_dma=True, tag=f"oxg{tt}", channel="xg_ring")
                for vs in range(4):
                    for k in range(8):
                        vc = vs * 8 + k
                        slot = wo_r[vc % 10]
                        SL = f"wo_r{vc % 10}"
                        p.op("pool", lambda e, vc=vc, slot=slot: e.dma_start(
                            out=slot[:], in_=wout[vc * 128:(vc + 1) * 128, :]),
                            writes=(SL,), is_dma=True, tag=f"owl{tt}.{vc}", channel=SL)
                    for dc in range(DC):
                        bank = pb[4 + dc % 2]
                        BK = PB[4 + dc % 2]

                        def mmo(e, vs=vs, dc=dc, bank=bank):
                            for k in range(8):
                                vc = vs * 8 + k
                                r = e.matmul(bank[:, :], wo_r[vc % 10][:, dc * 128:(dc + 1) * 128],
                                             xg_ring[:, vc * 512:(vc + 1) * 512],
                                             start=(k == 0), stop=(k == 7))
                            return r
                        p.op("pe", mmo,
                             reads=tuple(f"wo_r{(vs * 8 + k) % 10}" for k in range(8)) + ("xg_ring",),
                             writes=(BK,), tag=f"om{tt}.{vs}.{dc}")
                        if vs == 0:
                            p.op("dve", lambda e, dc=dc, bank=bank: e.tensor_copy(
                                y_att[:, dc * 512:(dc + 1) * 512], bank[:, :]),
                                reads=(BK,), writes=("y_att",), tag=f"oy0{tt}.{dc}")
                        else:
                            p.op("dve", lambda e, dc=dc, bank=bank: e.tensor_tensor(
                                out=y_att[:, dc * 512:(dc + 1) * 512],
                                in0=y_att[:, dc * 512:(dc + 1) * 512], in1=bank[:, :], op=ALU.add),
                                reads=(BK, "y_att"), writes=("y_att",), tag=f"oy{tt}.{vs}.{dc}")
                for dc in range(DC):
                    slot = dc % 3
                    xa = xsl[slot]
                    Xa = f"xsl{slot}"
                    p.op("sp", lambda e, dc=dc, t0=t0, xa=xa: e.dma_start(
                        out=xa[:], in_=x1d[dc * 128:(dc + 1) * 128, t0:t0 + 512]),
                        reads=(f"x1d:{dc}:{t0}",), writes=(Xa,), is_dma=True,
                        tag=f"orx{tt}.{dc}", channel=Xa)
                    p.op("dve", lambda e, dc=dc, xa=xa: e.tensor_tensor(
                        out=y_att[:, dc * 512:(dc + 1) * 512],
                        in0=y_att[:, dc * 512:(dc + 1) * 512], in1=xa[:], op=ALU.add),
                        reads=(Xa, "y_att"), writes=("y_att",), tag=f"orr{tt}.{dc}")
                    p.op("sp", lambda e, dc=dc, t0=t0: e.dma_start(
                        out=x2d[dc * 128:(dc + 1) * 128, t0:t0 + 512],
                        in_=y_att[:, dc * 512:(dc + 1) * 512]),
                        reads=("y_att",), writes=(f"x2d:{dc}:{t0}",), is_dma=True,
                        tag=f"orw{tt}.{dc}", channel="D:x2d")
                for dc in range(DC):
                    slot = dc % 3
                    xb = bfsl[slot]
                    Xb = f"bfsl{slot}"
                    p.op("dve", lambda e, dc=dc, xb=xb: e.tensor_copy(
                        xb[:], y_att[:, dc * 512:(dc + 1) * 512]),
                        reads=("y_att",), writes=(Xb,), tag=f"oc{tt}.{dc}")
                    stat_mms(Xb, xb, dc, f"o{tt}.", tt)
                stats_rows(f"o{tt}.", tt)
                for dc in range(DC):
                    slot = dc % 3
                    xa, xb = xsl[slot], bfsl[slot]
                    Xa, Xb = f"xsl{slot}", f"bfsl{slot}"
                    p.op("dve", lambda e, xa=xa, dc=dc: e.tensor_tensor(
                        out=xa[:], in0=y_att[:, dc * 512:(dc + 1) * 512], in1=pb[4][:, :],
                        op=ALU.subtract),
                        reads=("y_att", PB[4]), writes=(Xa,), tag=f"oa{tt}.{dc}")
                    p.op("dve", lambda e, xa=xa, xb=xb: e.tensor_tensor(
                        out=xb[:], in0=xa[:], in1=pbs[:, :], op=ALU.mult),
                        reads=(Xa, PBS), writes=(Xb,), tag=f"on{tt}.{dc}")
                    p.op("sp", lambda e, dc=dc, t0=t0, xb=xb: e.dma_start(
                        out=ln3d[dc * 128:(dc + 1) * 128, t0:t0 + 512], in_=xb[:]),
                        reads=(Xb,), writes=(f"ln3d:{dc}:{t0}",), is_dma=True,
                        tag=f"olw{tt}.{dc}", channel="D:ln3d")

        no_tok = lambda dc, t0: ()
        x2_tok = lambda dc, t0: (f"x2d:{dc}:{t0}",)
        stop = cfg.get("stop")

        if stop == "bar":
            fstk = contextlib.ExitStack()
            with fstk:
                fs = alloc_ffn(fstk, "a")
                ffn_phase(fs, x0, no_tok, out, "outd", ln2d, "ln2d",
                          first=True, write_out=False, tagp="f1.")
            p.barrier(FLUSH)
            p.out_tokens += [f"outd:{dc}:{t0}" for dc in range(DC) for t0 in range(0, S, 512)]
        elif stop == "ffn1":
            fstk = contextlib.ExitStack()
            with fstk:
                fs = alloc_ffn(fstk, "a")
                ffn_phase(fs, x0, no_tok, out, "outd", ln2d, "ln2d",
                          first=True, write_out=False, tagp="f1.")
            p.out_tokens += [f"outd:{dc}:{t0}" for dc in range(DC) for t0 in range(0, S, 512)]
        else:
            fstk = contextlib.ExitStack()
            with fstk:
                fs = alloc_ffn(fstk, "a")
                ffn_phase(fs, x0, no_tok, x1d, "x1d", ln2d, "ln2d",
                          first=True, write_out=False, tagp="f1.")
            p.barrier(FLUSH)
            astk = contextlib.ExitStack()
            with astk:
                an = alloc_attn_persist(astk)
                p1stk = contextlib.ExitStack()
                with p1stk:
                    aa = alloc_attn_p1(p1stk)
                    an["lnr"] = aa["lnr"]
                    prestk = contextlib.ExitStack()
                    with prestk:
                        ap = alloc_attn_pre(prestk)
                        attn_prepass(ap, an)
                if stop == "prepass" and cfg.get("pre_n", 99) < 4:
                    p.out_dma("sp", lambda e: e.dma_start(
                        out=out[0:64, 0:2048], in_=ap["ba_all"][:]),
                        reads=("ba_all",), tag="tapba")
                    p.out_dma("sp", lambda e: e.dma_start(
                        out=out[64:192, 0:512], in_=ap["gof"][:]),
                        reads=("gof",), tag="tapgof")
                    p.out_dma("sp", lambda e: e.dma_start(
                        out=out[192:320, 0:512], in_=an["btm"][:]),
                        reads=("btm",), tag="tapbtm")
                elif stop == "prepass":
                    for (nm, r0) in (("btm", 0), ("ltm", 1), ("gtm", 2), ("grtm", 3)):
                        p.out_dma("sp", lambda e, nm=nm, r0=r0: e.dma_start(
                            out=out[r0 * 128:(r0 + 1) * 128, 0:512], in_=an[nm][:]),
                            reads=(nm,), tag=f"tapp{nm}")
                    p.out_dma("sp", lambda e: e.dma_start(
                        out=out[4 * 128:4 * 128 + 33, 0:2048], in_=an["lhsD"][:]),
                        reads=("lhsD", "lhsDr"), tag="tapl")
                    p.out_dma("sp", lambda e: e.dma_start(
                        out=out[6 * 128:6 * 128 + 32, 0:2048], in_=an["rhsD"][0:32, 0:2048]),
                        reads=("rhsDc",), tag="tapr")
                    p.out_dma("sp", lambda e: e.dma_start(
                        out=out[9 * 128:10 * 128, 0:512], in_=an["gcb"][:]),
                        reads=("gcb",), tag="tapg")
                else:
                    nheads = cfg.get("nheads", NK)
                    for i in range(nheads):
                        attn_head_proj(an, aa, i, taps=None)
                        if stop == "proj":
                            break
                    p1stk.close()
                    if stop != "proj":
                        p.barrier(FLUSH)
                        p2stk = contextlib.ExitStack()
                        with p2stk:
                            aa = alloc_attn_p2(p2stk)
                            for pr in range((nheads + 1) // 2):
                                i0, i1 = 2 * pr, 2 * pr + 1
                                attn_rec_loads(an, aa, i0)
                                if i1 < nheads:
                                    attn_rec_loads(an, aa, i1)
                                for c in range(NCH):
                                    rec_chunk(an, aa, i0, c)
                                    if i1 < nheads:
                                        rec_chunk(an, aa, i1, c)
                                attn_rec_ship(an, aa, i0)
                                if i1 < nheads:
                                    attn_rec_ship(an, aa, i1)
                    if stop == "proj":
                        for (nm, r0) in (("qhat", 0), ("khat", 1), ("khat_tm", 2)):
                            p.out_dma("pool", lambda e, nm=nm, r0=r0: e.dma_start(
                                out=out[r0 * 128:(r0 + 1) * 128, :], in_=aa[nm][:]),
                                reads=(nm,), tag=f"tapq{nm}")
                        for j in range(2):
                            p.out_dma("pool", lambda e, j=j: e.dma_start(
                                out=out[(3 + j) * 128:(4 + j) * 128, :], in_=aa["vtm"][j][:]),
                                reads=(f"vtm{j}",), tag=f"tapv{j}")
                            p.out_dma("pool", lambda e, j=j: e.dma_start(
                                out=out[(5 + j) * 128:(6 + j) * 128, :], in_=aa["zstm"][j][:]),
                                reads=(f"zstm{j}",), tag=f"tapz{j}")
                    elif stop == "xg":
                        for h in range(min(2 * nheads, 16)):
                            p.out_dma("pool", lambda e, h=h: e.dma_start(
                                out=out[h * 128:(h + 1) * 128, :], in_=xgd[h]),
                                reads=(f"xgd:{h}",), tag=f"tapxg{h}")
            if stop in (None, "", "attn", "full"):
                p.barrier(FLUSH)
                ostk = contextlib.ExitStack()
                with ostk:
                    os_ = alloc_outp(ostk)
                    outproj(os_)
                if stop == "attn":
                    x2_toks = tuple(f"x2d:{dc}:{t0}" for dc in range(DC)
                                    for t0 in range(0, S, 512))
                    p.out_dma("sp", lambda e: e.dma_start(out=out, in_=x2d),
                              reads=x2_toks, tag="tapx2")
                else:
                    p.barrier(FLUSH)
                    fstk2 = contextlib.ExitStack()
                    with fstk2:
                        fs2 = alloc_ffn(fstk2, "b")
                        ffn_phase(fs2, x2d, x2_tok, None, "xf", None, None,
                                  first=False, write_out=True, tagp="f3.")

        p.op("sp", lambda e: e.nop(), reads=tuple(p.out_tokens), tag="final_wait")

        sems, chan_sems = {}, {}
        for s in STREAMS:
            sems[s] = st.enter_context(nc.semaphore(f"sem_{s}"))
        for i, ch in enumerate(sorted(p.chan)):
            chan_sems[ch] = st.enter_context(nc.semaphore(f"csem{i}"))
        block = st.enter_context(nc.Block())
        p.finalize(block, sems, chan_sems)
        if cfg.get("verbose"):
            print("prog stats:", p.stats())
    return nc


def make_inputs(inputs):
    x = np.asarray(inputs["L0_attn_hidden_states"])
    base = {
        "wgT": np.ascontiguousarray(np.asarray(inputs["L0_ffn_w_gate"]).T),
        "wuT": np.ascontiguousarray(np.asarray(inputs["L0_ffn_w_up"]).T),
        "wdT": np.ascontiguousarray(np.asarray(inputs["L0_ffn_w_down"]).T),
        "wqkvz": np.ascontiguousarray(np.asarray(inputs["L0_attn_w_qkvz"])),
        "wba": np.ascontiguousarray(np.asarray(inputs["L0_attn_w_ba"])),
        "convw": np.ascontiguousarray(np.asarray(inputs["L0_attn_conv_w"])[:, 0, :]),
        "dtb": np.ascontiguousarray(np.asarray(inputs["L0_attn_dt_bias"])),
        "alog": np.ascontiguousarray(np.asarray(inputs["L0_attn_A_log"])),
        "normw": np.ascontiguousarray(np.asarray(inputs["L0_attn_norm_w"])),
        "wout": np.ascontiguousarray(np.asarray(inputs["L0_attn_w_out"])),
    }
    in_maps = []
    for b in range(8):
        m = dict(base)
        m["x0"] = np.ascontiguousarray(x[b].T)
        in_maps.append(m)
    return in_maps


_CFG = {}


def kernel(**inputs):
    cfg = dict(_CFG)
    nc = build_program(cfg)
    in_maps = make_inputs(inputs)
    res = run_bass_kernel_spmd(nc, in_maps, core_ids=list(range(8)))
    outs = [r["out"].T for r in res.results]
    return np.ascontiguousarray(np.stack(outs, axis=0)).astype(np.float32)


# version 08 (final)
# speedup vs baseline: 1.0272x; 1.0186x over previous
import contextlib
import numpy as np
import ml_dtypes
import concourse.bass as bass
import concourse.mybir as mybir
from concourse.bass_utils import run_bass_kernel_spmd

F32 = mybir.dt.float32
BF16 = mybir.dt.bfloat16
AF = mybir.ActivationFunctionType
ALU = mybir.AluOpType
AX = mybir.AxisListType
BFNP = ml_dtypes.bfloat16


STREAMS = ("pe", "act", "dve", "pool", "sp")
DMA_STREAMS = ("sp", "pool")
ORDERED_STREAMS = ("pe", "sp")


class Op:
    __slots__ = ("stream", "fn", "idx", "deps", "marked", "is_dma", "tag",
                 "channel", "chidx", "chan_waits")

    def __init__(self, stream, fn, idx, is_dma, tag, channel=None, chidx=-1):
        self.chan_waits = None
        self.stream = stream
        self.fn = fn
        self.idx = idx
        self.deps = {}
        self.is_dma = is_dma
        self.marked = False
        self.tag = tag
        self.channel = channel
        self.chidx = chidx


class Prog:

    def __init__(self):
        self.ops = {s: [] for s in STREAMS}
        self.chan = {}
        self.last_writer = {}
        self.readers = {}
        self.clock = {s: {} for s in STREAMS}
        self._opclock = {}
        self._nbar = 0
        self.out_tokens = []

    def op(self, stream, fn, reads=(), writes=(), is_dma=False, tag="", channel=None):
        assert stream in STREAMS, stream
        if is_dma:
            assert stream in DMA_STREAMS
            assert channel, f"DMA op needs a channel: {tag}"
        idx = len(self.ops[stream])
        chidx = -1
        if is_dma:
            channel = f"{channel}%{stream}"
            lst = self.chan.setdefault(channel, [])
            chidx = len(lst)
        o = Op(stream, fn, idx, is_dma, tag, channel, chidx)
        deps = {}

        def add_dep(s2, i2):
            o2 = self.ops[s2][i2]
            if o2.is_dma:
                k = ("c", o2.channel)
                v = o2.chidx
            else:
                k = ("s", s2)
                v = i2
            if v > deps.get(k, -1):
                deps[k] = v
        for r in reads:
            lw = self.last_writer.get(r)
            if lw:
                add_dep(*lw)
        for w in writes:
            lw = self.last_writer.get(w)
            if lw:
                add_dep(*lw)
            for (rs, ri) in self.readers.get(w, ()):
                add_dep(rs, ri)
        if is_dma and chidx > 0:
            add_dep(*self.chan[channel][chidx - 1])
        if stream == "pool" and idx > 0 and not is_dma:
            prev = self.ops["pool"][idx - 1]
            if not prev.is_dma:
                deps[("s", "pool")] = max(deps.get(("s", "pool"), -1), idx - 1)
            else:
                add_dep("pool", idx - 1)
        for w in writes:
            self.last_writer[w] = (stream, idx)
            self.readers[w] = []
        for r in reads:
            self.readers.setdefault(r, []).append((stream, idx))
        myclock = self.clock[stream]
        eff = {}
        for k, v in deps.items():
            if k == ("s", stream) and stream in ORDERED_STREAMS:
                continue
            if myclock.get(k, -1) >= v:
                continue
            eff[k] = v
        o.deps = eff
        for k, v in eff.items():
            if v > myclock.get(k, -1):
                myclock[k] = v
            if k[0] == "s":
                pc = self._opclock.get((k[1], v))
            else:
                pc = self._opclock.get(self.chan[k[1]][v])
            if pc:
                for k2, v2 in pc.items():
                    if k2 != ("s", stream) and v2 > myclock.get(k2, -1):
                        myclock[k2] = v2
        if stream in ORDERED_STREAMS:
            myclock[("s", stream)] = idx
        self._opclock[(stream, idx)] = dict(myclock)
        self.ops[stream].append(o)
        if is_dma:
            self.chan[channel].append((stream, idx))
        return o

    def out_dma(self, stream, fn, reads=(), tag=""):
        tok = f"d:out{len(self.out_tokens)}"
        self.out_tokens.append(tok)
        return self.op(stream, fn, reads=reads, writes=(tok,), is_dma=True,
                       tag=tag, channel="D:out")

    def barrier(self, flush):
        self._nbar += 1
        tok = f"__bar{self._nbar}_"
        snapshot = {ch: len(lst) for ch, lst in self.chan.items() if lst}
        prev = None
        for s in STREAMS:
            fn, xr, xw = flush[s]
            rd = ((prev,) if prev else ()) + tuple(xr)
            o = self.op(s, fn, reads=rd, writes=(tok + s,) + tuple(xw), tag="barrier")
            if s == "sp":
                o.chan_waits = snapshot
            prev = tok + s
        last = prev
        for s in STREAMS[:-1]:
            fn, xr, xw = flush[s]
            self.op(s, fn, reads=(last,) + tuple(xr), writes=tuple(xw), tag="barrier2")

    def finalize(self, block, sems, chan_sems):
        for s in STREAMS:
            for o in self.ops[s]:
                for k, v in o.deps.items():
                    if k[0] == "s":
                        self.ops[k[1]][v].marked = True
        compval = {}
        for s in STREAMS:
            acc_c, vc = 0, []
            for o in self.ops[s]:
                if o.marked and not o.is_dma:
                    acc_c += 1
                vc.append(acc_c)
            compval[s] = vc

        chtot = {ch: 16 * len(lst) for ch, lst in self.chan.items()}
        stot = {s: compval[s][-1] if self.ops[s] else 0 for s in STREAMS}
        for s in STREAMS:
            for o in self.ops[s]:
                for k, v in o.deps.items():
                    if k[0] == "c":
                        assert 16 * (v + 1) <= chtot[k[1]], (s, o.tag, k, v, chtot[k[1]])
                    else:
                        assert compval[k[1]][v] <= stot[k[1]], (s, o.tag, k, v)
                if o.chan_waits:
                    for ch, cnt in o.chan_waits.items():
                        assert 16 * cnt <= chtot[ch], (o.tag, ch, cnt)

        def emit_stream(eng, s):
            for o in self.ops[s]:
                if o.chan_waits:
                    for ch, cnt in sorted(o.chan_waits.items()):
                        eng.wait_ge(chan_sems[ch], 16 * cnt)
                for k, v in sorted(o.deps.items()):
                    if k[0] == "c":
                        eng.wait_ge(chan_sems[k[1]], 16 * (v + 1))
                    else:
                        dep_op = self.ops[k[1]][v]
                        assert not dep_op.is_dma
                        eng.wait_ge(sems[k[1]], compval[k[1]][v])
                inst = o.fn(eng)
                if o.is_dma:
                    inst.then_inc(chan_sems[o.channel], 16)
                elif o.marked:
                    if inst is None:
                        raise RuntimeError(f"marked op with no inst: {s} {o.tag}")
                    inst.then_inc(sems[s], 1)

        @block.tensor
        def _(eng):
            emit_stream(eng, "pe")

        @block.scalar
        def _(eng):
            emit_stream(eng, "act")

        @block.vector
        def _(eng):
            emit_stream(eng, "dve")

        @block.gpsimd
        def _(eng):
            emit_stream(eng, "pool")

        @block.sync
        def _(eng):
            emit_stream(eng, "sp")

    def stats(self):
        d = {s: len(self.ops[s]) for s in STREAMS}
        d["marked"] = sum(o.marked for s in STREAMS for o in self.ops[s])
        d["waits"] = sum(len(o.deps) for s in STREAMS for o in self.ops[s])
        d["channels"] = len(self.chan)
        return d



D = 2048
S = 2048
DC = D // 128
H = 8192
HC = H // 128
SUP = 8
NTH = 2
TH = S // NTH
NK, NV, DK, DV = 16, 32, 128, 128
C = 128
NCH = S // C
LN_EPS = 1e-5
L2_EPS = 1e-6
RMS_EPS = 1e-6
QSCALE = DK ** -0.5


def build_program(cfg):
    nc = bass.Bass("TRN2", target_bir_lowering=False)

    x0 = nc.dram_tensor("x0", [D, S], F32, kind="ExternalInput").ap()
    wgT = nc.dram_tensor("wgT", [D, H], F32, kind="ExternalInput").ap()
    wuT = nc.dram_tensor("wuT", [D, H], F32, kind="ExternalInput").ap()
    wdT = nc.dram_tensor("wdT", [H, D], F32, kind="ExternalInput").ap()
    wqkvz = nc.dram_tensor("wqkvz", [D, 12288], F32, kind="ExternalInput").ap()
    wba = nc.dram_tensor("wba", [D, 64], F32, kind="ExternalInput").ap()
    convw = nc.dram_tensor("convw", [8192, 4], F32, kind="ExternalInput").ap()
    dtb = nc.dram_tensor("dtb", [32], F32, kind="ExternalInput").ap()
    alog = nc.dram_tensor("alog", [32], F32, kind="ExternalInput").ap()
    normw = nc.dram_tensor("normw", [128], F32, kind="ExternalInput").ap()
    wout = nc.dram_tensor("wout", [4096, D], F32, kind="ExternalInput").ap()
    out = nc.dram_tensor("out", [D, S], F32, kind="ExternalOutput").ap()

    x1d = nc.dram_tensor("x1d", [D, S], F32, kind="Internal").ap()
    x2d = nc.dram_tensor("x2d", [D, S], F32, kind="Internal").ap()
    ln2d = nc.dram_tensor("ln2d", [D, S], BF16, kind="Internal").ap()
    ln3d = nc.dram_tensor("ln3d", [D, S], BF16, kind="Internal").ap()
    xgd = nc.dram_tensor("xgd", [NV, 128, S], BF16, kind="Internal").ap()
    qkd = nc.dram_tensor("qkd", [NK, 3, 128, S], BF16, kind="Internal").ap()
    vzd = nc.dram_tensor("vzd", [NV, 2, 128, S], BF16, kind="Internal").ap()
    lhd = nc.dram_tensor("lhd", [NCH, 32, C], F32, kind="Internal").ap()
    wgB = nc.dram_tensor("wgB", [HC, 128, D], BF16, kind="Internal").ap()
    wuB = nc.dram_tensor("wuB", [HC, 128, D], BF16, kind="Internal").ap()
    wdB = nc.dram_tensor("wdB", [HC, 128, D], BF16, kind="Internal").ap()

    p = Prog()
    st = contextlib.ExitStack()
    sb = lambda name, shape, dt: st.enter_context(nc.sbuf_tensor(name, shape, dt))
    ps = lambda name, shape, dt: st.enter_context(nc.psum_tensor(name, shape, dt))

    with st:
        pb = [ps(f"pb{i}", [128, 512], F32) for i in range(6)]
        pbt = ps("pbt", [128, 1024], BF16)
        pbs = ps("pbs", [128, 512], F32)
        PB = [f"PB{i}" for i in range(6)]
        PBT, PBS = "PBT", "PBS"

        ones_col_bf = sb("ones_col_bf", [128, 1], BF16)
        ones_row_f = sb("ones_row_f", [1, 128], F32)
        ident_bf = sb("ident_bf", [128, 128], BF16)
        ident_f = sb("ident_f", [128, 128], F32)
        triU_inc = sb("triU_inc", [128, 128], F32)
        triU_str = sb("triU_str", [128, 128], F32)
        mask_low = sb("mask_low", [128, 512], F32)
        ident_t4 = sb("ident_t4", [128, 512], F32)
        nw_bcast = sb("nw_bcast", [128, 512], F32)
        nw_row = sb("nw_row", [1, 128], F32)
        eps_t = sb("eps_t", [128, 6], F32)
        xsl = [sb(f"xsl{i}", [128, 512], F32) for i in range(3)]
        bfsl = [sb(f"bfsl{i}", [128, 512], BF16) for i in range(3)]
        sqsl2 = [sb(f"sqsl{i}", [128, 512], BF16) for i in range(2)]
        sqsl = sqsl2[0]
        srow = [sb(f"srow{i}", [1, 512], F32) for i in range(4)]

        def build_constants():
            CW = ("C:all", "C:nw_pre")

            def pop(fn, tag):
                p.op("pool", fn, reads=CW, writes=CW, tag=tag)
            pop(lambda g: g.memset(ones_col_bf[:], 1.0), "c0")
            pop(lambda g: g.memset(ones_row_f[:], 1.0), "c1")
            for t in (ident_bf, ident_f):
                pop(lambda g, t=t: g.memset(t[:], 1.0), "c2")
                pop(lambda g, t=t: g.affine_select(out=t[:], in_=t[:], compare_op=ALU.is_equal,
                                                   fill=0.0, base=0, pattern=[[-1, 128]],
                                                   channel_multiplier=1), "c3")
            pop(lambda g: g.memset(triU_inc[:], 1.0), "c4")
            pop(lambda g: g.affine_select(out=triU_inc[:], in_=triU_inc[:], compare_op=ALU.is_ge,
                                          fill=0.0, base=0, pattern=[[1, 128]],
                                          channel_multiplier=-1), "c5")
            pop(lambda g: g.memset(triU_str[:], 1.0), "c6")
            pop(lambda g: g.affine_select(out=triU_str[:], in_=triU_str[:], compare_op=ALU.is_gt,
                                          fill=0.0, base=0, pattern=[[-1, 128]],
                                          channel_multiplier=1), "c7")
            pop(lambda g: g.memset(mask_low[:], 0.0), "c8")
            pop(lambda g: g.affine_select(out=mask_low[:], in_=mask_low[:], compare_op=ALU.is_ge,
                                          fill=-10000.0, base=0, pattern=[[0, 4], [-1, 128]],
                                          channel_multiplier=1), "c9")
            pop(lambda g: g.memset(eps_t[:, 0:1], LN_EPS), "c10")
            pop(lambda g: g.memset(eps_t[:, 1:2], L2_EPS), "c11")
            pop(lambda g: g.memset(eps_t[:, 2:3], RMS_EPS), "c12")
            pop(lambda g: g.memset(eps_t[:, 3:4], 1.0), "c13")
            pop(lambda g: g.memset(eps_t[:, 4:5], float(np.log(QSCALE))), "c14")
            pop(lambda g: g.memset(eps_t[:, 5:6], 0.0), "c15")
            pop(lambda g: g.memset(ident_t4[:], 1.0), "c16")
            pop(lambda g: g.affine_select(out=ident_t4[:], in_=ident_t4[:],
                                          compare_op=ALU.is_equal, fill=0.0, base=0,
                                          pattern=[[0, 4], [-1, 128]], channel_multiplier=1), "c17")
            p.op("sp", lambda e: e.dma_start(out=nw_row[:], in_=normw.rearrange("(o f) -> o f", o=1)),
                 writes=("C:nw_row",), is_dma=True, tag="nwrow", channel="nw_row")
            p.op("pe", lambda e: e.matmul(pbs[:, 0:128], ones_row_f[:], nw_row[:], start=True, stop=True),
                 reads=("C:all", "C:nw_row"), writes=(PBS,), tag="nwmm")

            def nw_evac(e):
                for j in range(4):
                    i = e.tensor_copy(nw_bcast[:, j * 128:(j + 1) * 128], pbs[:, 0:128])
                return i
            p.op("dve", nw_evac, reads=(PBS, "C:nw_pre"), writes=("C:nw_bcast",), tag="nwevac")

        build_constants()

        FLUSH = {
            "pe": (lambda e: e.matmul(pbs[0:1, 0:4], ones_row_f[0:1, 0:1],
                                      ones_row_f[0:1, 0:4], start=True, stop=True),
                   ("C:all",), (PBS,)),
            "act": (lambda e: e.activation(out=srow[0][0:1, 0:4], in_=ones_row_f[0:1, 0:4],
                                           func=AF.Copy),
                    ("C:all",), ("srow0",)),
            "dve": (lambda e: e.engine_nop(), (), ()),
            "pool": (lambda e: e.engine_nop(), (), ()),
            "sp": (lambda e: e.nop(), (), ()),
        }

        def alloc_ffn(stk, sfx):
            s2 = lambda n, sh, dt: stk.enter_context(
                nc.sbuf_tensor(n + sfx, sh, dt, unsafe_allow_realloc=True))
            fs = {}
            fs["lnres"] = s2("lnres", [128, DC * TH], BF16)
            fs["y_acc"] = s2("y_acc", [128, DC * TH], F32)
            fs["gated"] = [s2(f"gated{j}", [128, TH], BF16) for j in range(SUP)]
            fs["wg_r"] = [s2(f"wg_r{i}", [128, D], BF16) for i in range(3)]
            fs["wu_r"] = [s2(f"wu_r{i}", [128, D], BF16) for i in range(3)]
            fs["wd_r"] = [s2(f"wd_r{i}", [128, D], BF16) for i in range(SUP + 2)]
            fs["sfx"] = sfx
            return fs

        def stats_rows(tagp, tt):
            p.op("act", lambda e: e.activation(out=srow[0][:], in_=pb[4][:1, :], func=AF.Copy,
                                               scale=1.0 / D),
                 reads=(PB[4],), writes=("srow0",), tag=f"{tagp}m{tt}")
            p.op("act", lambda e: e.activation(out=srow[1][:], in_=srow[0][:], func=AF.Square),
                 reads=("srow0",), writes=("srow1",), tag=f"{tagp}m2{tt}")
            p.op("dve", lambda e: e.tensor_scalar(out=srow[2][:], in0=pb[5][:1, :],
                                                  scalar1=1.0 / D, scalar2=None, op0=ALU.mult),
                 reads=(PB[5],), writes=("srow2",), tag=f"{tagp}ex2{tt}")
            p.op("dve", lambda e: e.tensor_tensor(out=srow[2][:], in0=srow[2][:], in1=srow[1][:],
                                                  op=ALU.subtract),
                 reads=("srow2", "srow1"), writes=("srow2",), tag=f"{tagp}var{tt}")
            p.op("act", lambda e: e.activation(out=srow[3][:], in_=srow[2][:], func=AF.Ln,
                                               bias=eps_t[0:1, 0:1]),
                 reads=("srow2", "C:all"), writes=("srow3",), tag=f"{tagp}lnv{tt}")
            p.op("act", lambda e: e.activation(out=srow[3][:], in_=srow[3][:], func=AF.Exp,
                                               scale=-0.5),
                 reads=("srow3",), writes=("srow3",), tag=f"{tagp}rstd{tt}")
            p.op("pe", lambda e: e.matmul(pb[4][:, :], ones_row_f[:], srow[0][:], start=True, stop=True),
                 reads=("srow0", "C:all"), writes=(PB[4],), tag=f"{tagp}mb{tt}")
            p.op("pe", lambda e: e.matmul(pbs[:, :], ones_row_f[:], srow[3][:], start=True, stop=True),
                 reads=("srow3", "C:all"), writes=(PBS,), tag=f"{tagp}rb{tt}")

        def stat_mms(Xb, xb, dc, tagp, tt):
            p.op("pe", lambda e, xb=xb, dc=dc: e.matmul(
                pb[4][:1, :], ones_col_bf[:], xb[:], start=(dc == 0), stop=(dc == DC - 1)),
                reads=(Xb, "C:all"), writes=(PB[4],), tag=f"{tagp}s1.{tt}.{dc}")
            p.op("act", lambda e, xb=xb: e.activation(out=sqsl[:], in_=xb[:], func=AF.Square),
                 reads=(Xb,), writes=("sqsl0",), tag=f"{tagp}sq{tt}.{dc}")
            p.op("pe", lambda e, dc=dc: e.matmul(
                pb[5][:1, :], ones_col_bf[:], sqsl[:], start=(dc == 0), stop=(dc == DC - 1)),
                reads=("sqsl0", "C:all"), writes=(PB[5],), tag=f"{tagp}s2.{tt}.{dc}")

        def ln_pass(fs, src, src_tok_fn, th, tagp):
            lnres = fs["lnres"]

            def LNR(dc, tt):
                return lnres[:, dc * TH + tt * 512: dc * TH + (tt + 1) * 512]
            for tt in range(2):
                t0 = th * TH + tt * 512
                for dc in range(DC):
                    slot = dc % 3
                    xa, xb = xsl[slot], bfsl[slot]
                    Xa, Xb = f"xsl{slot}", f"bfsl{slot}"
                    p.op("sp", lambda e, dc=dc, t0=t0, xa=xa: e.dma_start(
                        out=xa[:], in_=src[dc * 128:(dc + 1) * 128, t0:t0 + 512]),
                        reads=src_tok_fn(dc, t0), writes=(Xa,), is_dma=True,
                        tag=f"{tagp}lx{tt}.{dc}", channel=Xa)
                    p.op("dve", lambda e, xa=xa, xb=xb: e.tensor_copy(xb[:], xa[:]),
                         reads=(Xa,), writes=(Xb,), tag=f"{tagp}cast{tt}.{dc}")
                    stat_mms(Xb, xb, dc, tagp, tt)
                stats_rows(tagp, tt)
                for dc in range(DC):
                    slot = dc % 3
                    xa = xsl[slot]
                    Xa = f"xsl{slot}"
                    p.op("sp", lambda e, dc=dc, t0=t0, xa=xa: e.dma_start(
                        out=xa[:], in_=src[dc * 128:(dc + 1) * 128, t0:t0 + 512]),
                        reads=src_tok_fn(dc, t0), writes=(Xa,), is_dma=True,
                        tag=f"{tagp}lx2{tt}.{dc}", channel=Xa)
                    p.op("dve", lambda e, xa=xa: e.tensor_tensor(
                        out=xa[:], in0=xa[:], in1=pb[4][:, :], op=ALU.subtract),
                        reads=(Xa, PB[4]), writes=(Xa,), tag=f"{tagp}sub{tt}.{dc}")
                    p.op("dve", lambda e, xa=xa, dc=dc, tt=tt: e.tensor_tensor(
                        out=LNR(dc, tt), in0=xa[:], in1=pbs[:, :], op=ALU.mult),
                        reads=(Xa, PBS), writes=("lnres" + fs["sfx"],), tag=f"{tagp}mul{tt}.{dc}")

        def ffn_phase(fs, src, src_tok_fn, dst, dst_name, ln_dst, ln_dst_name,
                      first, write_out, tagp):
            sfx = fs["sfx"]
            lnres, y_acc, gated = fs["lnres"], fs["y_acc"], fs["gated"]
            wg_r, wu_r, wd_r = fs["wg_r"], fs["wu_r"], fs["wd_r"]
            WRING, WDRING = len(wg_r), len(wd_r)
            LNTOK = "lnres" + sfx
            YTOK = "ya" + sfx

            def LNR(dc, tt):
                return lnres[:, dc * TH + tt * 512: dc * TH + (tt + 1) * 512]

            def YA(dc, tt):
                return y_acc[:, dc * TH + tt * 512: dc * TH + (tt + 1) * 512]

            for th in range(NTH):
                ln_pass(fs, src, src_tok_fn, th, tagp + f"ln{th}.")
                for hc in range(HC):
                    gslot, uslot = wg_r[hc % WRING], wu_r[hc % WRING]
                    Gs, Us = f"wg_r{hc % WRING}{sfx}", f"wu_r{hc % WRING}{sfx}"
                    dslot = wd_r[hc % WDRING]
                    Ds = f"wd_r{hc % WDRING}{sfx}"
                    if first and th == 0:
                        for (wsrc, slot, tok, bdst, bnm) in ((wgT, gslot, Gs, wgB, "wgB"),
                                                             (wuT, uslot, Us, wuB, "wuB")):
                            p.op("pool", lambda e, wsrc=wsrc, slot=slot, hc=hc: e.dma_start(
                                out=slot[:].rearrange("p (dc h) -> p dc h", dc=DC),
                                in_=wsrc[:, hc * 128:(hc + 1) * 128]
                                .rearrange("(dc p) h -> p dc h", p=128)),
                                writes=(tok,), is_dma=True, tag=f"{tagp}wl{hc}",
                                channel=f"wg_r{hc % WRING}" if wsrc is wgT else f"wu_r{hc % WRING}")
                            p.op("pool", lambda e, slot=slot, hc=hc, bdst=bdst: e.dma_start(
                                out=bdst[hc], in_=slot[:]),
                                reads=(tok,), writes=(f"wB:{bnm}:{hc}",),
                                is_dma=True, tag=f"{tagp}wb{hc}", channel=f"D:{bnm}")
                        p.op("pool", lambda e, hc=hc, dslot=dslot: e.dma_start(
                            out=dslot[:], in_=wdT[hc * 128:(hc + 1) * 128, :]),
                            writes=(Ds,), is_dma=True, tag=f"{tagp}wdl{hc}",
                            channel=f"wd_r{hc % WDRING}")
                        p.op("pool", lambda e, hc=hc, dslot=dslot: e.dma_start(
                            out=wdB[hc], in_=dslot[:]),
                            reads=(Ds,), writes=(f"wB:wdB:{hc}",), is_dma=True,
                            tag=f"{tagp}wdb{hc}", channel="D:wdB")
                    else:
                        for (bsrc, slot, tok, nm) in ((wgB, gslot, Gs, "wgB"), (wuB, uslot, Us, "wuB"),
                                                      (wdB, dslot, Ds, "wdB")):
                            p.op("sp", lambda e, bsrc=bsrc, slot=slot, hc=hc: e.dma_start(
                                out=slot[:], in_=bsrc[hc]),
                                reads=(f"wB:{nm}:{hc}",), writes=(tok,), is_dma=True,
                                tag=f"{tagp}wr{hc}",
                                channel=(f"wg_r{hc % WRING}" if nm == "wgB" else
                                         f"wu_r{hc % WRING}" if nm == "wuB" else
                                         f"wd_r{hc % WDRING}"))
                    for tt in range(2):
                        gsel = (hc * 2 + tt) % 3
                        usel = (hc * 2 + tt) % 2
                        bg, bu = pb[gsel], pb[3 + usel]
                        BG, BU = PB[gsel], PB[3 + usel]
                        sq = sqsl2[(hc * 2 + tt) % 2]
                        SQ = f"sqsl{(hc * 2 + tt) % 2}"

                        def mk_mm(slot, tt, bank):
                            def f(e):
                                for dc in range(DC):
                                    i = e.matmul(bank[:, :], slot[:, dc * 128:(dc + 1) * 128],
                                                 LNR(dc, tt), start=(dc == 0), stop=(dc == DC - 1))
                                return i
                            return f
                        p.op("pe", mk_mm(gslot, tt, bg), reads=(Gs, LNTOK), writes=(BG,),
                             tag=f"{tagp}gm{hc}.{tt}")
                        p.op("pe", mk_mm(uslot, tt, bu), reads=(Us, LNTOK), writes=(BU,),
                             tag=f"{tagp}um{hc}.{tt}")
                        gt = gated[hc % SUP]
                        GT = f"gated{hc % SUP}{sfx}"
                        p.op("act", lambda e, bank=bg, sq=sq: e.activation(
                            out=sq[:], in_=bank[:, :], func=AF.Silu),
                            reads=(BG,), writes=(SQ,), tag=f"{tagp}silu{hc}.{tt}")
                        p.op("dve", lambda e, gt=gt, tt=tt, bank=bu, sq=sq: e.tensor_tensor(
                            out=gt[:, tt * 512:(tt + 1) * 512], in0=sq[:], in1=bank[:, :],
                            op=ALU.mult),
                            reads=(SQ, BU), writes=(GT,), tag=f"{tagp}gated{hc}.{tt}")
                    if hc % SUP == SUP - 1:
                        s0 = hc - SUP + 1
                        for dc in range(DC):
                            for tt in range(2):
                                bank = (pb[5], pbs)[(dc * 2 + tt) % 2]
                                BK = (PB[5], PBS)[(dc * 2 + tt) % 2]

                                def mmd(e, s0=s0, dc=dc, tt=tt, bank=bank):
                                    for j in range(SUP):
                                        i = e.matmul(
                                            bank[:, :],
                                            wd_r[(s0 + j) % WDRING][:, dc * 128:(dc + 1) * 128],
                                            gated[j][:, tt * 512:(tt + 1) * 512],
                                            start=(j == 0), stop=(j == SUP - 1))
                                    return i
                                p.op("pe", mmd,
                                     reads=tuple(f"wd_r{(s0 + j) % WDRING}{sfx}" for j in range(SUP))
                                     + tuple(f"gated{j}{sfx}" for j in range(SUP)),
                                     writes=(BK,), tag=f"{tagp}dm{hc}.{dc}.{tt}")
                                if s0 == 0:
                                    p.op("dve", lambda e, dc=dc, tt=tt, bank=bank: e.tensor_copy(
                                        YA(dc, tt), bank[:, :]),
                                        reads=(BK,), writes=(YTOK,), tag=f"{tagp}y0.{dc}.{tt}")
                                else:
                                    p.op("dve", lambda e, dc=dc, tt=tt, bank=bank: e.tensor_tensor(
                                        out=YA(dc, tt), in0=YA(dc, tt), in1=bank[:, :], op=ALU.add),
                                        reads=(BK, YTOK), writes=(YTOK,),
                                        tag=f"{tagp}ya{hc}.{dc}.{tt}")
                for tt in range(2):
                    t0 = th * TH + tt * 512
                    for dc in range(DC):
                        slot = dc % 3
                        xa = xsl[slot]
                        Xa = f"xsl{slot}"
                        p.op("sp", lambda e, dc=dc, t0=t0, xa=xa: e.dma_start(
                            out=xa[:], in_=src[dc * 128:(dc + 1) * 128, t0:t0 + 512]),
                            reads=src_tok_fn(dc, t0), writes=(Xa,), is_dma=True,
                            tag=f"{tagp}rx{tt}.{dc}", channel=Xa)
                        p.op("dve", lambda e, dc=dc, tt=tt, xa=xa: e.scalar_tensor_tensor(
                            out=YA(dc, tt), in0=YA(dc, tt), scalar=0.5, in1=xa[:],
                            op0=ALU.mult, op1=ALU.add),
                            reads=(Xa, YTOK), writes=(YTOK,), tag=f"{tagp}rr{tt}.{dc}")
                        if write_out:
                            p.out_dma("sp", lambda e, dc=dc, t0=t0, tt=tt: e.dma_start(
                                out=out[dc * 128:(dc + 1) * 128, t0:t0 + 512], in_=YA(dc, tt)),
                                reads=(YTOK,), tag=f"{tagp}ow{tt}.{dc}")
                        if dst is not None:
                            p.op("sp", lambda e, dc=dc, t0=t0, tt=tt: e.dma_start(
                                out=dst[dc * 128:(dc + 1) * 128, t0:t0 + 512], in_=YA(dc, tt)),
                                reads=(YTOK,), writes=(f"{dst_name}:{dc}:{t0}",), is_dma=True,
                                tag=f"{tagp}rw{tt}.{dc}", channel=f"D:{dst_name}")
                if ln_dst is not None:
                    for tt in range(2):
                        t0 = th * TH + tt * 512
                        for dc in range(DC):
                            slot = dc % 3
                            xb = bfsl[slot]
                            Xb = f"bfsl{slot}"
                            p.op("dve", lambda e, dc=dc, tt=tt, xb=xb: e.tensor_copy(
                                xb[:], YA(dc, tt)),
                                reads=(YTOK,), writes=(Xb,), tag=f"{tagp}rc{tt}.{dc}")
                            stat_mms(Xb, xb, dc, tagp + "r", tt)
                        stats_rows(tagp + "r", tt)
                        for dc in range(DC):
                            slot = dc % 3
                            xa, xb = xsl[slot], bfsl[slot]
                            Xa, Xb = f"xsl{slot}", f"bfsl{slot}"
                            p.op("dve", lambda e, xa=xa, dc=dc, tt=tt: e.tensor_tensor(
                                out=xa[:], in0=YA(dc, tt), in1=pb[4][:, :], op=ALU.subtract),
                                reads=(YTOK, PB[4]), writes=(Xa,), tag=f"{tagp}ra{tt}.{dc}")
                            p.op("dve", lambda e, xa=xa, xb=xb: e.tensor_tensor(
                                out=xb[:], in0=xa[:], in1=pbs[:, :], op=ALU.mult),
                                reads=(Xa, PBS), writes=(Xb,), tag=f"{tagp}rn{tt}.{dc}")
                            p.op("sp", lambda e, dc=dc, t0=t0, xb=xb: e.dma_start(
                                out=ln_dst[dc * 128:(dc + 1) * 128, t0:t0 + 512], in_=xb[:]),
                                reads=(Xb,), writes=(f"{ln_dst_name}:{dc}:{t0}",), is_dma=True,
                                tag=f"{tagp}lw{tt}.{dc}", channel=f"D:{ln_dst_name}")

        def alloc_attn_pre(stk):
            s2 = lambda n, sh, dt: stk.enter_context(
                nc.sbuf_tensor(n, sh, dt, unsafe_allow_realloc=True))
            a = {}
            a["ba_all"] = s2("ba_all", [64, S], F32)
            a["wba_s"] = s2("wba_s", [128, DC * 64], BF16)
            a["araw"] = s2("araw", [128, 512], F32)
            a["dt_b"] = s2("dt_b", [128, 512], F32)
            a["al_b"] = s2("al_b", [128, 512], F32)
            a["spa"] = s2("spa", [128, 512], F32)
            a["spb"] = s2("spb", [128, 512], F32)
            a["gof"] = s2("gof", [128, 512], F32)
            a["dtrow"] = s2("dtrow", [1, 512], F32)
            a["alrow"] = s2("alrow", [1, 512], F32)
            return a

        def alloc_attn_p1(stk):
            s2 = lambda n, sh, dt: stk.enter_context(
                nc.sbuf_tensor(n, sh, dt, unsafe_allow_realloc=True))
            a = {}
            a["lnr"] = [s2(f"lnrs{i}", [128, DC * 512], BF16) for i in range(2)]
            a["wq_r"] = [s2(f"wq_r{i}", [128, D], BF16) for i in range(6)]
            a["raw"] = s2("raw", [128, 515], F32)
            a["hist"] = s2("hist", [128, 4 * 4], F32)
            a["qhat"] = s2("qhat", [128, S], BF16)
            a["khat"] = s2("khat", [128, S], BF16)
            a["khat_tm"] = s2("khat_tm", [128, S], BF16)
            a["vtm"] = [s2(f"vtm{j}", [128, S], BF16) for j in range(2)]
            a["zstm"] = [s2(f"zstm{j}", [128, S], BF16) for j in range(2)]
            return a

        def alloc_attn_p2(stk):
            s2 = lambda n, sh, dt: stk.enter_context(
                nc.sbuf_tensor(n, sh, dt, unsafe_allow_realloc=True))
            a = {}
            for par in range(2):
                for nm in ("rq", "rk", "rkt", "rv0", "rv1", "rz0", "rz1"):
                    a[f"{nm}{par}"] = s2(f"{nm}{par}", [128, S], BF16)
            a["scr"] = []
            for par in range(2):
                sc = {}
                sc["xg_st"] = [s2(f"xg_st{par}{j}", [128, S], BF16) for j in range(2)]
                for nm, dt in (("elow", F32), ("eup", F32), ("kk", F32), ("qkt", F32),
                               ("msc", F32), ("mcur", BF16), ("at", BF16), ("xcur", BF16),
                               ("ucur", BF16), ("oall", F32), ("zs", F32), ("sqa", F32),
                               ("xgn", BF16), ("kts", BF16), ("tmp0", F32)):
                    sc[nm] = s2(f"{nm}{par}", [128, 256], dt)
                sc["mtall"] = s2(f"mtall{par}", [128, 7 * 256], BF16)
                sc["rms2"] = s2(f"rms2{par}", [128, 2], F32)
                a["scr"].append(sc)
            return a

        def alloc_attn_persist(stk):
            s2 = lambda n, sh, dt: stk.enter_context(
                nc.sbuf_tensor(n, sh, dt, unsafe_allow_realloc=True))
            a = {}
            a["rhsD"] = s2("rhsD", [33, 4096], F32)
            a["lhsD"] = s2("lhsD", [33, NCH * 128], F32)
            a["btm"] = s2("btm", [128, 512], F32)
            a["ltm"] = s2("ltm", [128, 512], F32)
            a["lrtm"] = s2("lrtm", [128, 512], F32)
            a["gtm"] = s2("gtm", [128, 512], F32)
            a["grtm"] = s2("grtm", [128, 512], F32)
            a["gcb"] = s2("gcb", [128, 512], F32)
            a["states"] = s2("states", [128, NV * 128], F32)
            a["states_bf"] = s2("states_bf", [128, NV * 128], BF16)
            a["convs"] = s2("convs", [128, 64 * 4], F32)
            a["lcrow"] = s2("lcrow", [1, 512], F32)
            return a

        def attn_prepass(ap, an):
            pre_n = cfg.get("pre_n", 99)
            p.op("pool", lambda e: e.dma_start(
                out=an["convs"][:].rearrange("p (ci k) -> p ci k", ci=64),
                in_=convw.rearrange("(ci p) k -> p ci k", p=128)),
                writes=("convs",), is_dma=True, tag="convl", channel="convs")
            p.op("pool", lambda g: g.memset(an["rhsD"][0:32, :], 1.0),
                 writes=("rhsDc",), tag="pc0")
            p.op("pool", lambda g: g.affine_select(
                out=an["rhsD"][0:32, :], in_=an["rhsD"][0:32, :],
                compare_op=ALU.is_equal, fill=0.0, base=0,
                pattern=[[-2, 16], [-1, 2], [0, 128]], channel_multiplier=1),
                reads=("rhsDc",), writes=("rhsDc",), tag="pc1")
            p.op("pool", lambda g: g.memset(an["lhsD"][32:33, :], -1.0),
                 writes=("lhsDr",), tag="pc2")
            p.op("pool", lambda g: g.memset(an["states"][:], 0.0),
                 writes=("states",), tag="pc3")
            p.op("pool", lambda g: g.memset(an["states_bf"][:], 0.0),
                 writes=("states_bf",), tag="pc4")
            for (row, srcv, tok) in ((ap["dtrow"], dtb, "dtrow"), (ap["alrow"], alog, "alrow")):
                p.op("sp", lambda e, row=row, srcv=srcv: e.dma_start(
                    out=row[:, 0:32], in_=srcv.rearrange("(o f) -> o f", o=1)),
                    writes=(tok,), is_dma=True, tag=f"ld{tok}", channel=tok)

                def tile16(e, row=row):
                    for c in range(1, NCH):
                        r = e.tensor_copy(row[:, c * 32:(c + 1) * 32], row[:, 0:32])
                    return r
                p.op("dve", tile16, reads=(tok,), writes=(tok,), tag=f"tl{tok}")
            p.op("pe", lambda e: e.matmul(pb[4][:, :], ones_row_f[:], ap["dtrow"][:],
                                          start=True, stop=True),
                 reads=("dtrow", "C:all"), writes=(PB[4],), tag="dtbm")
            p.op("dve", lambda e: e.tensor_copy(ap["dt_b"][:], pb[4][:, :]),
                 reads=(PB[4],), writes=("dt_b",), tag="dtbe")
            p.op("pe", lambda e: e.matmul(pb[4][:, :], ones_row_f[:], ap["alrow"][:],
                                          start=True, stop=True),
                 reads=("alrow", "C:all"), writes=(PB[4],), tag="albm")
            p.op("act", lambda e: e.activation(out=ap["al_b"][:], in_=pb[4][:, :], func=AF.Exp),
                 reads=(PB[4],), writes=("al_b",), tag="albe")
            if pre_n < 2:
                return
            p.op("pool", lambda e: e.dma_start(
                out=ap["wba_s"][:].rearrange("p (dc o) -> p dc o", dc=DC),
                in_=wba.rearrange("(dc p) o -> p dc o", p=128)),
                writes=("wba_s",), is_dma=True, tag="wbal", channel="wba_s")
            for tt in range(4):
                lbuf = an["lnr"][tt % 2]
                LB = f"lnrs{tt % 2}"
                p.op("sp", lambda e, tt=tt, lbuf=lbuf: e.dma_start(
                    out=lbuf[:].rearrange("p (dc t) -> p dc t", dc=DC),
                    in_=ln2d[:, tt * 512:(tt + 1) * 512].rearrange("(dc p) t -> p dc t", p=128)),
                    reads=tuple(f"ln2d:{dc}:{tt * 512}" for dc in range(DC)),
                    writes=(LB,), is_dma=True, tag=f"balnr{tt}", channel=LB)

                def mmba(e, tt=tt, lbuf=lbuf):
                    for dc in range(DC):
                        i = e.matmul(pb[4][0:64, :], ap["wba_s"][:, dc * 64:(dc + 1) * 64],
                                     lbuf[:, dc * 512:(dc + 1) * 512],
                                     start=(dc == 0), stop=(dc == DC - 1))
                    return i
                p.op("pe", mmba, reads=("wba_s", LB), writes=(PB[4],), tag=f"bam{tt}")
                p.op("dve", lambda e, tt=tt: e.tensor_copy(
                    ap["ba_all"][:, tt * 512:(tt + 1) * 512], pb[4][0:64, :]),
                    reads=(PB[4],), writes=("ba_all",), tag=f"bae{tt}")
            if pre_n < 3:
                return
            for c in range(NCH):
                cs = slice(c * 128, (c + 1) * 128)
                p.op("pe", lambda e, cs=cs: e.transpose(pbs[:, 0:64], ap["ba_all"][:, cs],
                                                        ident_f[0:64, 0:64]),
                     reads=("ba_all", "C:all"), writes=(PBS,), tag=f"btr{c}")
                p.op("dve", lambda e, c=c: e.tensor_copy(
                    an["btm"][:, c * 32:(c + 1) * 32].rearrange("p (i j) -> p i j", j=2),
                    pbs[:, 0:64].rearrange("p (i r) -> p i r", r=4)[:, :, 0:2]),
                    reads=(PBS,), writes=("btm",), tag=f"bcp{c}")
                p.op("dve", lambda e, c=c: e.tensor_copy(
                    ap["araw"][:, c * 32:(c + 1) * 32].rearrange("p (i j) -> p i j", j=2),
                    pbs[:, 0:64].rearrange("p (i r) -> p i r", r=4)[:, :, 2:4]),
                    reads=(PBS,), writes=("araw",), tag=f"acp{c}")
            p.op("act", lambda e: e.activation(out=an["btm"][:], in_=an["btm"][:], func=AF.Sigmoid),
                 reads=("btm",), writes=("btm",), tag="bsig")
            p.op("dve", lambda e: e.tensor_tensor(out=ap["araw"][:], in0=ap["araw"][:],
                                                  in1=ap["dt_b"][:], op=ALU.add),
                 reads=("araw", "dt_b"), writes=("araw",), tag="adt")
            p.op("act", lambda e: e.activation(out=ap["spa"][:], in_=ap["araw"][:], func=AF.Abs),
                 reads=("araw",), writes=("spa",), tag="sp_abs")
            p.op("act", lambda e: e.activation(out=ap["spa"][:], in_=ap["spa"][:], func=AF.Exp,
                                               scale=-1.0),
                 reads=("spa",), writes=("spa",), tag="sp_exp")
            p.op("act", lambda e: e.activation(out=ap["spa"][:], in_=ap["spa"][:], func=AF.Ln,
                                               bias=eps_t[:, 3:4]),
                 reads=("spa", "C:all"), writes=("spa",), tag="sp_ln")
            p.op("act", lambda e: e.activation(out=ap["spb"][:], in_=ap["araw"][:], func=AF.Relu),
                 reads=("araw",), writes=("spb",), tag="sp_relu")
            p.op("dve", lambda e: e.tensor_tensor(out=ap["spa"][:], in0=ap["spa"][:],
                                                  in1=ap["spb"][:], op=ALU.add),
                 reads=("spa", "spb"), writes=("spa",), tag="sp_add")
            p.op("dve", lambda e: e.scalar_tensor_tensor(out=ap["gof"][:], in0=ap["spa"][:],
                                                         scalar=-1.0, in1=ap["al_b"][:],
                                                         op0=ALU.mult, op1=ALU.mult),
                 reads=("spa", "al_b"), writes=("gof",), tag="gmul")
            if pre_n < 4:
                return
            for c in range(NCH):
                p.op("pe", lambda e, c=c: e.matmul(
                    pb[4][:, 0:32], triU_inc[:], ap["gof"][:, c * 32:(c + 1) * 32],
                    start=True, stop=True),
                    reads=("gof", "C:all"), writes=(PB[4],), tag=f"lmm{c}")
                p.op("pe", lambda e, c=c: e.matmul(
                    pb[4][:, 32:64], triU_str[:], ap["gof"][:, c * 32:(c + 1) * 32],
                    start=True, stop=True),
                    reads=("gof", "C:all"), writes=(PB[4],), tag=f"lrmm{c}")
                p.op("dve", lambda e, c=c: e.tensor_scalar(
                    out=an["ltm"][:, c * 32:(c + 1) * 32], in0=pb[4][:, 0:32],
                    scalar1=0.0, scalar2=None, op0=ALU.min),
                    reads=(PB[4],), writes=("ltm",), tag=f"lcp{c}")
                p.op("dve", lambda e, c=c: e.tensor_scalar(
                    out=an["lrtm"][:, c * 32:(c + 1) * 32], in0=pb[4][:, 32:64],
                    scalar1=0.0, scalar2=None, op0=ALU.min),
                    reads=(PB[4],), writes=("lrtm",), tag=f"lrcp{c}")
                p.op("pe", lambda e, c=c: e.transpose(
                    pbs[0:32, 128:256], an["ltm"][:, c * 32:(c + 1) * 32], ident_f[:]),
                    reads=("ltm", "C:all"), writes=(PBS,), tag=f"lhm{c}")
                p.op("dve", lambda e, c=c: e.tensor_copy(
                    an["lhsD"][0:32, c * 128:(c + 1) * 128], pbs[0:32, 128:256]),
                    reads=(PBS,), writes=("lhsD",), tag=f"lhme{c}")
                p.op("sp", lambda e, c=c: e.dma_start(
                    out=lhd[c], in_=an["lhsD"][0:32, c * 128:(c + 1) * 128]),
                    reads=("lhsD",), writes=(f"lhd:{c}",), is_dma=True, tag=f"lhw{c}",
                    channel="lhd")
            if pre_n < 5:
                return
            p.op("act", lambda e: e.activation(out=an["gtm"][:], in_=an["ltm"][:], func=AF.Exp),
                 reads=("ltm",), writes=("gtm",), tag="gexp")
            p.op("act", lambda e: e.activation(out=an["grtm"][:], in_=an["lrtm"][:], func=AF.Exp),
                 reads=("lrtm",), writes=("grtm",), tag="grexp")
            p.op("dve", lambda e: e.tensor_tensor(out=an["lcrow"][:], in0=an["ltm"][0:1, :],
                                                  in1=an["lrtm"][0:1, :], op=ALU.add),
                 reads=("ltm", "lrtm"), writes=("lcrow",), tag="lcadd")
            p.op("act", lambda e: e.activation(out=an["lcrow"][:], in_=an["lcrow"][:], func=AF.Exp),
                 reads=("lcrow",), writes=("lcrow",), tag="lcexp")
            p.op("pe", lambda e: e.matmul(pb[4][:, :], ones_row_f[:], an["lcrow"][:],
                                          start=True, stop=True),
                 reads=("lcrow", "C:all"), writes=(PB[4],), tag="gcmm")
            p.op("dve", lambda e: e.tensor_copy(an["gcb"][:], pb[4][:, :]),
                 reads=(PB[4],), writes=("gcb",), tag="gce")

        def attn_head_proj(an, aa, i, taps):
            base = 768 * i
            occols = [base, base + 128, base + 256, base + 384, base + 512, base + 640]
            convci = [i, 16 + i, 32 + 2 * i, 32 + 2 * i + 1]
            p.op("dve", lambda e: e.memset(aa["hist"][:], 0.0), writes=("hist",),
                 tag=f"hz{i}")
            for tt in range(4):
                lbuf = an["lnr"][tt % 2]
                LB = f"lnrs{tt % 2}"
                p.op("sp", lambda e, tt=tt, lbuf=lbuf: e.dma_start(
                    out=lbuf[:].rearrange("p (dc t) -> p dc t", dc=DC),
                    in_=ln2d[:, tt * 512:(tt + 1) * 512].rearrange("(dc p) t -> p dc t", p=128)),
                    reads=tuple(f"ln2d:{dc}:{tt * 512}" for dc in range(DC)),
                    writes=(LB,), is_dma=True, tag=f"plnr{i}.{tt}", channel=LB)
                for oc in range(6):
                    slot = aa["wq_r"][oc]
                    SL = f"wq_r{oc}"
                    if tt == 0:
                        p.op("pool", lambda e, oc=oc, slot=slot: e.dma_start(
                            out=slot[:].rearrange("p (dc o) -> p dc o", dc=DC),
                            in_=wqkvz[:, occols[oc]:occols[oc] + 128]
                            .rearrange("(dc p) o -> p dc o", p=128)),
                            writes=(SL,), is_dma=True, tag=f"pwl{i}.{oc}", channel=SL)
                    bank = pb[(tt * 6 + oc) % 4]
                    BK = PB[(tt * 6 + oc) % 4]

                    def mmq(e, slot=slot, lbuf=lbuf, bank=bank):
                        for dc in range(DC):
                            r = e.matmul(bank[:, :], slot[:, dc * 128:(dc + 1) * 128],
                                         lbuf[:, dc * 512:(dc + 1) * 512],
                                         start=(dc == 0), stop=(dc == DC - 1))
                        return r
                    p.op("pe", mmq, reads=(SL, LB), writes=(BK,), tag=f"pm{i}.{oc}.{tt}")
                    if oc < 4:
                        p.op("dve", lambda e, oc=oc: e.tensor_copy(
                            aa["raw"][:, 0:3], aa["hist"][:, oc * 4:oc * 4 + 3]),
                            reads=("hist",), writes=("raw",), tag=f"ph{i}.{oc}.{tt}")
                        p.op("dve", lambda e, bank=bank: e.tensor_copy(
                            aa["raw"][:, 3:515], bank[:, :]),
                            reads=(BK, "raw"), writes=("raw",), tag=f"pe{i}.{oc}.{tt}")
                        p.op("dve", lambda e, oc=oc: e.tensor_copy(
                            aa["hist"][:, oc * 4:oc * 4 + 3], aa["raw"][:, 512:515]),
                            reads=("raw",), writes=("hist",), tag=f"phs{i}.{oc}.{tt}")
                        ci = convci[oc]
                        p.op("dve", lambda e, ci=ci: e.tensor_scalar(
                            out=xsl[0][:], in0=aa["raw"][:, 0:512],
                            scalar1=an["convs"][:, ci * 4:ci * 4 + 1],
                            scalar2=None, op0=ALU.mult),
                            reads=("raw", "convs"), writes=("xsl0",),
                            tag=f"pc{i}.{oc}.{tt}.0")
                        for j in range(1, 4):
                            p.op("dve", lambda e, ci=ci, j=j: e.scalar_tensor_tensor(
                                out=xsl[0][:], in0=aa["raw"][:, j:j + 512],
                                scalar=an["convs"][:, ci * 4 + j:ci * 4 + j + 1],
                                in1=xsl[0][:], op0=ALU.mult, op1=ALU.add),
                                reads=("raw", "convs", "xsl0"), writes=("xsl0",),
                                tag=f"pc{i}.{oc}.{tt}.{j}")
                        if oc < 2:
                            dstb = aa["qhat"] if oc == 0 else aa["khat"]
                            DT = "qhat" if oc == 0 else "khat"
                            p.op("act", lambda e, dstb=dstb, tt=tt: e.activation(
                                out=dstb[:, tt * 512:(tt + 1) * 512], in_=xsl[0][:], func=AF.Silu),
                                reads=("xsl0",), writes=(DT,), tag=f"ps{i}.{oc}.{tt}")
                        else:
                            j = oc - 2
                            p.op("act", lambda e: e.activation(
                                out=bfsl[0][:], in_=xsl[0][:], func=AF.Silu),
                                reads=("xsl0",), writes=("bfsl0",), tag=f"ps{i}.{oc}.{tt}")

                            def vtr(e):
                                for k in range(4):
                                    r = e.transpose(pbt[:, k * 128:(k + 1) * 128],
                                                    bfsl[0][:, k * 128:(k + 1) * 128], ident_bf[:])
                                return r
                            p.op("pe", vtr, reads=("bfsl0", "C:all"), writes=(PBT,),
                                 tag=f"pvt{i}.{oc}.{tt}")
                            p.op("dve", lambda e, j=j, tt=tt: e.tensor_copy(
                                aa["vtm"][j][:, tt * 512:(tt + 1) * 512], pbt[:, 0:512]),
                                reads=(PBT,), writes=(f"vtm{j}",), tag=f"pve{i}.{oc}.{tt}")
                    else:
                        j = oc - 4
                        p.op("act", lambda e, bank=bank: e.activation(
                            out=bfsl[0][:], in_=bank[:, :], func=AF.Silu),
                            reads=(BK,), writes=("bfsl0",), tag=f"pz{i}.{oc}.{tt}")

                        def ztr(e):
                            for k in range(4):
                                r = e.transpose(pbt[:, k * 128:(k + 1) * 128],
                                                bfsl[0][:, k * 128:(k + 1) * 128], ident_bf[:])
                            return r
                        p.op("pe", ztr, reads=("bfsl0", "C:all"), writes=(PBT,),
                             tag=f"pzt{i}.{oc}.{tt}")
                        p.op("dve", lambda e, j=j, tt=tt: e.tensor_copy(
                            aa["zstm"][j][:, tt * 512:(tt + 1) * 512], pbt[:, 0:512]),
                            reads=(PBT,), writes=(f"zstm{j}",), tag=f"pze{i}.{oc}.{tt}")
            for (nm, scale_bias_col) in (("qhat", 4), ("khat", 5)):
                buf = aa[nm]
                for tt in range(4):
                    sl = slice(tt * 512, (tt + 1) * 512)
                    p.op("act", lambda e, buf=buf, sl=sl: e.activation(
                        out=sqsl[:], in_=buf[:, sl], func=AF.Square),
                        reads=(nm,), writes=("sqsl0",), tag=f"n2s{i}.{nm}.{tt}")
                    p.op("pe", lambda e: e.matmul(pb[4][0:1, :], ones_col_bf[:], sqsl[:],
                                                  start=True, stop=True),
                         reads=("sqsl0", "C:all"), writes=(PB[4],), tag=f"n2m{i}.{nm}.{tt}")
                    p.op("act", lambda e: e.activation(out=srow[0][:], in_=pb[4][0:1, :], func=AF.Ln,
                                                       bias=eps_t[0:1, 1:2]),
                         reads=(PB[4], "C:all"), writes=("srow0",), tag=f"n2l{i}.{nm}.{tt}")
                    p.op("act", lambda e, sc=scale_bias_col: e.activation(
                        out=srow[0][:], in_=srow[0][:], func=AF.Exp, scale=-0.5,
                        bias=eps_t[0:1, sc:sc + 1]),
                        reads=("srow0", "C:all"), writes=("srow0",), tag=f"n2e{i}.{nm}.{tt}")
                    p.op("pe", lambda e: e.matmul(pb[5][:, :], ones_row_f[:], srow[0][:],
                                                  start=True, stop=True),
                         reads=("srow0", "C:all"), writes=(PB[5],), tag=f"n2b{i}.{nm}.{tt}")
                    p.op("dve", lambda e, buf=buf, sl=sl: e.tensor_tensor(
                        out=buf[:, sl], in0=buf[:, sl], in1=pb[5][:, :], op=ALU.mult),
                        reads=(nm, PB[5]), writes=(nm,), tag=f"n2n{i}.{nm}.{tt}")
            for tt in range(4):
                def ktr(e, tt=tt):
                    for k in range(4):
                        r = e.transpose(pbt[:, k * 128:(k + 1) * 128],
                                        aa["khat"][:, tt * 512 + k * 128:tt * 512 + (k + 1) * 128],
                                        ident_bf[:])
                    return r
                p.op("pe", ktr, reads=("khat", "C:all"), writes=(PBT,), tag=f"ktt{i}.{tt}")
                p.op("dve", lambda e, tt=tt: e.tensor_copy(
                    aa["khat_tm"][:, tt * 512:(tt + 1) * 512], pbt[:, 0:512]),
                    reads=(PBT,), writes=("khat_tm",), tag=f"kte{i}.{tt}")
            for (nm, dst) in (("qhat", qkd[i, 0]), ("khat", qkd[i, 1]), ("khat_tm", qkd[i, 2])):
                p.op("sp", lambda e, nm=nm, dst=dst: e.dma_start(out=dst, in_=aa[nm][:]),
                     reads=(nm,), writes=(f"D:{nm}:{i}",), is_dma=True,
                     tag=f"pq{i}.{nm}", channel="D:qkd")
            for j in range(2):
                h = 2 * i + j
                p.op("sp", lambda e, j=j, h=h: e.dma_start(out=vzd[h, 0], in_=aa["vtm"][j][:]),
                     reads=(f"vtm{j}",), writes=(f"D:v:{h}",), is_dma=True,
                     tag=f"pv{i}.{j}", channel="D:vzd")
                p.op("sp", lambda e, j=j, h=h: e.dma_start(out=vzd[h, 1], in_=aa["zstm"][j][:]),
                     reads=(f"zstm{j}",), writes=(f"D:z:{h}",), is_dma=True,
                     tag=f"pz{i}.{j}", channel="D:vzd")

        def attn_rec_loads(an, aa, i):
            par = i % 2
            Rq, Rk, Rkt = f"rq{par}", f"rk{par}", f"rkt{par}"
            qhat, khat, khat_tm = aa[f"rq{par}"], aa[f"rk{par}"], aa[f"rkt{par}"]
            vtm = [aa[f"rv0{par}"], aa[f"rv1{par}"]]
            zstm = [aa[f"rz0{par}"], aa[f"rz1{par}"]]
            VT = [f"rv0{par}", f"rv1{par}"]
            ZT = [f"rz0{par}", f"rz1{par}"]
            for (tok, buf, srcap, rdt) in ((Rq, qhat, qkd[i, 0], f"D:qhat:{i}"),
                                           (Rk, khat, qkd[i, 1], f"D:khat:{i}"),
                                           (Rkt, khat_tm, qkd[i, 2], f"D:khat_tm:{i}")):
                p.op("sp", lambda e, buf=buf, srcap=srcap: e.dma_start(out=buf[:], in_=srcap),
                     reads=(rdt,), writes=(tok,), is_dma=True, tag=f"rl{i}.{tok}", channel=tok)
            for j in range(2):
                h = 2 * i + j
                p.op("sp", lambda e, j=j, h=h: e.dma_start(out=vtm[j][:], in_=vzd[h, 0]),
                     reads=(f"D:v:{h}",), writes=(VT[j],), is_dma=True,
                     tag=f"rlv{i}.{j}", channel=VT[j])
                p.op("sp", lambda e, j=j, h=h: e.dma_start(out=zstm[j][:], in_=vzd[h, 1]),
                     reads=(f"D:z:{h}",), writes=(ZT[j],), is_dma=True,
                     tag=f"rlz{i}.{j}", channel=ZT[j])

        def rec_chunk(an, aa, i, c):
            par = i % 2
            sc = aa["scr"][par]
            sc = {**sc, **{k + str(par): v for k, v in sc.items() if isinstance(k, str)}}
            T_elow = "elow" + str(par)
            T_eup = "eup" + str(par)
            T_kk = "kk" + str(par)
            T_qkt = "qkt" + str(par)
            T_msc = "msc" + str(par)
            T_mcur = "mcur" + str(par)
            T_mtall = "mtall" + str(par)
            T_at = "at" + str(par)
            T_xcur = "xcur" + str(par)
            T_ucur = "ucur" + str(par)
            T_oall = "oall" + str(par)
            T_zs = "zs" + str(par)
            T_sqa = "sqa" + str(par)
            T_rms2 = "rms2" + str(par)
            T_xgn = "xgn" + str(par)
            T_kts = "kts" + str(par)
            T_tmp0 = "tmp0" + str(par)
            Rq, Rk, Rkt = f"rq{par}", f"rk{par}", f"rkt{par}"
            qhat, khat, khat_tm = aa[f"rq{par}"], aa[f"rk{par}"], aa[f"rkt{par}"]
            vtm = [aa[f"rv0{par}"], aa[f"rv1{par}"]]
            zstm = [aa[f"rz0{par}"], aa[f"rz1{par}"]]
            VT = [f"rv0{par}", f"rv1{par}"]
            ZT = [f"rz0{par}", f"rz1{par}"]
            g = i

            def vcol(buf, j, c):
                return buf[:, c * 32 + 2 * i + j: c * 32 + 2 * i + j + 1]
            cs = slice(c * 128, (c + 1) * 128)
            cs = slice(c * 128, (c + 1) * 128)
            p.op("sp", lambda e, c=c: e.dma_start(
                out=an["rhsD"][32:33, g * 256:(g + 1) * 256],
                in_=lhd[c, 2 * i:2 * i + 2].rearrange("(o a) b -> o (a b)", o=1)),
                reads=(f"lhd:{c}",), writes=(f"rhsDr:{i}",), is_dma=True, tag=f"rr{i}.{c}",
                channel=f"rhsrow{par}")
            p.op("pe", lambda e, c=c: e.matmul(
                pb[0][:, 0:256], an["lhsD"][:, c * 128:(c + 1) * 128],
                an["rhsD"][:, g * 256:(g + 1) * 256], start=True, stop=True),
                reads=("lhsD", "lhsDr", "rhsDc", f"rhsDr:{i}"), writes=(PB[0],), tag=f"dm{i}.{c}")
            p.op("dve", lambda e: e.tensor_tensor(out=sc[T_elow][:], in0=pb[0][:, 0:256],
                                                  in1=mask_low[:, 0:256], op=ALU.add),
                 reads=(PB[0], "C:all"), writes=(T_elow,), tag=f"em{i}.{c}")
            p.op("act", lambda e: e.activation(out=sc[T_elow][:], in_=sc[T_elow][:], func=AF.Exp),
                 reads=(T_elow,), writes=(T_elow,), tag=f"ee{i}.{c}")

            def eutr(e):
                for k in range(2):
                    r = e.transpose(pbs[:, k * 128:(k + 1) * 128],
                                    sc[T_elow][:, k * 128:(k + 1) * 128], ident_f[:])
                return r
            p.op("pe", eutr, reads=(T_elow, "C:all"), writes=(PBS,), tag=f"eut{i}.{c}")
            p.op("dve", lambda e: e.tensor_copy(sc[T_eup][:], pbs[:, 0:256]),
                 reads=(PBS,), writes=(T_eup,), tag=f"eue{i}.{c}")
            def kkqk(e, cs=cs):
                e.matmul(pb[2][:, 0:128], khat[:, cs], khat[:, cs], start=True, stop=True)
                return e.matmul(pb[2][:, 128:256], khat[:, cs], qhat[:, cs],
                                start=True, stop=True)
            p.op("pe", kkqk, reads=(Rk, Rq), writes=(PB[2],), tag=f"kk{i}.{c}")

            def kke(e):
                e.tensor_copy(sc[T_kk][:, 0:128], pb[2][:, 0:128])
                e.tensor_copy(sc[T_kk][:, 128:256], pb[2][:, 0:128])
                e.tensor_copy(sc[T_qkt][:, 0:128], pb[2][:, 128:256])
                return e.tensor_copy(sc[T_qkt][:, 128:256], pb[2][:, 128:256])
            p.op("dve", kke, reads=(PB[2],), writes=(T_kk, T_qkt), tag=f"kke{i}.{c}")
            p.op("dve", lambda e: e.tensor_tensor(out=sc[T_elow][:], in0=sc[T_elow][:],
                                                  in1=ident_t4[:, 0:256], op=ALU.subtract),
                 reads=(T_elow, "C:all"), writes=(T_elow,), tag=f"es{i}.{c}")
            p.op("dve", lambda e: e.tensor_tensor(out=sc[T_msc][:], in0=sc[T_elow][:],
                                                  in1=sc[T_kk][:], op=ALU.mult),
                 reads=(T_elow, T_kk), writes=(T_msc,), tag=f"ms{i}.{c}")
            def mbf(e, c=c):
                for j in range(2):
                    r = e.tensor_scalar(
                        out=sc[T_mcur][:, j * 128:(j + 1) * 128],
                        in0=sc[T_msc][:, j * 128:(j + 1) * 128],
                        scalar1=vcol(an["btm"], j, c), scalar2=None, op0=ALU.mult)
                return r
            p.op("dve", mbf, reads=(T_msc, "btm"), writes=(T_mcur,), tag=f"mb{i}.{c}")
            p.op("dve", lambda e: e.tensor_tensor(out=sc[T_at][:], in0=sc[T_eup][:],
                                                  in1=sc[T_qkt][:], op=ALU.mult),
                 reads=(T_eup, T_qkt), writes=(T_at,), tag=f"at{i}.{c}")

            def mtr(e):
                for j in range(2):
                    r = e.transpose(pbt[:, j * 128:(j + 1) * 128],
                                    sc[T_mcur][:, j * 128:(j + 1) * 128], ident_bf[:])
                return r
            p.op("pe", mtr, reads=(T_mcur, "C:all"), writes=(PBT,), tag=f"mt{i}.{c}")
            p.op("dve", lambda e: e.tensor_copy(sc[T_mtall][:, 0:256], pbt[:, 0:256]),
                 reads=(PBT,), writes=(T_mtall,), tag=f"mte{i}.{c}")
            for L in range(1, 7):
                def sqm(e, L=L):
                    for j in range(2):
                        js = slice(j * 128, (j + 1) * 128)
                        e.matmul(pb[4][:, j * 128:(j + 1) * 128],
                                 sc[T_mtall][:, (L - 1) * 256 + j * 128:(L - 1) * 256 + (j + 1) * 128],
                                 sc[T_mcur][:, js], start=True, stop=True)
                        r = e.matmul(pb[5][:, j * 128:(j + 1) * 128],
                                     sc[T_mcur][:, js],
                                     sc[T_mtall][:, (L - 1) * 256 + j * 128:(L - 1) * 256 + (j + 1) * 128],
                                     start=True, stop=True)
                    return r
                p.op("pe", sqm, reads=(T_mtall, T_mcur), writes=(PB[4], PB[5]),
                     tag=f"sq{i}.{c}.{L}")
                def sqev(e, L=L):
                    if L < 6:
                        e.tensor_copy(sc[T_mcur][:], pb[4][:, 0:256])
                    return e.tensor_copy(sc[T_mtall][:, L * 256:(L + 1) * 256], pb[5][:, 0:256])
                p.op("dve", sqev, reads=(PB[4], PB[5]), writes=(T_mcur, T_mtall),
                     tag=f"sqe{i}.{c}.{L}")
            def ksf(e, cs=cs):
                for j in range(2):
                    h = 2 * i + j
                    r = e.matmul(pb[0][:, j * 128:(j + 1) * 128], khat[:, cs],
                                 an["states_bf"][:, h * 128:(h + 1) * 128],
                                 start=True, stop=True)
                return r
            p.op("pe", ksf, reads=(Rk, f"stb{2*i}", f"stb{2*i+1}"), writes=(PB[0],),
                 tag=f"ks{i}.{c}")
            def bv1f(e, c=c):
                for j in range(2):
                    r = e.tensor_scalar(
                        out=sc[T_tmp0][:, j * 128:(j + 1) * 128],
                        in0=pb[0][:, j * 128:(j + 1) * 128],
                        scalar1=vcol(an["gtm"], j, c), scalar2=None, op0=ALU.mult)
                return r
            p.op("dve", bv1f, reads=(PB[0], "gtm"), writes=(T_tmp0,), tag=f"bv1{i}.{c}")

            def bv2f(e, cs=cs):
                for j in range(2):
                    r = e.tensor_tensor(
                        out=sc[T_tmp0][:, j * 128:(j + 1) * 128],
                        in0=vtm[j][:, cs], in1=sc[T_tmp0][:, j * 128:(j + 1) * 128],
                        op=ALU.subtract)
                return r
            p.op("dve", bv2f, reads=(VT[0], VT[1], T_tmp0), writes=(T_tmp0,),
                 tag=f"bv2{i}.{c}")

            def bv3f(e, c=c):
                for j in range(2):
                    r = e.tensor_scalar(
                        out=sc[T_xcur][:, j * 128:(j + 1) * 128],
                        in0=sc[T_tmp0][:, j * 128:(j + 1) * 128],
                        scalar1=vcol(an["btm"], j, c), scalar2=None, op0=ALU.mult)
                return r
            p.op("dve", bv3f, reads=(T_tmp0, "btm"), writes=(T_xcur,), tag=f"bv3{i}.{c}")
            for L in range(6, -1, -1):
                def apm(e, L=L):
                    for j in range(2):
                        js = slice(j * 128, (j + 1) * 128)
                        r = e.matmul(pb[1][:, j * 128:(j + 1) * 128],
                                     sc[T_mtall][:, L * 256 + j * 128:L * 256 + (j + 1) * 128],
                                     sc[T_xcur][:, js], start=True, stop=True)
                    return r
                p.op("pe", apm, reads=(T_mtall, T_xcur), writes=(PB[1],), tag=f"ap{i}.{c}.{L}")
                if L > 0:
                    p.op("dve", lambda e: e.tensor_tensor(
                        out=sc[T_xcur][:], in0=sc[T_xcur][:], in1=pb[1][:, 0:256], op=ALU.add),
                        reads=(PB[1], T_xcur), writes=(T_xcur,), tag=f"apa{i}.{c}.{L}")
                else:
                    p.op("dve", lambda e: e.tensor_tensor(
                        out=sc[T_ucur][:], in0=sc[T_xcur][:], in1=pb[1][:, 0:256],
                        op=ALU.subtract),
                        reads=(PB[1], T_xcur), writes=(T_ucur,), tag=f"apu{i}.{c}")
            def qsau(e, cs=cs):
                for j in range(2):
                    h = 2 * i + j
                    e.matmul(pb[2][:, j * 128:(j + 1) * 128], qhat[:, cs],
                             an["states_bf"][:, h * 128:(h + 1) * 128],
                             start=True, stop=True)
                    r = e.matmul(pb[3][:, j * 128:(j + 1) * 128],
                                 sc[T_at][:, j * 128:(j + 1) * 128],
                                 sc[T_ucur][:, j * 128:(j + 1) * 128], start=True, stop=True)
                return r
            p.op("pe", qsau, reads=(Rq, f"stb{2*i}", f"stb{2*i+1}", T_at, T_ucur),
                 writes=(PB[2], PB[3]), tag=f"qs{i}.{c}")
            def oif(e, c=c):
                for j in range(2):
                    r = e.tensor_scalar(
                        out=sc[T_oall][:, j * 128:(j + 1) * 128],
                        in0=pb[2][:, j * 128:(j + 1) * 128],
                        scalar1=vcol(an["gtm"], j, c), scalar2=None, op0=ALU.mult)
                return r
            p.op("dve", oif, reads=(PB[2], "gtm"), writes=(T_oall,), tag=f"oi{i}.{c}")
            p.op("dve", lambda e: e.tensor_tensor(out=sc[T_oall][:], in0=sc[T_oall][:],
                                                  in1=pb[3][:, 0:256], op=ALU.add),
                 reads=(PB[3], T_oall), writes=(T_oall,), tag=f"oa{i}.{c}")
            def zgf(e, cs=cs):
                for j in range(2):
                    r = e.tensor_copy(sc[T_zs][:, j * 128:(j + 1) * 128], zstm[j][:, cs])
                return r
            p.op("dve", zgf, reads=(ZT[0], ZT[1]), writes=(T_zs,), tag=f"zg{i}.{c}")
            p.op("dve", lambda e: e.tensor_tensor(out=sc[T_oall][:], in0=sc[T_oall][:],
                                                  in1=sc[T_zs][:], op=ALU.mult),
                 reads=(T_zs, T_oall), writes=(T_oall,), tag=f"xg{i}.{c}")
            p.op("act", lambda e: e.activation(out=sc[T_sqa][:], in_=sc[T_oall][:],
                                               func=AF.Square),
                 reads=(T_oall,), writes=(T_sqa,), tag=f"xs{i}.{c}")
            p.op("dve", lambda e: e.tensor_reduce(
                out=sc[T_rms2][:], in_=sc[T_sqa][:].rearrange("p (j v) -> p j v", j=2),
                axis=AX.X, op=ALU.add),
                reads=(T_sqa,), writes=(T_rms2,), tag=f"xr{i}.{c}")
            p.op("act", lambda e: e.activation(out=sc[T_rms2][:], in_=sc[T_rms2][:], func=AF.Ln,
                                               scale=1.0 / DV, bias=eps_t[:, 2:3]),
                 reads=(T_rms2, "C:all"), writes=(T_rms2,), tag=f"xl{i}.{c}")
            p.op("act", lambda e: e.activation(out=sc[T_rms2][:], in_=sc[T_rms2][:], func=AF.Exp,
                                               scale=-0.5),
                 reads=(T_rms2,), writes=(T_rms2,), tag=f"xe{i}.{c}")
            def xnf(e):
                for j in range(2):
                    r = e.tensor_scalar(
                        out=sc[T_oall][:, j * 128:(j + 1) * 128],
                        in0=sc[T_oall][:, j * 128:(j + 1) * 128],
                        scalar1=sc[T_rms2][:, j:j + 1], scalar2=None, op0=ALU.mult)
                return r
            p.op("dve", xnf, reads=(T_rms2, T_oall), writes=(T_oall,), tag=f"xn{i}.{c}")
            p.op("dve", lambda e: e.tensor_tensor(out=sc[T_xgn][:], in0=sc[T_oall][:],
                                                  in1=nw_bcast[:, 0:256], op=ALU.mult),
                 reads=(T_oall, "C:nw_bcast"), writes=(T_xgn,), tag=f"xw{i}.{c}")

            def xtr(e):
                for j in range(2):
                    r = e.transpose(pbt[:, j * 128:(j + 1) * 128],
                                    sc[T_xgn][:, j * 128:(j + 1) * 128], ident_bf[:])
                return r
            p.op("pe", xtr, reads=(T_xgn, "C:all"), writes=(PBT,), tag=f"xt{i}.{c}")
            def xsef(e, cs=cs):
                for j in range(2):
                    r = e.tensor_copy(sc["xg_st"][j][:, cs], pbt[:, j * 128:(j + 1) * 128])
                return r
            p.op("dve", xsef, reads=(PBT,), writes=(f"xg_st{par}0", f"xg_st{par}1"),
                 tag=f"xse{i}.{c}")
            def ktf(e, cs=cs, c=c):
                for j in range(2):
                    r = e.tensor_scalar(
                        out=sc[T_kts][:, j * 128:(j + 1) * 128],
                        in0=khat_tm[:, cs],
                        scalar1=vcol(an["grtm"], j, c), scalar2=None, op0=ALU.mult)
                return r
            p.op("dve", ktf, reads=(Rkt, "grtm"), writes=(T_kts,), tag=f"kt{i}.{c}")
            def stf(e):
                for j in range(2):
                    r = e.matmul(pb[4][:, j * 128:(j + 1) * 128],
                                 sc[T_kts][:, j * 128:(j + 1) * 128],
                                 sc[T_ucur][:, j * 128:(j + 1) * 128], start=True, stop=True)
                return r
            p.op("pe", stf, reads=(T_kts, T_ucur), writes=(PB[4],), tag=f"st{i}.{c}")
            STH = (f"st{2*i}", f"st{2*i+1}")
            STBH = (f"stb{2*i}", f"stb{2*i+1}")

            def ssf(e, c=c):
                for j in range(2):
                    h = 2 * i + j
                    r = e.tensor_scalar(
                        out=an["states"][:, h * 128:(h + 1) * 128],
                        in0=an["states"][:, h * 128:(h + 1) * 128],
                        scalar1=vcol(an["gcb"], j, c), scalar2=None, op0=ALU.mult)
                return r
            p.op("dve", ssf, reads=STH + ("gcb",), writes=STH, tag=f"ss{i}.{c}")

            def saf(e):
                for j in range(2):
                    h = 2 * i + j
                    r = e.tensor_tensor(
                        out=an["states"][:, h * 128:(h + 1) * 128],
                        in0=an["states"][:, h * 128:(h + 1) * 128],
                        in1=pb[4][:, j * 128:(j + 1) * 128], op=ALU.add)
                return r
            p.op("dve", saf, reads=STH + (PB[4],), writes=STH, tag=f"sa{i}.{c}")

            def sbf(e):
                for j in range(2):
                    h = 2 * i + j
                    r = e.tensor_copy(an["states_bf"][:, h * 128:(h + 1) * 128],
                                      an["states"][:, h * 128:(h + 1) * 128])
                return r
            p.op("dve", sbf, reads=STH, writes=STBH, tag=f"sb{i}.{c}")

        def attn_rec_ship(an, aa, i):
            par = i % 2
            sc = aa["scr"][par]
            for j in range(2):
                h = 2 * i + j
                p.op("sp", lambda e, j=j, h=h, sc=sc: e.dma_start(out=xgd[h], in_=sc["xg_st"][j][:]),
                     reads=(f"xg_st{par}{j}",), writes=(f"xgd:{h}",), is_dma=True,
                     tag=f"xgw{i}.{j}", channel="D:xgd")

        def alloc_outp(stk):
            s2 = lambda n, sh, dt: stk.enter_context(
                nc.sbuf_tensor(n, sh, dt, unsafe_allow_realloc=True))
            o = {}
            o["y_att"] = s2("y_att", [128, DC * 512], F32)
            o["xg_ring"] = s2("xg_ring", [128, NV * 512], BF16)
            o["wo_r"] = [s2(f"wo_r{i}", [128, D], BF16) for i in range(10)]
            return o

        def outproj(os_):
            y_att, xg_ring, wo_r = os_["y_att"], os_["xg_ring"], os_["wo_r"]
            for tt in range(4):
                t0 = tt * 512
                p.op("sp", lambda e, tt=tt: e.dma_start(
                    out=xg_ring[:].rearrange("p (v t) -> p v t", v=NV),
                    in_=xgd[:, :, tt * 512:(tt + 1) * 512].rearrange("v p t -> p v t")),
                    reads=tuple(f"xgd:{h}" for h in range(NV)),
                    writes=("xg_ring",), is_dma=True, tag=f"oxg{tt}", channel="xg_ring")
                for vs in range(4):
                    for k in range(8):
                        vc = vs * 8 + k
                        slot = wo_r[vc % 10]
                        SL = f"wo_r{vc % 10}"
                        p.op("pool", lambda e, vc=vc, slot=slot: e.dma_start(
                            out=slot[:], in_=wout[vc * 128:(vc + 1) * 128, :]),
                            writes=(SL,), is_dma=True, tag=f"owl{tt}.{vc}", channel=SL)
                    for dc in range(DC):
                        bank = pb[4 + dc % 2]
                        BK = PB[4 + dc % 2]

                        def mmo(e, vs=vs, dc=dc, bank=bank):
                            for k in range(8):
                                vc = vs * 8 + k
                                r = e.matmul(bank[:, :], wo_r[vc % 10][:, dc * 128:(dc + 1) * 128],
                                             xg_ring[:, vc * 512:(vc + 1) * 512],
                                             start=(k == 0), stop=(k == 7))
                            return r
                        p.op("pe", mmo,
                             reads=tuple(f"wo_r{(vs * 8 + k) % 10}" for k in range(8)) + ("xg_ring",),
                             writes=(BK,), tag=f"om{tt}.{vs}.{dc}")
                        if vs == 0:
                            p.op("dve", lambda e, dc=dc, bank=bank: e.tensor_copy(
                                y_att[:, dc * 512:(dc + 1) * 512], bank[:, :]),
                                reads=(BK,), writes=("y_att",), tag=f"oy0{tt}.{dc}")
                        else:
                            p.op("dve", lambda e, dc=dc, bank=bank: e.tensor_tensor(
                                out=y_att[:, dc * 512:(dc + 1) * 512],
                                in0=y_att[:, dc * 512:(dc + 1) * 512], in1=bank[:, :], op=ALU.add),
                                reads=(BK, "y_att"), writes=("y_att",), tag=f"oy{tt}.{vs}.{dc}")
                for dc in range(DC):
                    slot = dc % 3
                    xa = xsl[slot]
                    Xa = f"xsl{slot}"
                    p.op("sp", lambda e, dc=dc, t0=t0, xa=xa: e.dma_start(
                        out=xa[:], in_=x1d[dc * 128:(dc + 1) * 128, t0:t0 + 512]),
                        reads=(f"x1d:{dc}:{t0}",), writes=(Xa,), is_dma=True,
                        tag=f"orx{tt}.{dc}", channel=Xa)
                    p.op("dve", lambda e, dc=dc, xa=xa: e.tensor_tensor(
                        out=y_att[:, dc * 512:(dc + 1) * 512],
                        in0=y_att[:, dc * 512:(dc + 1) * 512], in1=xa[:], op=ALU.add),
                        reads=(Xa, "y_att"), writes=("y_att",), tag=f"orr{tt}.{dc}")
                    p.op("sp", lambda e, dc=dc, t0=t0: e.dma_start(
                        out=x2d[dc * 128:(dc + 1) * 128, t0:t0 + 512],
                        in_=y_att[:, dc * 512:(dc + 1) * 512]),
                        reads=("y_att",), writes=(f"x2d:{dc}:{t0}",), is_dma=True,
                        tag=f"orw{tt}.{dc}", channel="D:x2d")
                for dc in range(DC):
                    slot = dc % 3
                    xb = bfsl[slot]
                    Xb = f"bfsl{slot}"
                    p.op("dve", lambda e, dc=dc, xb=xb: e.tensor_copy(
                        xb[:], y_att[:, dc * 512:(dc + 1) * 512]),
                        reads=("y_att",), writes=(Xb,), tag=f"oc{tt}.{dc}")
                    stat_mms(Xb, xb, dc, f"o{tt}.", tt)
                stats_rows(f"o{tt}.", tt)
                for dc in range(DC):
                    slot = dc % 3
                    xa, xb = xsl[slot], bfsl[slot]
                    Xa, Xb = f"xsl{slot}", f"bfsl{slot}"
                    p.op("dve", lambda e, xa=xa, dc=dc: e.tensor_tensor(
                        out=xa[:], in0=y_att[:, dc * 512:(dc + 1) * 512], in1=pb[4][:, :],
                        op=ALU.subtract),
                        reads=("y_att", PB[4]), writes=(Xa,), tag=f"oa{tt}.{dc}")
                    p.op("dve", lambda e, xa=xa, xb=xb: e.tensor_tensor(
                        out=xb[:], in0=xa[:], in1=pbs[:, :], op=ALU.mult),
                        reads=(Xa, PBS), writes=(Xb,), tag=f"on{tt}.{dc}")
                    p.op("sp", lambda e, dc=dc, t0=t0, xb=xb: e.dma_start(
                        out=ln3d[dc * 128:(dc + 1) * 128, t0:t0 + 512], in_=xb[:]),
                        reads=(Xb,), writes=(f"ln3d:{dc}:{t0}",), is_dma=True,
                        tag=f"olw{tt}.{dc}", channel="D:ln3d")

        no_tok = lambda dc, t0: ()
        x2_tok = lambda dc, t0: (f"x2d:{dc}:{t0}",)
        stop = cfg.get("stop")

        if stop == "bar":
            fstk = contextlib.ExitStack()
            with fstk:
                fs = alloc_ffn(fstk, "a")
                ffn_phase(fs, x0, no_tok, out, "outd", ln2d, "ln2d",
                          first=True, write_out=False, tagp="f1.")
            p.barrier(FLUSH)
            p.out_tokens += [f"outd:{dc}:{t0}" for dc in range(DC) for t0 in range(0, S, 512)]
        elif stop == "ffn1":
            fstk = contextlib.ExitStack()
            with fstk:
                fs = alloc_ffn(fstk, "a")
                ffn_phase(fs, x0, no_tok, out, "outd", ln2d, "ln2d",
                          first=True, write_out=False, tagp="f1.")
            p.out_tokens += [f"outd:{dc}:{t0}" for dc in range(DC) for t0 in range(0, S, 512)]
        else:
            fstk = contextlib.ExitStack()
            with fstk:
                fs = alloc_ffn(fstk, "a")
                ffn_phase(fs, x0, no_tok, x1d, "x1d", ln2d, "ln2d",
                          first=True, write_out=False, tagp="f1.")
            p.barrier(FLUSH)
            astk = contextlib.ExitStack()
            with astk:
                an = alloc_attn_persist(astk)
                p1stk = contextlib.ExitStack()
                with p1stk:
                    aa = alloc_attn_p1(p1stk)
                    an["lnr"] = aa["lnr"]
                    prestk = contextlib.ExitStack()
                    with prestk:
                        ap = alloc_attn_pre(prestk)
                        attn_prepass(ap, an)
                if stop == "prepass" and cfg.get("pre_n", 99) < 4:
                    p.out_dma("sp", lambda e: e.dma_start(
                        out=out[0:64, 0:2048], in_=ap["ba_all"][:]),
                        reads=("ba_all",), tag="tapba")
                    p.out_dma("sp", lambda e: e.dma_start(
                        out=out[64:192, 0:512], in_=ap["gof"][:]),
                        reads=("gof",), tag="tapgof")
                    p.out_dma("sp", lambda e: e.dma_start(
                        out=out[192:320, 0:512], in_=an["btm"][:]),
                        reads=("btm",), tag="tapbtm")
                elif stop == "prepass":
                    for (nm, r0) in (("btm", 0), ("ltm", 1), ("gtm", 2), ("grtm", 3)):
                        p.out_dma("sp", lambda e, nm=nm, r0=r0: e.dma_start(
                            out=out[r0 * 128:(r0 + 1) * 128, 0:512], in_=an[nm][:]),
                            reads=(nm,), tag=f"tapp{nm}")
                    p.out_dma("sp", lambda e: e.dma_start(
                        out=out[4 * 128:4 * 128 + 33, 0:2048], in_=an["lhsD"][:]),
                        reads=("lhsD", "lhsDr"), tag="tapl")
                    p.out_dma("sp", lambda e: e.dma_start(
                        out=out[6 * 128:6 * 128 + 32, 0:2048], in_=an["rhsD"][0:32, 0:2048]),
                        reads=("rhsDc",), tag="tapr")
                    p.out_dma("sp", lambda e: e.dma_start(
                        out=out[9 * 128:10 * 128, 0:512], in_=an["gcb"][:]),
                        reads=("gcb",), tag="tapg")
                else:
                    nheads = cfg.get("nheads", NK)
                    for i in range(nheads):
                        attn_head_proj(an, aa, i, taps=None)
                        if stop == "proj":
                            break
                    p1stk.close()
                    if stop != "proj":
                        p.barrier(FLUSH)
                        p2stk = contextlib.ExitStack()
                        with p2stk:
                            aa = alloc_attn_p2(p2stk)
                            for pr in range((nheads + 1) // 2):
                                i0, i1 = 2 * pr, 2 * pr + 1
                                attn_rec_loads(an, aa, i0)
                                if i1 < nheads:
                                    attn_rec_loads(an, aa, i1)
                                for c in range(NCH):
                                    rec_chunk(an, aa, i0, c)
                                    if i1 < nheads:
                                        rec_chunk(an, aa, i1, c)
                                attn_rec_ship(an, aa, i0)
                                if i1 < nheads:
                                    attn_rec_ship(an, aa, i1)
                    if stop == "proj":
                        for (nm, r0) in (("qhat", 0), ("khat", 1), ("khat_tm", 2)):
                            p.out_dma("pool", lambda e, nm=nm, r0=r0: e.dma_start(
                                out=out[r0 * 128:(r0 + 1) * 128, :], in_=aa[nm][:]),
                                reads=(nm,), tag=f"tapq{nm}")
                        for j in range(2):
                            p.out_dma("pool", lambda e, j=j: e.dma_start(
                                out=out[(3 + j) * 128:(4 + j) * 128, :], in_=aa["vtm"][j][:]),
                                reads=(f"vtm{j}",), tag=f"tapv{j}")
                            p.out_dma("pool", lambda e, j=j: e.dma_start(
                                out=out[(5 + j) * 128:(6 + j) * 128, :], in_=aa["zstm"][j][:]),
                                reads=(f"zstm{j}",), tag=f"tapz{j}")
                    elif stop == "xg":
                        for h in range(min(2 * nheads, 16)):
                            p.out_dma("pool", lambda e, h=h: e.dma_start(
                                out=out[h * 128:(h + 1) * 128, :], in_=xgd[h]),
                                reads=(f"xgd:{h}",), tag=f"tapxg{h}")
            if stop in (None, "", "attn", "full"):
                p.barrier(FLUSH)
                ostk = contextlib.ExitStack()
                with ostk:
                    os_ = alloc_outp(ostk)
                    outproj(os_)
                if stop == "attn":
                    x2_toks = tuple(f"x2d:{dc}:{t0}" for dc in range(DC)
                                    for t0 in range(0, S, 512))
                    p.out_dma("sp", lambda e: e.dma_start(out=out, in_=x2d),
                              reads=x2_toks, tag="tapx2")
                else:
                    p.barrier(FLUSH)
                    fstk2 = contextlib.ExitStack()
                    with fstk2:
                        fs2 = alloc_ffn(fstk2, "b")
                        ffn_phase(fs2, x2d, x2_tok, None, "xf", None, None,
                                  first=False, write_out=True, tagp="f3.")

        p.op("sp", lambda e: e.nop(), reads=tuple(p.out_tokens), tag="final_wait")

        sems, chan_sems = {}, {}
        for s in STREAMS:
            sems[s] = st.enter_context(nc.semaphore(f"sem_{s}"))
        for i, ch in enumerate(sorted(p.chan)):
            chan_sems[ch] = st.enter_context(nc.semaphore(f"csem{i}"))
        block = st.enter_context(nc.Block())
        p.finalize(block, sems, chan_sems)
        if cfg.get("verbose"):
            print("prog stats:", p.stats())
    return nc


def make_inputs(inputs):
    x = np.asarray(inputs["L0_attn_hidden_states"])
    base = {
        "wgT": np.ascontiguousarray(np.asarray(inputs["L0_ffn_w_gate"]).T),
        "wuT": np.ascontiguousarray(np.asarray(inputs["L0_ffn_w_up"]).T),
        "wdT": np.ascontiguousarray(np.asarray(inputs["L0_ffn_w_down"]).T),
        "wqkvz": np.ascontiguousarray(np.asarray(inputs["L0_attn_w_qkvz"])),
        "wba": np.ascontiguousarray(np.asarray(inputs["L0_attn_w_ba"])),
        "convw": np.ascontiguousarray(np.asarray(inputs["L0_attn_conv_w"])[:, 0, :]),
        "dtb": np.ascontiguousarray(np.asarray(inputs["L0_attn_dt_bias"])),
        "alog": np.ascontiguousarray(np.asarray(inputs["L0_attn_A_log"])),
        "normw": np.ascontiguousarray(np.asarray(inputs["L0_attn_norm_w"])),
        "wout": np.ascontiguousarray(np.asarray(inputs["L0_attn_w_out"])),
    }
    in_maps = []
    for b in range(8):
        m = dict(base)
        m["x0"] = np.ascontiguousarray(x[b].T)
        in_maps.append(m)
    return in_maps


_CFG = {}


def kernel(**inputs):
    cfg = dict(_CFG)
    nc = build_program(cfg)
    in_maps = make_inputs(inputs)
    res = run_bass_kernel_spmd(nc, in_maps, core_ids=list(range(8)))
    outs = [r["out"].T for r in res.results]
    return np.ascontiguousarray(np.stack(outs, axis=0)).astype(np.float32)
